# version 05
# baseline (speedup 1.0000x reference)
import contextlib

import numpy as np
import ml_dtypes

import concourse.bass as bass
import concourse.mybir as mybir
from concourse.bass_utils import run_bass_kernel_spmd

F32 = mybir.dt.float32
BF16 = mybir.dt.bfloat16
AF = mybir.ActivationFunctionType
ALU = mybir.AluOpType
BF = ml_dtypes.bfloat16

NCORE = 8
NT = 2048
HALO = 256
NTH = NT + HALO
C = 128
NCHH = NTH // C
KVH = 8
NH = 32
FC = 32
TA = 256
NTILES_A = NTH // TA
TB = 512
NTB = NT // TB
SCL = 1.0 / float(np.sqrt(np.float32(128)))
G16 = 0.0625


def build_nc(debug=(), upto="E", with_cc=True, a_tiles=None, a_skip=()):
    nc = bass.Bass("TRN2", target_bir_lowering=False)

    def dram(name, shape, dt):
        k = "ExternalOutput" if name in debug else "Internal"
        return nc.dram_tensor(name, shape, dt, kind=k).ap()

    ei = lambda n, s, d: nc.dram_tensor(n, s, d, kind="ExternalInput").ap()
    xT = ei("xT", [4096, NTH], F32)
    cosT = ei("cosT", [128, NTH], F32)
    sinT = ei("sinT", [128, NTH], F32)
    mg_d = ei("mg", [128, 128], F32)
    mA_d = ei("mA", [128, 128], F32)
    mAf_d = ei("mAf", [128, 128], F32)
    mB_d = ei("mB", [128, 128], F32)
    idb_d = ei("idb", [128, 128], BF16)
    idf_d = ei("idf", [128, 128], F32)
    Rt_d = ei("Rt", [128, 128], F32)
    ohv_d = ei("ohv", [128, 8], F32)
    wq_d = ei("wq", [32, 128, 4096], F32)
    wk_d = ei("wk", [8, 128, 4096], F32)
    wv_d = ei("wv", [8, 128, 4096], F32)
    wo_d = ei("wo", [32, 128, 4096], F32)
    wf_d = ei("wf", [32, 128, 4096], F32)
    wa_d = ei("wa", [2, 128, 4096], F32)
    wb_d = ei("wb", [32, 128, 256], F32)
    bias_d = ei("bias_pp", [128, 32], F32)
    cvec_d = ei("cvec", [128, 2], F32)
    svec_d = ei("svec", [128, 2], F32)

    outT = nc.dram_tensor("outT", [4096, NT], F32, kind="ExternalOutput").ap()

    sk_sp = dram("sk_sp", [KVH, 128, NTH], BF16)
    v_sp = dram("v_sp", [NTH, 1024], BF16)
    kt_sp = dram("kt_sp", [KVH, 128, NTH], BF16)
    ecs_sp = dram("ecs_sp", [KVH, 128, NTH], BF16)
    sc_sp = dram("sc_sp", [KVH, 128, NT], BF16)
    qt_sp = dram("qt_sp", [NH, 128, NT], BF16)
    sq_sp = dram("sq_sp", [NH, 128, NT], BF16)
    rq_sp = dram("rq_sp", [NH, 128, 16], F32)
    ot_sp = dram("ot_sp", [NH, 128, NT], BF16)
    s1_sp = dram("s1_sp", [FC, 128, NT], F32)
    x1_sp = dram("x1_sp", [FC, 128, NT], F32)
    s2_sp = dram("s2_sp", [FC, 128, NT], F32)
    ursum = dram("ursum", [128, KVH * 129], F32)
    cc_in = nc.dram_tensor("cc_in", [128, KVH * 129], F32, kind="Internal").ap()
    cc_out = nc.dram_tensor("cc_out", [NCORE * 128, KVH * 129], F32,
                            kind="Internal", addr_space="Shared").ap()

    es = contextlib.ExitStack()
    with es:
        sb = lambda n, s, d: es.enter_context(nc.sbuf_tensor(n, s, d))
        mgs = sb("mgs", [128, 128], F32)
        mAs = sb("mAs", [128, 128], F32)
        mAfs = sb("mAfs", [128, 128], F32)
        mBs = sb("mBs", [128, 128], F32)
        idbs = sb("idbs", [128, 128], BF16)
        idfs = sb("idfs", [128, 128], F32)
        Rts = sb("Rts", [128, 128], F32)
        Rtb = sb("Rtb", [128, 128], BF16)
        ohs = sb("ohs", [128, 8], F32)
        ones = sb("ones", [128, 1], F32)
        onesb = sb("onesb", [128, 1], BF16)
        twosb = sb("twosb", [128, 1], BF16)
        twos = sb("twos", [128, 1], F32)
        onesr = sb("onesr", [1, 128], F32)
        zer = sb("zer", [128, TA], F32)
        rk_all = sb("rk_all", [128, KVH * NCHH], F32)
        Ec_all = sb("Ec_all", [128, KVH * NCHH], F32)
        E_run = sb("E_run", [128, KVH], F32)
        statacc = sb("statacc", [128, 32], F32)
        ab1 = sb("ab1", [128, 32], F32)
        ab2 = sb("ab2", [128, 32], F32)
        bias_s = sb("bias_s", [128, 32], F32)
        wvec_s = sb("wvec_s", [128, 2], F32)
        cvec_s = sb("cvec_s", [128, 2], F32)
        svec_s = sb("svec_s", [128, 2], F32)
        eab = contextlib.ExitStack()
        cosw = eab.enter_context(nc.sbuf_tensor("cosw", [128, NTH], F32))
        sinw = eab.enter_context(nc.sbuf_tensor("sinw", [128, NTH], F32))

        pA = es.enter_context(nc.psum_tensor([128, 512], F32))
        pB = es.enter_context(nc.psum_tensor([128, 512], F32))
        pC = es.enter_context(nc.psum_tensor([128, 512], F32))
        pD = es.enter_context(nc.psum_tensor([128, 512], F32))
        pE = es.enter_context(nc.psum_tensor([128, 512], F32))
        pF = es.enter_context(nc.psum_tensor([128, 512], BF16))
        pG = es.enter_context(nc.psum_tensor([128, 129], F32))
        pT = es.enter_context(nc.psum_tensor([128, 128], BF16))

        SX = es.enter_context(nc.semaphore("SX"))
        SW = es.enter_context(nc.semaphore("SW"))
        SS = es.enter_context(nc.semaphore("SS"))
        ST_ = es.enter_context(nc.semaphore("STp"))
        SV = es.enter_context(nc.semaphore("SVp"))
        SC_ = es.enter_context(nc.semaphore("SCp"))
        SG = es.enter_context(nc.semaphore("SGp"))
        SY = es.enter_context(nc.semaphore("SYp"))
        PH = [es.enter_context(nc.semaphore(f"PH{i}")) for i in range(5)]

        block = es.enter_context(nc.Block())

        cnt = {"SX": 0, "SW": 0, "SS": 0, "ST": 0, "SV": 0, "SC": 0, "SG": 0,
               "SY": 0}
        sem_of = {"SX": SX, "SW": SW, "SS": SS, "ST": ST_, "SV": SV,
                  "SC": SC_, "SG": SG, "SY": SY}

        def inc(ins, name, n=1):
            ins.then_inc(sem_of[name], n)
            cnt[name] += n
            return cnt[name]

        phase_no = [1]

        def phase_end(eng, idx):
            eng.nop().then_inc(PH[idx], 1)

        def phase_start(eng):
            for i in range(5):
                eng.wait_ge(PH[i], phase_no[0] - 1)

        @block.sync
        def _(sync):
            for dst, src in [(cosw, cosT), (sinw, sinT), (mgs, mg_d),
                             (mAs, mA_d), (mAfs, mAf_d), (mBs, mB_d),
                             (idbs, idb_d), (idfs, idf_d), (Rts, Rt_d),
                             (ohs, ohv_d), (bias_s, bias_d),
                             (cvec_s, cvec_d), (svec_s, svec_d)]:
                inc(sync.dma_start(out=dst[:], in_=src[:]), "SW", 16)
        const_loads = cnt["SW"]

        @block.vector
        def _(vector):
            vector.wait_ge(SW, const_loads)
            inc(vector.memset(ones[:], 1.0), "SV")
            inc(vector.memset(onesb[:], 1.0), "SV")
            inc(vector.memset(twosb[:], 2.0), "SV")
            inc(vector.tensor_copy(Rtb[:], Rts[:]), "SV")
            inc(vector.memset(twos[:], 2.0), "SV")
            inc(vector.memset(onesr[:], 1.0), "SV")
            inc(vector.memset(zer[:], 0.0), "SV")
            inc(vector.memset(E_run[:], 1.0), "SV")
            inc(vector.memset(statacc[:], 0.0), "SV")
            inc(vector.tensor_tensor(wvec_s[:], cvec_s[:], svec_s[:], ALU.mult),
                "SV")
        v_const = cnt["SV"]

        with contextlib.ExitStack() as ea:
            sba = lambda n, s, d: ea.enter_context(nc.sbuf_tensor(n, s, d))
            xa_b = [sba(f"xa{i}", [128, FC * TA], BF16) for i in range(2)]
            wk_sb = sba("wk_sb", [128, 4 * 4096], BF16)
            wv_sb = sba("wv_sb", [128, 4 * 4096], BF16)
            kT_f = [sba(f"kT{i}", [128, TA], F32) for i in range(2)]
            e1_f = sba("e1_f", [128, TA], F32)
            L_f = [sba(f"L{i}", [128, TA], F32) for i in range(2)]
            P_f = [sba(f"P{i}", [128, TA], F32) for i in range(2)]
            kp_f = [sba(f"kp{i}", [128, TA], F32) for i in range(2)]
            ek_f = [sba(f"ek{i}", [128, TA], F32) for i in range(2)]
            t1_f = sba("t1_f", [128, TA], F32)
            m_f = sba("m_f", [128, TA], F32)
            dc_f = [sba(f"dc{i}", [128, 2], F32) for i in range(2)]
            ktil_b = [sba(f"ktil{i}", [128, TA], BF16) for i in range(2)]
            ecs_b = [sba(f"ecsb{i}", [128, TA], BF16) for i in range(2)]
            sk_b = [sba(f"skb{i}", [128, TA], BF16) for i in range(2)]
            vt_b = [sba(f"vtb{i}", [128, TA], BF16) for i in range(2)]
            vtm_b = [sba(f"vtm{i}", [128, 256], BF16) for i in range(2)]
            ktm_b = [sba(f"ktm{i}", [128, 256], BF16) for i in range(2)]
            sct_b = [sba(f"sct{i}", [128, 256], BF16) for i in range(2)]
            tmpS = sba("tmpS", [128, 128], F32)
            S_run = sba("S_run", [128, KVH * 128], F32)
            cc_st = sba("cc_st", [128, KVH * 129], F32)

            @block.vector
            def _(vector):
                vector.wait_ge(SV, cnt["SV"])
                inc(vector.memset(S_run[:], 0.0), "SV")

            HNIT = NTILES_A * 4
            NIT = 2 * HNIT
            if a_tiles is not None:
                NIT = a_tiles * 4
            ST0, SC0, SV0, SS0 = (cnt[k] for k in ("ST", "SC", "SV", "SS"))
            TPI, CPI, VPI = 10, 13, 19

            def split_it(it):
                half, rem = divmod(it, HNIT)
                ti, jj = divmod(rem, 4)
                return half, ti, jj, half * 4 + jj

            def stv(it, k):
                return ST0 + it * TPI + k if it >= 0 else ST0

            def scv(it, k):
                return SC0 + it * CPI + k if it >= 0 else SC0

            def svv(it, k):
                return SV0 + it * VPI + k if it >= 0 else SV0

            _ss_cum = []
            _tot = 0
            for _it in range(NIT):
                _, _ti, _, _ = split_it(_it)
                _tot += 5 if _ti == 0 else 6
                _ss_cum.append(_tot)

            def ssv(it):
                return SS0 + 16 * _ss_cum[it] if it >= 0 else SS0

            def sx_tile(half, ti):
                return (half * 17 + 8 + ti + 1) * 16

            @block.gpsimd
            def _(gpsimd):
                phase_start(gpsimd)
                for half in range(2):
                    if half * HNIT >= NIT:
                        break
                    if half == 1:
                        gpsimd.wait_ge(ST_, stv(HNIT - 1, TPI))
                    for jj in range(4):
                        j = half * 4 + jj
                        if cnt["SX"] > 0:
                            gpsimd.wait_ge(SX, cnt["SX"])
                        inc(gpsimd.dma_start(out=wk_sb[:, jj * 4096:(jj + 1) * 4096],
                                             in_=wk_d[j]), "SX", 16)
                        gpsimd.wait_ge(SX, cnt["SX"])
                        inc(gpsimd.dma_start(out=wv_sb[:, jj * 4096:(jj + 1) * 4096],
                                             in_=wv_d[j]), "SX", 16)
                    for ti in range(NTILES_A):
                        gti = half * NTILES_A + ti
                        if half * HNIT + ti * 4 >= NIT:
                            break
                        if gti >= 2:
                            lt = gti - 2
                            lh, lti = divmod(lt, NTILES_A)
                            gpsimd.wait_ge(ST_, stv(lh * HNIT + lti * 4 + 3, 2))
                        gpsimd.wait_ge(SX, cnt["SX"])
                        inc(gpsimd.dma_start(
                            out=xa_b[gti % 2][:].rearrange("p (fc t) -> p fc t", fc=FC),
                            in_=xT[:, ti * TA:(ti + 1) * TA]
                            .rearrange("(fc p) t -> p fc t", p=128)), "SX", 16)

            @block.tensor
            def _(tensor):
                phase_start(tensor)
                tensor.wait_ge(SV, v_const)
                for it in range(NIT):
                    half, ti, jj, j = split_it(it)
                    par = it % 2
                    gti = half * NTILES_A + ti
                    if it >= 2:
                        tensor.wait_ge(SC_, scv(it - 2, CPI))
                        tensor.wait_ge(SV, svv(it - 2, VPI))
                    if jj == 0:
                        tensor.wait_ge(SX, sx_tile(half, ti))
                    xa = xa_b[gti % 2]
                    if it >= 1:
                        tensor.wait_ge(SC_, scv(it - 1, 1))
                    for fc in range(FC):
                        ins = tensor.matmul(pA[:, 0:TA],
                                            wk_sb[:, jj * 4096 + fc * 128: jj * 4096 + fc * 128 + 128],
                                            xa[:, fc * TA:(fc + 1) * TA],
                                            start=(fc == 0), stop=(fc == FC - 1))
                    inc(ins, "ST")
                    if it >= 1:
                        tensor.wait_ge(SC_, scv(it - 1, 2))
                    for fc in range(FC):
                        ins = tensor.matmul(pB[:, 0:TA],
                                            wv_sb[:, jj * 4096 + fc * 128: jj * 4096 + fc * 128 + 128],
                                            xa[:, fc * TA:(fc + 1) * TA],
                                            start=(fc == 0), stop=(fc == FC - 1))
                    inc(ins, "ST")
                    tensor.wait_ge(SC_, scv(it, 1))
                    if it >= 1:
                        tensor.wait_ge(SV, svv(it - 1, 6))
                    inc(tensor.matmul(pC[:, 0:TA], Rts[:], kT_f[par][:],
                                      start=True, stop=True), "ST")
                    tensor.wait_ge(SC_, scv(it, 5))
                    if it >= 1:
                        tensor.wait_ge(SV, svv(it - 1, 4))
                    tensor.matmul(pG[:, 0:1], ek_f[par][:, 0:128], ones[:],
                                  start=True, stop=False)
                    inc(tensor.matmul(pG[:, 1:2], ek_f[par][:, 128:256], ones[:],
                                      start=False, stop=True), "ST")
                    tensor.wait_ge(SC_, scv(it, 2))
                    if it >= 1:
                        tensor.wait_ge(SC_, scv(it - 1, 13))
                    inc(tensor.matmul(pT[:], vt_b[par][:, 0:128], idbs[:],
                                      is_transpose=True, start=True, stop=True),
                        "ST")
                    tensor.wait_ge(SC_, scv(it, 7))
                    inc(tensor.matmul(pT[:], vt_b[par][:, 128:256], idbs[:],
                                      is_transpose=True, start=True, stop=True),
                        "ST")
                    tensor.wait_ge(SC_, scv(it, 8))
                    tensor.wait_ge(SC_, scv(it, 9))
                    inc(tensor.matmul(pT[:], ktil_b[par][:, 0:128], idbs[:],
                                      is_transpose=True, start=True, stop=True),
                        "ST")
                    tensor.wait_ge(SC_, scv(it, 12))
                    inc(tensor.matmul(pT[:], ktil_b[par][:, 128:256], idbs[:],
                                      is_transpose=True, start=True, stop=True),
                        "ST")
                    tensor.wait_ge(SC_, scv(it, 13))
                    if it >= 1:
                        tensor.wait_ge(SV, svv(it - 1, 11))
                    inc(tensor.matmul(pD[:, 0:128], ktm_b[par][:, 0:128],
                                      vtm_b[par][:, 0:128], start=True, stop=True),
                        "ST")
                    if it >= 1:
                        tensor.wait_ge(SV, svv(it - 1, 16))
                    inc(tensor.matmul(pE[:, 0:128], ktm_b[par][:, 128:256],
                                      vtm_b[par][:, 128:256], start=True, stop=True),
                        "ST")
                phase_end(tensor, 0)

            @block.scalar
            def _(scalar):
                phase_start(scalar)

                def sop(fn, name="SC"):
                    scalar.wait_ge(SC_, cnt["SC"])
                    inc(fn(), name)

                for it in range(NIT):
                    half, ti, jj, j = split_it(it)
                    par = it % 2
                    cA = j * NCHH + 2 * ti
                    if it >= 2:
                        scalar.wait_ge(ST_, stv(it - 2, TPI))
                        scalar.wait_ge(SV, svv(it - 2, VPI))
                        scalar.wait_ge(SS, ssv(it - 2))
                    scalar.wait_ge(ST_, stv(it, 1))
                    sop(lambda: scalar.copy(kT_f[par][:], pA[:, 0:TA]))
                    scalar.wait_ge(ST_, stv(it, 2))
                    sop(lambda: scalar.copy(vt_b[par][:], pB[:, 0:TA]))
                    sop(lambda: scalar.activation(e1_f[:], kT_f[par][:], AF.Exp,
                                                  scale=-1.0))
                    sop(lambda: scalar.activation(L_f[par][:], e1_f[:], AF.Ln,
                                                  bias=1.0, scale=1.0))
                    sop(lambda: scalar.activation(ek_f[par][:], kT_f[par][:],
                                                  AF.Exp))
                    scalar.wait_ge(SV, svv(it, 2))
                    sop(lambda: scalar.activation(dc_f[par][:, 0:1],
                                                  P_f[par][:, 127:128],
                                                  AF.Exp, scale=-G16))
                    scalar.wait_ge(ST_, stv(it, 5))
                    sop(lambda: scalar.copy(vtm_b[par][:, 0:128], pT[:]))
                    scalar.wait_ge(ST_, stv(it, 6))
                    sop(lambda: scalar.copy(vtm_b[par][:, 128:256], pT[:]))
                    scalar.wait_ge(SV, svv(it, 3))
                    sop(lambda: scalar.activation(ktil_b[par][:], kp_f[par][:],
                                                  AF.Exp))
                    sop(lambda: scalar.activation(ecs_b[par][:], P_f[par][:],
                                                  AF.Exp, scale=-G16))
                    sop(lambda: scalar.activation(dc_f[par][:, 1:2],
                                                  P_f[par][:, 255:256],
                                                  AF.Exp, scale=-G16))
                    scalar.wait_ge(ST_, stv(it, 7))
                    scalar.wait_ge(SV, svv(it, 4))
                    sop(lambda: scalar.activation(ktm_b[par][:, 0:128], pT[:],
                                                  AF.Copy,
                                                  scale=rk_all[:, cA:cA + 1]))
                    scalar.wait_ge(ST_, stv(it, 8))
                    sop(lambda: scalar.activation(ktm_b[par][:, 128:256], pT[:],
                                                  AF.Copy,
                                                  scale=rk_all[:, cA + 1:cA + 2]))
                phase_end(scalar, 2)

            @block.vector
            def _(vector):
                phase_start(vector)

                def vop(fn):
                    vector.wait_ge(SV, cnt["SV"])
                    inc(fn(), "SV")

                for it in range(NIT):
                    half, ti, jj, j = split_it(it)
                    par = it % 2
                    cA = j * NCHH + 2 * ti
                    jslice = S_run[:, j * 128:(j + 1) * 128]
                    if it >= 2:
                        vector.wait_ge(SC_, scv(it - 2, CPI))
                        vector.wait_ge(ST_, stv(it - 2, TPI))
                        vector.wait_ge(SS, ssv(it - 2))
                    vector.wait_ge(SC_, scv(it, 4))
                    vop(lambda: vector.tensor_tensor_scan(
                        P_f[par][:, 0:128], L_f[par][:, 0:128],
                        zer[:, 0:128], 0.0, ALU.add, ALU.add))
                    vop(lambda: vector.tensor_tensor_scan(
                        P_f[par][:, 128:256], L_f[par][:, 128:256],
                        zer[:, 0:128], 0.0, ALU.add, ALU.add))
                    vop(lambda: vector.scalar_tensor_tensor(
                        kp_f[par][:], P_f[par][:], G16, kT_f[par][:],
                        ALU.mult, ALU.add))
                    vector.wait_ge(ST_, stv(it, 4))
                    vop(lambda: vector.reciprocal(rk_all[:, cA:cA + 2],
                                                  pG[:, 0:2]))
                    vop(lambda: vector.tensor_tensor(
                        t1_f[:], kT_f[par][:], cosw[:, ti * TA:(ti + 1) * TA],
                        ALU.mult))
                    vector.wait_ge(ST_, stv(it, 3))
                    vop(lambda: vector.tensor_tensor(
                        m_f[:], pC[:, 0:TA], sinw[:, ti * TA:(ti + 1) * TA],
                        ALU.mult))
                    vop(lambda: vector.tensor_tensor(sk_b[par][:], t1_f[:],
                                                     m_f[:], ALU.add))
                    vop(lambda: vector.tensor_copy(Ec_all[:, cA:cA + 1],
                                                   E_run[:, j:j + 1]))
                    vector.wait_ge(SC_, scv(it, 6))
                    vop(lambda: vector.tensor_scalar(
                        E_run[:, j:j + 1], E_run[:, j:j + 1], dc_f[par][:, 0:1],
                        None, ALU.mult))
                    vop(lambda: vector.tensor_copy(sct_b[par][:, 0:128], jslice))
                    vector.wait_ge(ST_, stv(it, 9))
                    vop(lambda: vector.tensor_tensor(tmpS[:], jslice,
                                                     pD[:, 0:128], ALU.add))
                    vop(lambda: vector.tensor_scalar(jslice, tmpS[:],
                                                     dc_f[par][:, 0:1], None,
                                                     ALU.mult))
                    vop(lambda: vector.tensor_copy(Ec_all[:, cA + 1:cA + 2],
                                                   E_run[:, j:j + 1]))
                    vector.wait_ge(SC_, scv(it, 11))
                    vop(lambda: vector.tensor_scalar(
                        E_run[:, j:j + 1], E_run[:, j:j + 1], dc_f[par][:, 1:2],
                        None, ALU.mult))
                    vop(lambda: vector.tensor_copy(sct_b[par][:, 128:256],
                                                   jslice))
                    vector.wait_ge(ST_, stv(it, 10))
                    vop(lambda: vector.tensor_tensor(tmpS[:], jslice,
                                                     pE[:, 0:128], ALU.add))
                    vop(lambda: vector.tensor_scalar(jslice, tmpS[:],
                                                     dc_f[par][:, 1:2], None,
                                                     ALU.mult))
                    if it % HNIT == 3:
                        h0 = (it // HNIT) * 4
                        vop(lambda: vector.memset(
                            S_run[:, h0 * 128:(h0 + 4) * 128], 0.0))
                        vop(lambda: vector.memset(E_run[:, h0:h0 + 4], 1.0))
                    else:
                        inc(vector.nop(), "SV")
                        inc(vector.nop(), "SV")
                for j in range(KVH):
                    vector.wait_ge(SV, cnt["SV"])
                    inc(vector.tensor_copy(cc_st[:, j * 129:j * 129 + 128],
                                           S_run[:, j * 128:(j + 1) * 128]), "SV")
                    vector.wait_ge(SV, cnt["SV"])
                    inc(vector.tensor_copy(
                        cc_st[:, j * 129 + 128:j * 129 + 129], E_run[:, j:j + 1]),
                        "SV")
                phase_end(vector, 1)
            SV_CC = cnt["SV"]

            @block.sync
            def _(sync):
                phase_start(sync)
                for it in range(NIT):
                    half, ti, jj, j = split_it(it)
                    par = it % 2
                    t0 = ti * TA
                    sync.wait_ge(SC_, scv(it, 9))
                    sync.wait_ge(SS, cnt["SS"])
                    inc(sync.dma_start(out=kt_sp[j][:, t0:t0 + TA],
                                       in_=ktil_b[par][:]), "SS", 16)
                    sync.wait_ge(SC_, scv(it, 10))
                    sync.wait_ge(SS, cnt["SS"])
                    inc(sync.dma_start(out=ecs_sp[j][:, t0:t0 + TA],
                                       in_=ecs_b[par][:]), "SS", 16)
                    sync.wait_ge(SC_, scv(it, 8))
                    sync.wait_ge(SS, cnt["SS"])
                    inc(sync.dma_start(out=v_sp[t0:t0 + 128, j * 128:(j + 1) * 128],
                                       in_=vtm_b[par][:, 0:128]), "SS", 16)
                    sync.wait_ge(SS, cnt["SS"])
                    inc(sync.dma_start(out=v_sp[t0 + 128:t0 + 256,
                                               j * 128:(j + 1) * 128],
                                       in_=vtm_b[par][:, 128:256]), "SS", 16)
                    sync.wait_ge(SV, svv(it, 7))
                    sync.wait_ge(SS, cnt["SS"])
                    inc(sync.dma_start(out=sk_sp[j][:, t0:t0 + TA],
                                       in_=sk_b[par][:]), "SS", 16)
                    if ti >= 1:
                        sync.wait_ge(SV, svv(it, 15))
                        sync.wait_ge(SS, cnt["SS"])
                        inc(sync.dma_start(
                            out=sc_sp[j][:, (ti - 1) * TA:ti * TA],
                            in_=sct_b[par][:]), "SS", 16)
                sync.wait_ge(SV, SV_CC)
                sync.wait_ge(SS, cnt["SS"])
                inc(sync.dma_start(out=cc_in[:], in_=cc_st[:]), "SS", 16)
                sync.wait_ge(SS, cnt["SS"])
                inc(sync.dma_start(out=ursum[:], in_=cc_st[:]), "SS", 16)
                phase_end(sync, 4)
            A_SS_END = cnt["SS"]

            @block.gpsimd
            def _(gpsimd):
                gpsimd.wait_ge(SS, A_SS_END)
                if with_cc:
                    inc(gpsimd.collective_compute(
                        "AllGather", ALU.bypass,
                        replica_groups=[list(range(NCORE))],
                        ins=[cc_in[:]],
                        outs=[cc_out[:]],
                    ), "SG")
                else:
                    inc(gpsimd.nop(), "SG")
                phase_end(gpsimd, 3)
        AG_DONE = cnt["SG"]
        phase_no[0] += 1

        if upto == "A":
            @block.sync
            def _(sync):
                phase_start(sync)
                n0 = cnt["SS"]
                for fc in range(FC):
                    for tt in range(NT // 256):
                        inc(sync.dma_start(
                            out=outT[fc * 128:(fc + 1) * 128,
                                     tt * 256:(tt + 1) * 256],
                            in_=zer[:, 0:256]), "SS", 16)
                sync.wait_ge(SS, cnt["SS"])
                sync.wait_ge(SG, AG_DONE)
            return nc

        def _tail_zeros():
            @block.sync
            def _(sync):
                phase_start(sync)
                for fc in range(FC):
                    for tt in range(NT // 256):
                        sync.wait_ge(SS, cnt["SS"])
                        inc(sync.dma_start(
                            out=outT[fc * 128:(fc + 1) * 128,
                                     tt * 256:(tt + 1) * 256],
                            in_=zer[:, 0:256]), "SS", 16)
                sync.wait_ge(SS, cnt["SS"])
                sync.wait_ge(SG, AG_DONE)

        with contextlib.ExitStack() as eb:
            sbb = lambda n, s, d: eb.enter_context(nc.sbuf_tensor(n, s, d))
            x_b1 = sbb("x_b1", [128, FC * NT], BF16)
            wq_sb = [sbb(f"wqs{i}", [128, 4096], BF16) for i in range(2)]
            ecs_ld = [sbb(f"ecsl{i}", [128, NT], BF16) for i in range(2)]
            qT_q = sbb("qT_q", [128, NT], F32)
            qTb = sbb("qTb", [128, NT], BF16)
            eq_f = sbb("eq_f", [128, NT], BF16)
            t1q = sbb("t1q", [128, TB], F32)
            mq = sbb("mq", [128, TB], F32)
            qt_st = sbb("qt_st", [128, NT], BF16)
            sq_st = sbb("sq_st", [128, NT], BF16)
            rq_st = sbb("rq_st", [128, 16], F32)

            T0, C0, V0, S0 = (cnt[k] for k in ("ST", "SC", "SV", "SS"))
            TPIq, CPIq, VPIq = 9, 6, 17

            def stq(h, k):
                return T0 + h * TPIq + k if h >= 0 else T0

            def scq(h, k):
                return C0 + h * CPIq + k if h >= 0 else C0

            def svq(h, k):
                return V0 + h * VPIq + k if h >= 0 else V0

            ss_after_load = {}
            ss_after_spill = {}


            @block.sync
            def _(sync):
                phase_start(sync)
                for h in range(NH):
                    j = h // 4
                    if h % 4 == 0:
                        if j >= 2:
                            sync.wait_ge(SV, svq(4 * (j - 2) + 3, 16))
                        sync.wait_ge(SS, cnt["SS"])
                        inc(sync.dma_start(out=ecs_ld[j % 2][:],
                                           in_=ecs_sp[j][:, HALO:NTH]), "SS", 16)
                        ss_after_load[j] = cnt["SS"]
                    sync.wait_ge(SV, svq(h, 16))
                    sync.wait_ge(SS, cnt["SS"])
                    inc(sync.dma_start(out=qt_sp[h][:], in_=qt_st[:]), "SS", 16)
                    sync.wait_ge(SV, svq(h, 12))
                    sync.wait_ge(SS, cnt["SS"])
                    inc(sync.dma_start(out=sq_sp[h][:], in_=sq_st[:]), "SS", 16)
                    sync.wait_ge(SV, svq(h, 17))
                    sync.wait_ge(SS, cnt["SS"])
                    inc(sync.dma_start(out=rq_sp[h][:], in_=rq_st[:]), "SS", 16)
                    ss_after_spill[h] = cnt["SS"]
                phase_end(sync, 4)

            @block.gpsimd
            def _(gpsimd):
                phase_start(gpsimd)
                gpsimd.wait_ge(SS, S0)
                for i in range(8):
                    gpsimd.wait_ge(SX, cnt["SX"])
                    inc(gpsimd.dma_start(
                        out=x_b1[:, i * 4 * NT:(i + 1) * 4 * NT]
                        .rearrange("p (fc t) -> p fc t", fc=4),
                        in_=xT[i * 4 * 128:(i + 1) * 4 * 128, HALO:NTH]
                        .rearrange("(fc p) t -> p fc t", p=128)), "SX", 16)
                for h in range(NH):
                    if h >= 2:
                        gpsimd.wait_ge(ST_, stq(h - 2, 4))
                    gpsimd.wait_ge(SX, cnt["SX"])
                    inc(gpsimd.dma_start(out=wq_sb[h % 2][:], in_=wq_d[h]),
                        "SX", 16)
                phase_end(gpsimd, 3)
            SXq0 = T0 * 0 + cnt["SX"] - NH * 16

            @block.tensor
            def _(tensor):
                phase_start(tensor)
                for h in range(NH):
                    projb = [pA, pB, pC, pD]
                    tensor.wait_ge(SX, SXq0 + (h + 1) * 16)
                    if h >= 1:
                        tensor.wait_ge(SV, svq(h - 1, 17))
                        tensor.wait_ge(SC_, scq(h - 1, 6))
                    for tt in range(NTB):
                        for fc in range(FC):
                            ins = tensor.matmul(
                                projb[tt][:, 0:TB],
                                wq_sb[h % 2][:, fc * 128:(fc + 1) * 128],
                                x_b1[:, fc * NT + tt * TB: fc * NT + (tt + 1) * TB],
                                start=(fc == 0), stop=(fc == FC - 1))
                        inc(ins, "ST")
                    for tt in range(NTB):
                        tensor.wait_ge(SC_, scq(h, 6))
                        if tt == 0:
                            if h >= 1:
                                tensor.wait_ge(SV, svq(h - 1, 11))
                        else:
                            tensor.wait_ge(SV, svq(h, 3 * (tt - 1) + 2))
                        inc(tensor.matmul(pE[:, 0:TB], Rtb[:],
                                          qTb[:, tt * TB:(tt + 1) * TB],
                                          start=True, stop=True), "ST")
                    tensor.wait_ge(SC_, scq(h, 5))
                    if h >= 1:
                        tensor.wait_ge(SV, svq(h - 1, 17))
                    for sub in range(16):
                        ins = tensor.matmul(pG[:, sub:sub + 1],
                                            eq_f[:, sub * 128:(sub + 1) * 128],
                                            twosb[:],
                                            start=(sub == 0), stop=(sub == 15))
                    inc(ins, "ST")
                phase_end(tensor, 0)

            @block.scalar
            def _(scalar):
                phase_start(scalar)

                def sop(fn):
                    scalar.wait_ge(SC_, cnt["SC"])
                    inc(fn(), "SC")

                for h in range(NH):
                    projb = [pA, pB, pC, pD]
                    if h >= 1:
                        scalar.wait_ge(SV, svq(h - 1, 17))
                        scalar.wait_ge(ST_, stq(h - 1, 9))
                    for tt in range(NTB):
                        scalar.wait_ge(ST_, stq(h, tt + 1))
                        sop(lambda tt=tt: scalar.copy(
                            qT_q[:, tt * TB:(tt + 1) * TB], projb[tt][:, 0:TB]))
                    sop(lambda: scalar.activation(eq_f[:], qT_q[:], AF.Exp))
                    sop(lambda: scalar.copy(qTb[:], qT_q[:]))
                phase_end(scalar, 2)

            @block.vector
            def _(vector):
                phase_start(vector)

                def vop(fn):
                    vector.wait_ge(SV, cnt["SV"])
                    inc(fn(), "SV")

                for h in range(NH):
                    j = h // 4
                    if h >= 1:
                        vector.wait_ge(SS, ss_after_spill[h - 1])
                        vector.wait_ge(ST_, stq(h - 1, 9))
                    for tt in range(NTB):
                        vector.wait_ge(SC_, scq(h, tt + 1))
                        vop(lambda tt=tt: vector.tensor_tensor(
                            t1q[:], qT_q[:, tt * TB:(tt + 1) * TB],
                            cosw[:, HALO + tt * TB:HALO + (tt + 1) * TB],
                            ALU.mult))
                        vector.wait_ge(ST_, stq(h, 4 + tt + 1))
                        vop(lambda tt=tt: vector.tensor_tensor(
                            mq[:], pE[:, 0:TB],
                            sinw[:, HALO + tt * TB:HALO + (tt + 1) * TB],
                            ALU.mult))
                        vop(lambda tt=tt: vector.tensor_tensor(
                            sq_st[:, tt * TB:(tt + 1) * TB], t1q[:], mq[:],
                            ALU.add))
                    vector.wait_ge(SC_, scq(h, 5))
                    vector.wait_ge(SS, ss_after_load[j])
                    for tt in range(NTB):
                        vop(lambda tt=tt: vector.tensor_tensor(
                            qt_st[:, tt * TB:(tt + 1) * TB],
                            eq_f[:, tt * TB:(tt + 1) * TB],
                            ecs_ld[j % 2][:, tt * TB:(tt + 1) * TB],
                            ALU.mult))
                    vector.wait_ge(ST_, stq(h, 9))
                    vop(lambda: vector.reciprocal(rq_st[:], pG[:, 0:16]))
                phase_end(vector, 1)

        phase_no[0] += 1
        eab.close()
        if upto == "B1":
            _tail_zeros()
            return nc

        ebc = contextlib.ExitStack()
        oTr = ebc.enter_context(nc.sbuf_tensor("oTr", [128, NH * NT], BF16))
        with contextlib.ExitStack() as eb:
            sbb = lambda n, s, d: eb.enter_context(nc.sbuf_tensor(n, s, d))
            ag_sb = sbb("ag_sb", [128, KVH * 129], F32)
            ktil_w = [sbb(f"ktw{i}", [128, NT], BF16) for i in range(2)]
            sk_w = [sbb(f"skw{i}", [128, NT + 128], BF16) for i in range(2)]
            v_w = [sbb(f"vw{i}", [128, 17 * 129], BF16) for i in range(2)]
            sc_w = [sbb(f"scw{i}", [128, NT], BF16) for i in range(2)]
            qt_h = [sbb(f"qth{i}", [128, NT], BF16) for i in range(2)]
            sq_h = [sbb(f"sqh{i}", [128, NT], BF16) for i in range(2)]
            rq_h = [sbb(f"rqh{i}", [128, 16], F32) for i in range(2)]
            oh_acc = [sbb(f"oha{i}", [128, NT], BF16) for i in range(2)]
            Pb = [sbb(f"Pb{i}", [128, 128], F32) for i in range(2)]
            Sin_b = sbb("Sin_b", [128, 128], F32)
            egt_f = [sbb(f"egt{i}", [128, 128], F32) for i in range(2)]
            eAt_f = [sbb(f"eAt{i}", [128, 128], F32) for i in range(2)]
            eBt_f = [sbb(f"eBt{i}", [128, 128], F32) for i in range(2)]
            eg_sb = [sbb(f"egb{i}", [128, 128], BF16) for i in range(2)]
            eA_sb = [sbb(f"eAb{i}", [128, 128], BF16) for i in range(2)]
            eB_sb = [sbb(f"eBb{i}", [128, 128], BF16) for i in range(2)]
            tmp_f = [sbb(f"tmpf{i}", [128, 128], F32) for i in range(2)]
            r_swa_col = [sbb(f"rsw{i}", [128, 1], F32) for i in range(2)]

            T0, C0, V0, S0 = (cnt[k] for k in ("ST", "SC", "SV", "SS"))
            SY0 = cnt["SY"]
            TPIm, CPIm, VPIm = 5, 4, 5
            PVj = 34

            def stm(m, k):
                return T0 + m * TPIm + k if m >= 0 else T0

            def scm(m, k):
                return C0 + m * CPIm + k if m >= 0 else C0

            def svm(m, k):
                j = m // 64
                return V0 + (j + 1) * PVj + m * VPIm + k if m >= 0 else V0

            def svm_prol(j):
                return V0 + (j + 1) * PVj + j * 64 * VPIm

            ss_jload = {}
            ss_hload = {}
            ss_ohspill = {}
            ss_tr_all = [0]

            def oh_to_oTr(sync, h):
                sync.wait_ge(SV, svm((h + 1) * 16 - 1, 5))
                for c in range(0, 16, 2):
                    inc(sync.dma_start(
                        out=oTr[:, h * NT + c * 128:h * NT + (c + 1) * 128],
                        in_=oh_acc[h % 2][:, c * 128:(c + 1) * 128],
                        transpose=True), "SS", 16)
                if "ot_sp" in debug:
                    sync.wait_ge(SS, cnt["SS"])
                    inc(sync.dma_start(out=ot_sp[h][:], in_=oh_acc[h % 2][:]),
                        "SS", 16)
                ss_ohspill[h] = cnt["SS"]

            @block.sync
            def _(sync):
                phase_start(sync)
                sync.wait_ge(SG, AG_DONE)
                sync.wait_ge(SS, cnt["SS"])
                for j in range(KVH):
                    jp = j % 2
                    if j >= 2:
                        sync.wait_ge(ST_, stm((j - 1) * 64 - 1, TPIm))
                        sync.wait_ge(SV, svm((j - 1) * 64 - 1, VPIm))
                    sync.wait_ge(SS, cnt["SS"])
                    inc(sync.dma_start(
                        out=ag_sb[:].rearrange("p (a c) -> p a c", a=KVH),
                        in_=cc_out[:, j * 129:(j + 1) * 129]
                        .rearrange("(a p) c -> p a c", p=128)), "SS", 16)
                    for dst, srcap in [
                            (ktil_w[jp][:], kt_sp[j][:, HALO:NTH]),
                            (sk_w[jp][:], sk_sp[j][:, HALO - 128:NTH]),
                            (sc_w[jp][:], sc_sp[j][:])]:
                        sync.wait_ge(SS, cnt["SS"])
                        inc(sync.dma_start(out=dst, in_=srcap), "SS", 16)
                    for k in range(17):
                        sync.wait_ge(SS, cnt["SS"])
                        inc(sync.dma_start(
                            out=v_w[jp][:, k * 129:k * 129 + 128],
                            in_=v_sp[128 + k * 128:256 + k * 128,
                                     j * 128:(j + 1) * 128]), "SS", 16)
                    ss_jload[j] = cnt["SS"]
                    for hh in range(4):
                        h = j * 4 + hh
                        hp = h % 2
                        if h >= 2:
                            sync.wait_ge(ST_, stm((h - 1) * 16 - 1, TPIm))
                        for dst, srcap in [(qt_h[hp][:], qt_sp[h][:]),
                                           (sq_h[hp][:], sq_sp[h][:]),
                                           (rq_h[hp][:], rq_sp[h][:])]:
                            sync.wait_ge(SS, cnt["SS"])
                            inc(sync.dma_start(out=dst, in_=srcap), "SS", 16)
                        ss_hload[h] = cnt["SS"]
                        if h >= 1:
                            oh_to_oTr(sync, h - 1)
                oh_to_oTr(sync, NH - 1)
                ss_tr_all[0] = cnt["SS"]
                phase_end(sync, 4)

            @block.gpsimd
            def _(gpsimd):
                phase_start(gpsimd)
                phase_end(gpsimd, 3)

            @block.tensor
            def _(tensor):
                phase_start(tensor)
                for m in range(NH * 16):
                    h, c = divmod(m, 16)
                    j = h // 4
                    hp, p2 = h % 2, m % 2
                    scg = pA if p2 == 0 else pB
                    scA = pC if p2 == 0 else pD
                    if c == 0:
                        tensor.wait_ge(SS, ss_hload[h])
                        tensor.wait_ge(SS, ss_jload[j])
                        tensor.wait_ge(SV, svm_prol(j))
                    if m >= 2:
                        tensor.wait_ge(SC_, scm(m - 2, 1))
                        tensor.wait_ge(SC_, scm(m - 2, 2))
                    inc(tensor.matmul(scg[:, 0:128], ktil_w[j % 2][:, c * 128:(c + 1) * 128],
                                      qt_h[hp][:, c * 128:(c + 1) * 128],
                                      start=True, stop=True), "ST")
                    inc(tensor.matmul(scA[:, 0:128], sk_w[j % 2][:, c * 128:(c + 1) * 128],
                                      sq_h[hp][:, c * 128:(c + 1) * 128],
                                      start=True, stop=True), "ST")
                    if m >= 1:
                        tensor.wait_ge(SC_, scm(m - 1, 4))
                        tensor.wait_ge(SV, svm(m - 1, 4))
                    inc(tensor.matmul(pE[:, 0:128],
                                      sk_w[j % 2][:, 128 + c * 128:256 + c * 128],
                                      sq_h[hp][:, c * 128:(c + 1) * 128],
                                      start=True, stop=True), "ST")
                    tensor.wait_ge(SV, svm(m, 1))
                    if m >= 1:
                        tensor.wait_ge(SV, svm(m - 1, 5))
                    tensor.matmul(pG[:, 0:128], eg_sb[p2][:],
                                  v_w[j % 2][:, (c + 1) * 129:(c + 1) * 129 + 128],
                                  start=True, stop=False)
                    inc(tensor.matmul(pG[:, 0:128], qt_h[hp][:, c * 128:(c + 1) * 128],
                                      sc_w[j % 2][:, c * 128:(c + 1) * 128],
                                      start=False, stop=True), "ST")
                    tensor.wait_ge(SV, svm(m, 3))
                    tensor.wait_ge(SC_, scm(m, 3))
                    tensor.matmul(pE[:, 0:129], eA_sb[p2][:],
                                  v_w[j % 2][:, c * 129:c * 129 + 129],
                                  start=True, stop=False)
                    inc(tensor.matmul(pE[:, 0:129], eB_sb[p2][:],
                                      v_w[j % 2][:, (c + 1) * 129:(c + 2) * 129],
                                      start=False, stop=True), "ST")
                phase_end(tensor, 0)

            @block.scalar
            def _(scalar):
                phase_start(scalar)

                def sop(fn):
                    scalar.wait_ge(SC_, cnt["SC"])
                    inc(fn(), "SC")

                for m in range(NH * 16):
                    h, c = divmod(m, 16)
                    j = h // 4
                    p2 = m % 2
                    scg = pA if p2 == 0 else pB
                    scA = pC if p2 == 0 else pD
                    cR = j * NCHH + 2 + c
                    if c == 0 and h >= 1:
                        scalar.wait_ge(SV, svm(h * 16 - 1, 5))
                        scalar.wait_ge(SY, SY0 + (h - 1) * 128)
                        for tc in range(1, 16, 2):
                            inc(scalar.dma_start(
                                out=oTr[:, (h - 1) * NT + tc * 128:(h - 1) * NT + (tc + 1) * 128],
                                in_=oh_acc[(h - 1) % 2][:, tc * 128:(tc + 1) * 128],
                                transpose=True), "SY", 16)
                    if m >= 2:
                        scalar.wait_ge(SV, svm(m - 2, VPIm))
                    else:
                        scalar.wait_ge(SV, V0)
                    scalar.wait_ge(ST_, stm(m, 1))
                    sop(lambda: scalar.activation(egt_f[p2][:], scg[:, 0:128],
                                                  AF.Copy,
                                                  scale=rk_all[:, cR:cR + 1]))
                    scalar.wait_ge(ST_, stm(m, 2))
                    sop(lambda: scalar.activation(eAt_f[p2][:], scA[:, 0:128],
                                                  AF.Exp, scale=SCL))
                    scalar.wait_ge(ST_, stm(m, 3))
                    sop(lambda: scalar.activation(eBt_f[p2][:], pE[:, 0:128],
                                                  AF.Exp, scale=SCL))
                    scalar.wait_ge(ST_, stm(m, 5))
                    scalar.wait_ge(SV, svm(m, 4))
                    sop(lambda: scalar.activation(tmp_f[p2][:], pE[:, 0:128],
                                                  AF.Copy,
                                                  scale=r_swa_col[p2][:]))
                scalar.wait_ge(SV, svm(NH * 16 - 1, 5))
                scalar.wait_ge(SY, SY0 + (NH - 1) * 128)
                for tc in range(1, 16, 2):
                    inc(scalar.dma_start(
                        out=oTr[:, (NH - 1) * NT + tc * 128:(NH - 1) * NT + (tc + 1) * 128],
                        in_=oh_acc[(NH - 1) % 2][:, tc * 128:(tc + 1) * 128],
                        transpose=True), "SY", 16)
                phase_end(scalar, 2)

            @block.vector
            def _(vector):
                phase_start(vector)

                def vop(fn):
                    vector.wait_ge(SV, cnt["SV"])
                    inc(fn(), "SV")

                for j in range(KVH):
                    jp = j % 2
                    vector.wait_ge(SS, ss_jload[j])
                    nops = 0
                    vop(lambda: vector.memset(
                        v_w[jp][:].rearrange("p (k c) -> p k c", c=129)[:, :, 128:129],
                        2.0)); nops += 1
                    vop(lambda: vector.memset(Sin_b[:], 0.0)); nops += 1
                    for i in range(NCORE):
                        pcur, pprev = Pb[i % 2], Pb[(i + 1) % 2]
                        if i in (0, 4):
                            vop(lambda pcur=pcur: vector.memset(pcur[:], 0.0))
                        else:
                            off = (i - 1) * 129
                            vop(lambda pcur=pcur, pprev=pprev, off=off:
                                vector.scalar_tensor_tensor(
                                    pcur[:], pprev[:],
                                    ag_sb[:, off + 128:off + 129],
                                    ag_sb[:, off:off + 128],
                                    ALU.mult, ALU.add))
                        nops += 1
                        vop(lambda pcur=pcur, i=i: vector.scalar_tensor_tensor(
                            Sin_b[:], pcur[:], ohs[:, i:i + 1], Sin_b[:],
                            ALU.mult, ALU.add))
                        nops += 1
                    for c in range(16):
                        cR = j * NCHH + 2 + c
                        vop(lambda c=c, cR=cR: vector.scalar_tensor_tensor(
                            sc_w[jp][:, c * 128:(c + 1) * 128], Sin_b[:],
                            Ec_all[:, cR:cR + 1],
                            sc_w[jp][:, c * 128:(c + 1) * 128],
                            ALU.mult, ALU.add))
                        nops += 1
                    assert nops == PVj, nops
                    for mm_ in range(64):
                        m = j * 64 + mm_
                        h, c = divmod(m, 16)
                        p2 = m % 2
                        if m >= 2:
                            vector.wait_ge(ST_, stm(m - 2, TPIm))
                            vector.wait_ge(SC_, scm(m - 2, CPIm))
                        if c == 0:
                            vector.wait_ge(SS, ss_ohspill.get(h - 2, 0) or 0)
                            if h >= 2:
                                vector.wait_ge(SY, SY0 + (h - 1) * 8 * 16)
                        vector.wait_ge(SC_, scm(m, 1))
                        vop(lambda p2=p2: vector.tensor_tensor(
                            eg_sb[p2][:], egt_f[p2][:], mgs[:], ALU.mult))
                        vector.wait_ge(SC_, scm(m, 2))
                        mA_use = mAfs if c == 0 else mAs
                        vop(lambda p2=p2, mA_use=mA_use: vector.tensor_tensor(
                            eA_sb[p2][:], eAt_f[p2][:], mA_use[:], ALU.mult))
                        vector.wait_ge(SC_, scm(m, 3))
                        vop(lambda p2=p2: vector.tensor_tensor(
                            eB_sb[p2][:], eBt_f[p2][:], mBs[:], ALU.mult))
                        vector.wait_ge(ST_, stm(m, 5))
                        vop(lambda p2=p2: vector.reciprocal(
                            r_swa_col[p2][:], pE[:, 128:129]))
                        vector.wait_ge(ST_, stm(m, 4))
                        vector.wait_ge(SC_, scm(m, 4))
                        hp = h % 2
                        vop(lambda p2=p2, hp=hp, h=h, c=c: vector.scalar_tensor_tensor(
                            oh_acc[hp][:, c * 128:(c + 1) * 128], pG[:, 0:128],
                            rq_h[hp][:, c:c + 1], tmp_f[p2][:],
                            ALU.mult, ALU.add))
                phase_end(vector, 1)

        SY_TR_ALL = cnt["SY"]
        phase_no[0] += 1
        if upto == "B2":
            _tail_zeros()
            return nc

        with contextlib.ExitStack() as ec:
            sbc = lambda n, s, d: ec.enter_context(nc.sbuf_tensor(n, s, d))
            xw = [sbc(f"xw{i}", [128, NT], F32) for i in range(2)]
            s1_sb = [sbc(f"s1sb{i}", [128, NT], F32) for i in range(2)]
            s1sq = [sbc(f"s1sq{i}", [128, TB], BF16) for i in range(4)]
            s1bb = [sbc(f"s1bb{i}", [128, TB], BF16) for i in range(4)]
            wo_sb = [sbc(f"wos{i}", [128, 4096], BF16) for i in range(2)]
            m16 = sbc("m16", [128, 16], F32)
            e16 = sbc("e16", [128, 16], F32)
            t16 = sbc("t16", [128, 16], F32)
            v16 = sbc("v16", [128, 16], F32)

            T0, C0, V0, S0 = (cnt[k] for k in ("ST", "SC", "SV", "SS"))
            T1 = T0
            C1 = C0
            V1 = V0 + 1
            TPIc, CPIc, VPIc = 5, 8, 5

            def stc(do, k):
                return T1 + do * TPIc + k if do >= 0 else T1

            def scc(do, k):
                return C1 + do * CPIc + k if do >= 0 else C1

            def svc(do, k):
                return V1 + do * VPIc + k if do >= 0 else V1

            ss_xw = {}
            ss_s1sp = {}

            @block.sync
            def _(sync):
                phase_start(sync)
                for do in range(FC):
                    if do >= 2:
                        sync.wait_ge(SV, svc(do - 1, 4))
                    sync.wait_ge(SS, cnt["SS"])
                    inc(sync.dma_start(out=xw[do % 2][:],
                                       in_=xT[do * 128:(do + 1) * 128, HALO:NTH]),
                        "SS", 16)
                    ss_xw[do] = cnt["SS"]
                    sync.wait_ge(SV, svc(do, 4))
                    sync.wait_ge(SC_, scc(do, 8))
                    sync.wait_ge(SS, cnt["SS"])
                    inc(sync.dma_start(out=s1_sp[do][:], in_=s1_sb[do % 2][:]),
                        "SS", 16)
                    ss_s1sp[do] = cnt["SS"]
                phase_end(sync, 4)

            @block.gpsimd
            def _(gpsimd):
                phase_start(gpsimd)
                gpsimd.wait_ge(SS, S0)
                for do in range(FC):
                    if do >= 2:
                        gpsimd.wait_ge(ST_, stc(do - 1, 4))
                    gpsimd.wait_ge(SX, cnt["SX"])
                    inc(gpsimd.dma_start(out=wo_sb[do % 2][:], in_=wo_d[do]),
                        "SX", 16)
                phase_end(gpsimd, 3)
            SXc0 = cnt["SX"] - FC * 16

            @block.tensor
            def _(tensor):
                phase_start(tensor)
                tensor.wait_ge(SS, ss_tr_all[0])
                tensor.wait_ge(SY, SY_TR_ALL)
                for do in range(FC):
                    tensor.wait_ge(SX, SXc0 + (do + 1) * 16)
                    banks4 = [pA, pB, pC, pD]
                    for tt in range(NTB):
                        if do >= 1:
                            tensor.wait_ge(SV, svc(do - 1, tt + 1))
                        bank = banks4[tt]
                        for he in range(NH):
                            ins = tensor.matmul(
                                bank[:, 0:TB],
                                wo_sb[do % 2][:, he * 128:(he + 1) * 128],
                                oTr[:, he * NT + tt * TB: he * NT + (tt + 1) * TB],
                                start=(he == 0), stop=(he == NH - 1))
                        inc(ins, "ST")
                    if do >= 1:
                        tensor.wait_ge(SV, svc(do - 1, 5))
                    first = True
                    for tt in range(NTB):
                        tensor.wait_ge(SV, svc(do, tt + 1))
                        tensor.wait_ge(SC_, scc(do, 2 * tt + 2))
                        for sub in range(4):
                            col = tt * 8 + sub * 2
                            ins = tensor.matmul(
                                pG[:, col:col + 1],
                                s1bb[tt][:, sub * 128:(sub + 1) * 128],
                                onesb[:], start=first, stop=False)
                            first = False
                            last = (tt == NTB - 1 and sub == 3)
                            ins = tensor.matmul(
                                pG[:, col + 1:col + 2],
                                s1sq[tt][:, sub * 128:(sub + 1) * 128],
                                onesb[:], start=False, stop=last)
                    inc(ins, "ST")
                phase_end(tensor, 0)

            @block.scalar
            def _(scalar):
                phase_start(scalar)

                def sop(fn):
                    scalar.wait_ge(SC_, cnt["SC"])
                    inc(fn(), "SC")

                for do in range(FC):
                    if do >= 1:
                        scalar.wait_ge(ST_, stc(do - 1, 5))
                    for tt in range(NTB):
                        scalar.wait_ge(SV, svc(do, tt + 1))
                        sop(lambda do=do, tt=tt: scalar.activation(
                            s1sq[tt][:], s1_sb[do % 2][:, tt * TB:(tt + 1) * TB],
                            AF.Square))
                        sop(lambda do=do, tt=tt: scalar.copy(
                            s1bb[tt][:], s1_sb[do % 2][:, tt * TB:(tt + 1) * TB]))
                scalar.wait_ge(SV, svc(FC - 1, 5) + 6)
                sop(lambda: scalar.activation(ab1[:, 0:16], v16[:], AF.Sqrt))
                phase_end(scalar, 2)
            C_SQRT = cnt["SC"]

            @block.vector
            def _(vector):
                phase_start(vector)

                def vop(fn):
                    vector.wait_ge(SV, cnt["SV"])
                    inc(fn(), "SV")

                vop(lambda: vector.memset(statacc[:], 0.0))
                for do in range(FC):
                    if do >= 2:
                        vector.wait_ge(SC_, scc(do - 2, 8))
                        vector.wait_ge(ST_, stc(do - 2, 5))
                        vector.wait_ge(SS, ss_s1sp[do - 2])
                    for tt in range(NTB):
                        vector.wait_ge(ST_, stc(do, tt + 1))
                        vector.wait_ge(SS, ss_xw[do])
                        bank = [pA, pB, pC, pD][tt]
                        vop(lambda do=do, tt=tt, bank=bank: vector.scalar_tensor_tensor(
                            s1_sb[do % 2][:, tt * TB:(tt + 1) * TB],
                            bank[:, 0:TB], 1.0, xw[do % 2][:, tt * TB:(tt + 1) * TB],
                            ALU.mult, ALU.add))
                    vector.wait_ge(ST_, stc(do, 5))
                    vop(lambda: vector.tensor_tensor(statacc[:], statacc[:],
                                                     pG[:, 0:32], ALU.add))
                sevn = statacc[:].rearrange("p (g s) -> p g s", s=2)
                vop(lambda: vector.tensor_scalar(m16[:], sevn[:, :, 0:1],
                                                 1.0 / 4096.0, None, ALU.mult))
                vop(lambda: vector.tensor_scalar(e16[:], sevn[:, :, 1:2],
                                                 1.0 / 4096.0, None, ALU.mult))
                vop(lambda: vector.tensor_tensor(t16[:], m16[:], m16[:], ALU.mult))
                vop(lambda: vector.tensor_tensor(v16[:], e16[:], t16[:],
                                                 ALU.subtract))
                vop(lambda: vector.tensor_scalar(v16[:], v16[:], 1e-5, None,
                                                 ALU.add))
                vop(lambda: vector.reciprocal(v16[:], v16[:]))
                vector.wait_ge(SC_, C_SQRT)
                vop(lambda: vector.tensor_tensor(t16[:], m16[:], ab1[:, 0:16],
                                                 ALU.mult))
                vop(lambda: vector.tensor_scalar(ab1[:, 16:32], t16[:], -1.0,
                                                 None, ALU.mult))
                phase_end(vector, 1)

        ebc.close()
        phase_no[0] += 1
        if upto == "C":
            _tail_zeros()
            return nc

        with contextlib.ExitStack() as ed:
            sbd = lambda n, s, d: ed.enter_context(nc.sbuf_tensor(n, s, d))
            x1b = sbd("x1b", [128, FC * NT], BF16)
            xab = sbd("xab", [128, 2 * NT], BF16)
            m16d = sbd("m16d", [128, 16], F32)
            e16d = sbd("e16d", [128, 16], F32)
            t16d = sbd("t16d", [128, 16], F32)
            v16d = sbd("v16d", [128, 16], F32)

            T0, C0, V0, S0 = (cnt[k] for k in ("ST", "SC", "SV", "SS"))

            SGD0 = cnt["SG"]

            def svd2(do, k):
                return V0 + 65 + do * 5 + k if do >= 0 else V0 + 65

            def x1done(fc):
                if fc < 16:
                    return SV, V0 + 32 + 2 * fc + 2
                return SG, SGD0 + 2 * (fc - 16) + 2

            def s1buf(fc):
                return s1_ld[fc % 2] if fc < 16 else s1g[fc % 2]

            ss_s1ld = {}
            ss_x1w = {}
            ss_x1sp = {}
            ss_s2sp = {}

            ed1 = contextlib.ExitStack()
            s1_ld = [ed1.enter_context(nc.sbuf_tensor(f"s1ld{i}", [128, NT], F32))
                     for i in range(2)]
            s1g = [ed1.enter_context(nc.sbuf_tensor(f"s1g{i}", [128, NT], F32))
                   for i in range(2)]
            wa_sb = ed1.enter_context(nc.sbuf_tensor("wa_sb", [128, 4096], BF16))
            abc_a = ed1.enter_context(nc.sbuf_tensor("abc_a", [128, NT], F32))
            abc_b = ed1.enter_context(nc.sbuf_tensor("abc_b", [128, NT], F32))
            arow = ed1.enter_context(nc.sbuf_tensor("arow", [1, TB], F32))

            @block.sync
            def _(sync):
                phase_start(sync)
                sync.wait_ge(SS, S0)
                for fc in range(FC):
                    if fc >= 2 and (fc < 16 or fc >= 18):
                        fp = fc - 2
                        sync.wait_ge(SC_, C0 + 9 + fp)
                        sync.wait_ge(SS, ss_x1sp[fp])
                        sem, val = x1done(fp)
                        sync.wait_ge(sem, val)
                    sync.wait_ge(SS, cnt["SS"])
                    inc(sync.dma_start(out=s1buf(fc)[:], in_=s1_sp[fc][:]),
                        "SS", 16)
                    ss_s1ld[fc] = cnt["SS"]
                    sem, val = x1done(fc)
                    sync.wait_ge(sem, val)
                    sync.wait_ge(SS, cnt["SS"])
                    inc(sync.dma_start(out=x1_sp[fc][:], in_=s1buf(fc)[:]),
                        "SS", 16)
                    ss_x1sp[fc] = cnt["SS"]

            @block.gpsimd
            def _(gpsimd):
                phase_start(gpsimd)
                gpsimd.wait_ge(SS, S0)
                gpsimd.wait_ge(SX, cnt["SX"])
                inc(gpsimd.dma_start(out=wa_sb[:], in_=wa_d[0]), "SX", 16)
                gpsimd.wait_ge(SC_, C0 + 8)
                for fc in range(16, FC):
                    gpsimd.wait_ge(SS, ss_s1ld[fc])
                    if fc >= 18:
                        gpsimd.wait_ge(SC_, C0 + 9 + fc - 2)
                        gpsimd.wait_ge(SS, ss_x1sp[fc - 2])
                    gpsimd.wait_ge(SG, cnt["SG"])
                    inc(gpsimd.tensor_tensor(s1g[fc % 2][:], s1g[fc % 2][:],
                                             abc_a[:], ALU.mult), "SG")
                    gpsimd.wait_ge(SG, cnt["SG"])
                    inc(gpsimd.tensor_tensor(s1g[fc % 2][:], s1g[fc % 2][:],
                                             abc_b[:], ALU.add), "SG")
                gpsimd.wait_ge(ST_, T0 + 44)
                gpsimd.wait_ge(SX, cnt["SX"])
                inc(gpsimd.dma_start(out=wa_sb[:], in_=wa_d[1]), "SX", 16)
            SXd0 = cnt["SX"]

            @block.tensor
            def _(tensor):
                phase_start(tensor)
                for i in range(8):
                    bank = pC if i % 2 == 0 else pD
                    for c in range(4):
                        col = i * 4 + c
                        if col >= 1:
                            tensor.wait_ge(SV, V0 + col)
                        inc(tensor.matmul(pG[0:1, 0:128], ab1[:, col:col + 1],
                                          idfs[:], start=True, stop=True), "ST")
                    if i >= 2:
                        tensor.wait_ge(SC_, C0 + i - 1)
                    tensor.wait_ge(SV, V0 + i * 4 + 4)
                    inc(tensor.matmul(bank[:, 0:TB], onesr[:], arow[0:1, 0:TB],
                                      start=True, stop=True), "ST")
                for er in range(2):
                    tensor.wait_ge(SX, SXd0 - (16 if er == 0 else 0))
                    for tt in range(NTB):
                        gi = er * 4 + tt
                        bank = pA if gi % 2 == 0 else pB
                        if gi >= 2:
                            tensor.wait_ge(SC_, C0 + 39 + gi)
                        tensor.wait_ge(SC_, C0 + 8 + 32)
                        for fc in range(FC):
                            ins = tensor.matmul(
                                bank[:, 0:TB],
                                wa_sb[:, fc * 128:(fc + 1) * 128],
                                x1b[:, fc * NT + tt * TB: fc * NT + (tt + 1) * TB],
                                start=(fc == 0), stop=(fc == FC - 1))
                        inc(ins, "ST")

            @block.scalar
            def _(scalar):
                phase_start(scalar)

                def sop(fn):
                    scalar.wait_ge(SC_, cnt["SC"])
                    inc(fn(), "SC")

                for i in range(8):
                    bank = pC if i % 2 == 0 else pD
                    dst = abc_a if i < 4 else abc_b
                    tt = i % 4
                    scalar.wait_ge(ST_, T0 + 5 * i + 5)
                    sop(lambda dst=dst, tt=tt, bank=bank: scalar.copy(
                        dst[:, tt * TB:(tt + 1) * TB], bank[:, 0:TB]))
                for fc in range(FC):
                    sem, val = x1done(fc)
                    scalar.wait_ge(sem, val)
                    sop(lambda fc=fc: scalar.copy(
                        x1b[:, fc * NT:(fc + 1) * NT], s1buf(fc)[:]))
                for gi in range(8):
                    er, tt = divmod(gi, 4)
                    bank = pA if gi % 2 == 0 else pB
                    scalar.wait_ge(ST_, T0 + 40 + gi + 1)
                    sop(lambda er=er, tt=tt, bank=bank: scalar.activation(
                        xab[:, er * NT + tt * TB: er * NT + (tt + 1) * TB],
                        bank[:, 0:TB], AF.Copy, scale=wvec_s[:, er:er + 1]))

            @block.vector
            def _(vector):
                phase_start(vector)

                def vop(fn):
                    vector.wait_ge(SV, cnt["SV"])
                    inc(fn(), "SV")

                for col in range(32):
                    i, c = divmod(col, 4)
                    vector.wait_ge(ST_, T0 + 5 * i + c + 1)
                    if c == 0 and i >= 1:
                        vector.wait_ge(ST_, T0 + 5 * (i - 1) + 5)
                    vop(lambda i=i, c=c: vector.tensor_copy(
                        arow[0:1, c * 128:(c + 1) * 128], pG[0:1, 0:128]))
                for fc in range(16):
                    vector.wait_ge(SS, ss_s1ld[fc])
                    vector.wait_ge(SC_, C0 + 8)
                    vop(lambda fc=fc: vector.tensor_tensor(
                        s1_ld[fc % 2][:], s1_ld[fc % 2][:], abc_a[:], ALU.mult))
                    vop(lambda fc=fc: vector.tensor_tensor(
                        s1_ld[fc % 2][:], s1_ld[fc % 2][:], abc_b[:], ALU.add))
                vop(lambda: vector.memset(statacc[:], 0.0))

            ed1.close()
            wf_sb = sbd("wf_sb", [128, 4096], BF16)
            wb_sb = sbd("wb_sb", [128, 256], BF16)
            x1w = [sbd(f"x1w{i}", [128, NT], F32) for i in range(2)]
            s2_sb = [sbd(f"s2sb{i}", [128, NT], F32) for i in range(2)]
            s2sq = [sbd(f"s2sq{i}", [128, TB], BF16) for i in range(4)]
            s2bb = [sbd(f"s2bb{i}", [128, TB], BF16) for i in range(4)]
            D1_SC_END = C0 + 48
            D1_SS_X1 = lambda: ss_x1sp[FC - 1]

            @block.sync
            def _(sync):
                sync.wait_ge(SC_, D1_SC_END)
                sync.wait_ge(SV, V0 + 65)
                sync.wait_ge(SG, SGD0 + 32)
                for do in range(FC):
                    if do >= 2:
                        sync.wait_ge(SV, svd2(do - 2, 4))
                    sync.wait_ge(SS, cnt["SS"])
                    inc(sync.dma_start(out=x1w[do % 2][:], in_=x1_sp[do][:]),
                        "SS", 16)
                    ss_x1w[do] = cnt["SS"]
                    sync.wait_ge(SV, svd2(do, 4))
                    sync.wait_ge(SC_, C0 + 48 + do * 8 + 8)
                    sync.wait_ge(SS, cnt["SS"])
                    inc(sync.dma_start(out=s2_sp[do][:], in_=s2_sb[do % 2][:]),
                        "SS", 16)
                    ss_s2sp[do] = cnt["SS"]
                phase_end(sync, 4)

            @block.gpsimd
            def _(gpsimd):
                gpsimd.wait_ge(ST_, T0 + 48)
                gpsimd.wait_ge(SC_, D1_SC_END)
                gpsimd.wait_ge(SS, ss_x1sp[FC - 1])
                for do in range(FC):
                    if do >= 1:
                        gpsimd.wait_ge(ST_, T0 + 48 + do * 5)
                    gpsimd.wait_ge(SX, cnt["SX"])
                    inc(gpsimd.dma_start(out=wf_sb[:], in_=wf_d[do]), "SX", 16)
                    gpsimd.wait_ge(SX, cnt["SX"])
                    inc(gpsimd.dma_start(out=wb_sb[:], in_=wb_d[do]), "SX", 16)
                phase_end(gpsimd, 3)
            SXd1 = cnt["SX"] - FC * 32

            @block.tensor
            def _(tensor):
                for do in range(FC):
                    tensor.wait_ge(SX, SXd1 + (do + 1) * 32)
                    for tt in range(NTB):
                        gi2 = do * 4 + tt
                        bank = [pA, pB, pC, pD][tt]
                        if do >= 1:
                            tensor.wait_ge(SV, svd2(do - 1, tt + 1))
                        else:
                            tensor.wait_ge(SC_, D1_SC_END)
                        for fc in range(FC):
                            tensor.matmul(
                                bank[:, 0:TB],
                                wf_sb[:, fc * 128:(fc + 1) * 128],
                                x1b[:, fc * NT + tt * TB: fc * NT + (tt + 1) * TB],
                                start=(fc == 0), stop=False)
                        tensor.matmul(bank[:, 0:TB], wb_sb[:, 0:128],
                                      xab[:, 0 * NT + tt * TB: 0 * NT + (tt + 1) * TB],
                                      start=False, stop=False)
                        ins = tensor.matmul(bank[:, 0:TB], wb_sb[:, 128:256],
                                            xab[:, 1 * NT + tt * TB: 1 * NT + (tt + 1) * TB],
                                            start=False, stop=True)
                        inc(ins, "ST")
                    if do >= 1:
                        tensor.wait_ge(SV, svd2(do - 1, 5))
                    first = True
                    for tt in range(NTB):
                        tensor.wait_ge(SV, svd2(do, tt + 1))
                        tensor.wait_ge(SC_, C0 + 48 + do * 8 + 2 * tt + 2)
                        for sub in range(4):
                            col = tt * 8 + sub * 2
                            tensor.matmul(
                                pG[:, col:col + 1],
                                s2bb[tt][:, sub * 128:(sub + 1) * 128],
                                onesb[:], start=first, stop=False)
                            first = False
                            last = (tt == NTB - 1 and sub == 3)
                            ins = tensor.matmul(
                                pG[:, col + 1:col + 2],
                                s2sq[tt][:, sub * 128:(sub + 1) * 128],
                                onesb[:], start=False, stop=last)
                    inc(ins, "ST")
                phase_end(tensor, 0)

            @block.scalar
            def _(scalar):

                def sop(fn):
                    scalar.wait_ge(SC_, cnt["SC"])
                    inc(fn(), "SC")

                for do in range(FC):
                    if do >= 1:
                        scalar.wait_ge(ST_, T0 + 48 + (do - 1) * 5 + 5)
                    for tt in range(NTB):
                        scalar.wait_ge(SV, svd2(do, tt + 1))
                        sop(lambda do=do, tt=tt: scalar.activation(
                            s2sq[tt][:], s2_sb[do % 2][:, tt * TB:(tt + 1) * TB],
                            AF.Square))
                        sop(lambda do=do, tt=tt: scalar.copy(
                            s2bb[tt][:], s2_sb[do % 2][:, tt * TB:(tt + 1) * TB]))
                scalar.wait_ge(SV, svd2(FC - 1, 5) + 6)
                sop(lambda: scalar.activation(ab2[:, 0:16], v16d[:], AF.Sqrt))
                phase_end(scalar, 2)
            D_SQRT = cnt["SC"]

            @block.vector
            def _(vector):

                def vop(fn):
                    vector.wait_ge(SV, cnt["SV"])
                    inc(fn(), "SV")

                for do in range(FC):
                    if do >= 2:
                        vector.wait_ge(SC_, C0 + 48 + (do - 2) * 8 + 8)
                        vector.wait_ge(SS, ss_s2sp[do - 2])
                    for tt in range(NTB):
                        vector.wait_ge(ST_, T0 + 48 + do * 5 + tt + 1)
                        vector.wait_ge(SS, ss_x1w[do])
                        bank = [pA, pB, pC, pD][tt]
                        vop(lambda do=do, tt=tt, bank=bank: vector.scalar_tensor_tensor(
                            s2_sb[do % 2][:, tt * TB:(tt + 1) * TB],
                            bank[:, 0:TB], bias_s[:, do:do + 1],
                            x1w[do % 2][:, tt * TB:(tt + 1) * TB],
                            ALU.add, ALU.add))
                    vector.wait_ge(ST_, T0 + 48 + do * 5 + 5)
                    vop(lambda: vector.tensor_tensor(statacc[:], statacc[:],
                                                     pG[:, 0:32], ALU.add))
                sevn = statacc[:].rearrange("p (g s) -> p g s", s=2)
                vop(lambda: vector.tensor_scalar(m16d[:], sevn[:, :, 0:1],
                                                 1.0 / 4096.0, None, ALU.mult))
                vop(lambda: vector.tensor_scalar(e16d[:], sevn[:, :, 1:2],
                                                 1.0 / 4096.0, None, ALU.mult))
                vop(lambda: vector.tensor_tensor(t16d[:], m16d[:], m16d[:],
                                                 ALU.mult))
                vop(lambda: vector.tensor_tensor(v16d[:], e16d[:], t16d[:],
                                                 ALU.subtract))
                vop(lambda: vector.tensor_scalar(v16d[:], v16d[:], 1e-5, None,
                                                 ALU.add))
                vop(lambda: vector.reciprocal(v16d[:], v16d[:]))
                vector.wait_ge(SC_, D_SQRT)
                vop(lambda: vector.tensor_tensor(t16d[:], m16d[:], ab2[:, 0:16],
                                                 ALU.mult))
                vop(lambda: vector.tensor_scalar(ab2[:, 16:32], t16d[:], -1.0,
                                                 None, ALU.mult))
                phase_end(vector, 1)

        phase_no[0] += 1
        if upto == "D":
            _tail_zeros()
            return nc

        with contextlib.ExitStack() as ee:
            sbe = lambda n, s, d: ee.enter_context(nc.sbuf_tensor(n, s, d))
            ab2c_a = sbe("ab2c_a", [128, NT], F32)
            ab2c_b = sbe("ab2c_b", [128, NT], F32)
            arow2 = sbe("arow2", [1, TB], F32)
            s2_ld = [sbe(f"s2ld{i}", [128, NT], F32) for i in range(2)]
            out_sb = [sbe(f"outsb{i}", [128, NT], F32) for i in range(2)]
            s2g = [sbe(f"s2g{i}", [128, NT], F32) for i in range(2)]
            outg = [sbe(f"outg{i}", [128, NT], F32) for i in range(2)]

            T0, C0, V0, S0 = (cnt[k] for k in ("ST", "SC", "SV", "SS"))
            SGE0 = cnt["SG"]
            ss_s2ld = {}
            ss_out = {}

            def e_done(fc):
                if fc < 16:
                    return SV, V0 + 32 + 2 * fc + 2
                return SG, SGE0 + 2 * (fc - 16) + 2

            def s2buf(fc):
                return s2_ld[fc % 2] if fc < 16 else s2g[fc % 2]

            def obuf(fc):
                return out_sb[fc % 2] if fc < 16 else outg[fc % 2]

            @block.sync
            def _(sync):
                phase_start(sync)
                sync.wait_ge(SS, S0)
                for fc in range(FC):
                    if fc >= 2 and (fc < 16 or fc >= 18):
                        sem, val = e_done(fc - 2)
                        sync.wait_ge(sem, val)
                    sync.wait_ge(SS, cnt["SS"])
                    inc(sync.dma_start(out=s2buf(fc)[:], in_=s2_sp[fc][:]),
                        "SS", 16)
                    ss_s2ld[fc] = cnt["SS"]
                    sem, val = e_done(fc)
                    sync.wait_ge(sem, val)
                    sync.wait_ge(SS, cnt["SS"])
                    inc(sync.dma_start(out=outT[fc * 128:(fc + 1) * 128, :],
                                       in_=obuf(fc)[:]), "SS", 16)
                    ss_out[fc] = cnt["SS"]
                sync.wait_ge(SS, cnt["SS"])
                sync.wait_ge(SG, SGE0 + 32)
                phase_end(sync, 4)

            @block.gpsimd
            def _(gpsimd):
                phase_start(gpsimd)
                gpsimd.wait_ge(SC_, C0 + 8)
                for fc in range(16, FC):
                    gpsimd.wait_ge(SS, ss_s2ld[fc])
                    if fc >= 18:
                        gpsimd.wait_ge(SS, ss_out[fc - 2])
                    gpsimd.wait_ge(SG, cnt["SG"])
                    inc(gpsimd.tensor_tensor(outg[fc % 2][:], s2g[fc % 2][:],
                                             ab2c_a[:], ALU.mult), "SG")
                    gpsimd.wait_ge(SG, cnt["SG"])
                    inc(gpsimd.tensor_tensor(outg[fc % 2][:], outg[fc % 2][:],
                                             ab2c_b[:], ALU.add), "SG")
                phase_end(gpsimd, 3)

            @block.tensor
            def _(tensor):
                phase_start(tensor)
                for i in range(8):
                    bank = pC if i % 2 == 0 else pD
                    for c in range(4):
                        col = i * 4 + c
                        if col >= 1:
                            tensor.wait_ge(SV, V0 + col)
                        inc(tensor.matmul(pG[0:1, 0:128], ab2[:, col:col + 1],
                                          idfs[:], start=True, stop=True), "ST")
                    if i >= 2:
                        tensor.wait_ge(SC_, C0 + i - 1)
                    tensor.wait_ge(SV, V0 + i * 4 + 4)
                    inc(tensor.matmul(bank[:, 0:TB], onesr[:], arow2[0:1, 0:TB],
                                      start=True, stop=True), "ST")
                phase_end(tensor, 0)

            @block.scalar
            def _(scalar):
                phase_start(scalar)

                def sop(fn):
                    scalar.wait_ge(SC_, cnt["SC"])
                    inc(fn(), "SC")

                for i in range(8):
                    bank = pC if i % 2 == 0 else pD
                    dst = ab2c_a if i < 4 else ab2c_b
                    tt = i % 4
                    scalar.wait_ge(ST_, T0 + 5 * i + 5)
                    sop(lambda dst=dst, tt=tt, bank=bank: scalar.copy(
                        dst[:, tt * TB:(tt + 1) * TB], bank[:, 0:TB]))
                phase_end(scalar, 2)

            @block.vector
            def _(vector):
                phase_start(vector)

                def vop(fn):
                    vector.wait_ge(SV, cnt["SV"])
                    inc(fn(), "SV")

                for col in range(32):
                    i, c = divmod(col, 4)
                    vector.wait_ge(ST_, T0 + 5 * i + c + 1)
                    if c == 0 and i >= 1:
                        vector.wait_ge(ST_, T0 + 5 * (i - 1) + 5)
                    vop(lambda i=i, c=c: vector.tensor_copy(
                        arow2[0:1, c * 128:(c + 1) * 128], pG[0:1, 0:128]))
                for fc in range(16):
                    vector.wait_ge(SS, ss_s2ld[fc])
                    if fc >= 2:
                        vector.wait_ge(SS, ss_out[fc - 2])
                    vector.wait_ge(SC_, C0 + 8)
                    vop(lambda fc=fc: vector.tensor_tensor(
                        out_sb[fc % 2][:], s2_ld[fc % 2][:], ab2c_a[:], ALU.mult))
                    vop(lambda fc=fc: vector.tensor_tensor(
                        out_sb[fc % 2][:], out_sb[fc % 2][:], ab2c_b[:], ALU.add))
                phase_end(vector, 1)

    return nc



def _host_layout(inputs, core):
    b, r = divmod(core, 4)
    x = inputs["L0_attn_hidden_states"]
    tok0 = r * NT
    xb = x[b]
    xs = np.zeros((NTH, 4096), np.float32)
    lo = tok0 - HALO
    src_lo = max(lo, 0)
    xs[src_lo - lo:NTH] = xb[src_lo:tok0 + NT]
    xTl = np.ascontiguousarray(xs.T)

    pos = np.arange(lo, tok0 + NT, dtype=np.float32)
    half = np.arange(64, dtype=np.float32)
    inv = (1.0 / (10000.0 ** (2.0 * half / 128.0))).astype(np.float32)
    fr = pos[None, :] * inv[:, None]
    emb = np.concatenate([fr, fr], axis=0)
    cosl = np.ascontiguousarray(np.cos(emb).astype(np.float32))
    sinl = np.ascontiguousarray(np.sin(emb).astype(np.float32))

    s = np.arange(128)[:, None]
    t = np.arange(128)[None, :]
    mg = (s <= t).astype(np.float32)
    mA = (s - t >= 64).astype(np.float32)
    mB = ((t - s >= 0) & (t - s <= 64)).astype(np.float32)
    ident = np.eye(128, dtype=np.float32)
    Rt = np.zeros((128, 128), np.float32)
    for d in range(64):
        Rt[d + 64, d] = -1.0
        Rt[d, d + 64] = 1.0
    ohv = np.zeros((128, 8), np.float32)
    ohv[:, b * 4 + r] = 1.0

    def stat_blocks(W, ocs):
        return np.ascontiguousarray(
            W.reshape(32, 128, ocs, 128).transpose(2, 1, 0, 3)
            .reshape(ocs, 128, 4096))

    Acat = inputs["L0_ffn_A"].reshape(256, 4096)
    Bcat = np.ascontiguousarray(
        np.transpose(inputs["L0_ffn_Bf"], (0, 2, 1))).reshape(256, 4096)
    return {
        "xT": xTl, "cosT": cosl, "sinT": sinl,
        "mg": mg, "mA": mA, "mAf": (mA if r > 0 else np.zeros_like(mA)),
        "mB": mB, "idb": ident.astype(BF), "idf": ident, "Rt": Rt, "ohv": ohv,
        "wq": stat_blocks(inputs["L0_attn_Wq"], 32),
        "wk": stat_blocks(inputs["L0_attn_Wk"], 8),
        "wv": stat_blocks(inputs["L0_attn_Wv"], 8),
        "wo": stat_blocks(inputs["L0_attn_Wo"], 32),
        "wf": stat_blocks(np.ascontiguousarray(inputs["L0_ffn_W"].T), 32),
        "wa": stat_blocks(np.ascontiguousarray(Acat.T), 2),
        "wb": np.ascontiguousarray(
            Bcat.reshape(2, 128, 32, 128).transpose(2, 1, 0, 3)
            .reshape(32, 128, 256)),
        "bias_pp": np.ascontiguousarray(
            inputs["L0_ffn_bias"].reshape(32, 128).T),
        "cvec": np.ascontiguousarray(
            np.repeat(inputs["L0_ffn_coeffs"], 16).reshape(2, 128).T),
        "svec": np.ascontiguousarray(
            np.repeat(inputs["L0_ffn_scales"], 16).reshape(2, 128).T),
    }


def kernel(**inputs):
    nc = build_nc()
    in_maps = [_host_layout(inputs, c) for c in range(NCORE)]
    res = run_bass_kernel_spmd(nc, in_maps, core_ids=list(range(NCORE)))
    x = inputs["L0_attn_hidden_states"]
    out = np.zeros_like(x)
    for c in range(NCORE):
        b, r = divmod(c, 4)
        out[b, r * NT:(r + 1) * NT] = res.results[c]["outT"].T
    return out


# version 06 (final)
# speedup vs baseline: 1.0600x; 1.0600x over previous
import contextlib

import numpy as np
import ml_dtypes

import concourse.bass as bass
import concourse.mybir as mybir
from concourse.bass_utils import run_bass_kernel_spmd

F32 = mybir.dt.float32
BF16 = mybir.dt.bfloat16
AF = mybir.ActivationFunctionType
ALU = mybir.AluOpType
BF = ml_dtypes.bfloat16

NCORE = 8
NT = 2048
HALO = 256
NTH = NT + HALO
C = 128
NCHH = NTH // C
KVH = 8
NH = 32
FC = 32
TA = 256
NTILES_A = NTH // TA
TB = 512
NTB = NT // TB
SCL = 1.0 / float(np.sqrt(np.float32(128)))
G16 = 0.0625


def build_nc(debug=(), upto="E", with_cc=True, a_tiles=None, a_skip=()):
    nc = bass.Bass("TRN2", target_bir_lowering=False)

    def dram(name, shape, dt):
        k = "ExternalOutput" if name in debug else "Internal"
        return nc.dram_tensor(name, shape, dt, kind=k).ap()

    ei = lambda n, s, d: nc.dram_tensor(n, s, d, kind="ExternalInput").ap()
    xT = ei("xT", [4096, NTH], F32)
    cosT = ei("cosT", [128, NTH], F32)
    sinT = ei("sinT", [128, NTH], F32)
    mg_d = ei("mg", [128, 128], F32)
    mA_d = ei("mA", [128, 128], F32)
    mAf_d = ei("mAf", [128, 128], F32)
    mB_d = ei("mB", [128, 128], F32)
    idb_d = ei("idb", [128, 128], BF16)
    idf_d = ei("idf", [128, 128], F32)
    Rt_d = ei("Rt", [128, 128], F32)
    ohv_d = ei("ohv", [128, 8], F32)
    wq_d = ei("wq", [32, 128, 4096], F32)
    wk_d = ei("wk", [8, 128, 4096], F32)
    wv_d = ei("wv", [8, 128, 4096], F32)
    wo_d = ei("wo", [32, 128, 4096], F32)
    wf_d = ei("wf", [32, 128, 4096], F32)
    wa_d = ei("wa", [2, 128, 4096], F32)
    wb_d = ei("wb", [32, 128, 256], F32)
    bias_d = ei("bias_pp", [128, 32], F32)
    cvec_d = ei("cvec", [128, 2], F32)
    svec_d = ei("svec", [128, 2], F32)

    outT = nc.dram_tensor("outT", [4096, NT], F32, kind="ExternalOutput").ap()

    sk_sp = dram("sk_sp", [KVH, 128, NTH], BF16)
    v_sp = dram("v_sp", [NTH, 1024], BF16)
    kt_sp = dram("kt_sp", [KVH, 128, NTH], BF16)
    ecs_sp = dram("ecs_sp", [KVH, 128, NTH], BF16)
    sc_sp = dram("sc_sp", [KVH, 128, NT], BF16)
    qt_sp = dram("qt_sp", [NH, 128, NT], BF16)
    sq_sp = dram("sq_sp", [NH, 128, NT], BF16)
    rq_sp = dram("rq_sp", [NH, 128, 16], F32)
    ot_sp = dram("ot_sp", [NH, 128, NT], BF16)
    s1_sp = dram("s1_sp", [FC, 128, NT], F32)
    x1_sp = dram("x1_sp", [FC, 128, NT], F32)
    s2_sp = dram("s2_sp", [FC, 128, NT], F32)
    ursum = dram("ursum", [128, KVH * 129], F32)
    cc_in = nc.dram_tensor("cc_in", [128, KVH * 129], F32, kind="Internal").ap()
    cc_out = nc.dram_tensor("cc_out", [NCORE * 128, KVH * 129], F32,
                            kind="Internal", addr_space="Shared").ap()

    es = contextlib.ExitStack()
    with es:
        sb = lambda n, s, d: es.enter_context(nc.sbuf_tensor(n, s, d))
        mgs = sb("mgs", [128, 128], F32)
        mAs = sb("mAs", [128, 128], F32)
        mAfs = sb("mAfs", [128, 128], F32)
        mBs = sb("mBs", [128, 128], F32)
        idbs = sb("idbs", [128, 128], BF16)
        idfs = sb("idfs", [128, 128], F32)
        Rts = sb("Rts", [128, 128], F32)
        ohs = sb("ohs", [128, 8], F32)
        ones = sb("ones", [128, 1], F32)
        onesb = sb("onesb", [128, 1], BF16)
        twosb = sb("twosb", [128, 1], BF16)
        twos = sb("twos", [128, 1], F32)
        onesr = sb("onesr", [1, 128], F32)
        zer = sb("zer", [128, TA], F32)
        rk_all = sb("rk_all", [128, KVH * NCHH], F32)
        Ec_all = sb("Ec_all", [128, KVH * NCHH], F32)
        E_run = sb("E_run", [128, KVH], F32)
        statacc = sb("statacc", [128, 32], F32)
        ab1 = sb("ab1", [128, 32], F32)
        ab2 = sb("ab2", [128, 32], F32)
        bias_s = sb("bias_s", [128, 32], F32)
        wvec_s = sb("wvec_s", [128, 2], F32)
        cvec_s = sb("cvec_s", [128, 2], F32)
        svec_s = sb("svec_s", [128, 2], F32)
        eab = contextlib.ExitStack()
        cosw = eab.enter_context(nc.sbuf_tensor("cosw", [128, NTH], F32))
        sinw = eab.enter_context(nc.sbuf_tensor("sinw", [128, NTH], F32))

        pA = es.enter_context(nc.psum_tensor([128, 512], F32))
        pB = es.enter_context(nc.psum_tensor([128, 512], F32))
        pC = es.enter_context(nc.psum_tensor([128, 512], F32))
        pD = es.enter_context(nc.psum_tensor([128, 512], F32))
        pE = es.enter_context(nc.psum_tensor([128, 512], F32))
        pF = es.enter_context(nc.psum_tensor([128, 512], BF16))
        pG = es.enter_context(nc.psum_tensor([128, 129], F32))
        pT = es.enter_context(nc.psum_tensor([128, 128], BF16))

        SX = es.enter_context(nc.semaphore("SX"))
        SW = es.enter_context(nc.semaphore("SW"))
        SS = es.enter_context(nc.semaphore("SS"))
        ST_ = es.enter_context(nc.semaphore("STp"))
        SV = es.enter_context(nc.semaphore("SVp"))
        SC_ = es.enter_context(nc.semaphore("SCp"))
        SG = es.enter_context(nc.semaphore("SGp"))
        PH = [es.enter_context(nc.semaphore(f"PH{i}")) for i in range(5)]

        block = es.enter_context(nc.Block())

        cnt = {"SX": 0, "SW": 0, "SS": 0, "ST": 0, "SV": 0, "SC": 0, "SG": 0}
        sem_of = {"SX": SX, "SW": SW, "SS": SS, "ST": ST_, "SV": SV,
                  "SC": SC_, "SG": SG}

        def inc(ins, name, n=1):
            ins.then_inc(sem_of[name], n)
            cnt[name] += n
            return cnt[name]

        phase_no = [1]

        def phase_end(eng, idx):
            eng.nop().then_inc(PH[idx], 1)

        def phase_start(eng):
            for i in range(5):
                eng.wait_ge(PH[i], phase_no[0] - 1)

        @block.sync
        def _(sync):
            for dst, src in [(cosw, cosT), (sinw, sinT), (mgs, mg_d),
                             (mAs, mA_d), (mAfs, mAf_d), (mBs, mB_d),
                             (idbs, idb_d), (idfs, idf_d), (Rts, Rt_d),
                             (ohs, ohv_d), (bias_s, bias_d),
                             (cvec_s, cvec_d), (svec_s, svec_d)]:
                inc(sync.dma_start(out=dst[:], in_=src[:]), "SW", 16)
        const_loads = cnt["SW"]

        @block.vector
        def _(vector):
            vector.wait_ge(SW, const_loads)
            inc(vector.memset(ones[:], 1.0), "SV")
            inc(vector.memset(onesb[:], 1.0), "SV")
            inc(vector.memset(twosb[:], 2.0), "SV")
            inc(vector.memset(twos[:], 2.0), "SV")
            inc(vector.memset(onesr[:], 1.0), "SV")
            inc(vector.memset(zer[:], 0.0), "SV")
            inc(vector.memset(E_run[:], 1.0), "SV")
            inc(vector.memset(statacc[:], 0.0), "SV")
            inc(vector.tensor_tensor(wvec_s[:], cvec_s[:], svec_s[:], ALU.mult),
                "SV")
        v_const = cnt["SV"]

        with contextlib.ExitStack() as ea:
            sba = lambda n, s, d: ea.enter_context(nc.sbuf_tensor(n, s, d))
            xa_b = [sba(f"xa{i}", [128, FC * TA], BF16) for i in range(2)]
            wk_sb = sba("wk_sb", [128, 4 * 4096], BF16)
            wv_sb = sba("wv_sb", [128, 4 * 4096], BF16)
            kT_f = [sba(f"kT{i}", [128, TA], F32) for i in range(2)]
            e1_f = sba("e1_f", [128, TA], F32)
            L_f = [sba(f"L{i}", [128, TA], F32) for i in range(2)]
            P_f = [sba(f"P{i}", [128, TA], F32) for i in range(2)]
            kp_f = [sba(f"kp{i}", [128, TA], F32) for i in range(2)]
            ek_f = [sba(f"ek{i}", [128, TA], F32) for i in range(2)]
            t1_f = sba("t1_f", [128, TA], F32)
            m_f = sba("m_f", [128, TA], F32)
            dc_f = [sba(f"dc{i}", [128, 2], F32) for i in range(2)]
            ktil_b = [sba(f"ktil{i}", [128, TA], BF16) for i in range(2)]
            ecs_b = [sba(f"ecsb{i}", [128, TA], BF16) for i in range(2)]
            sk_b = [sba(f"skb{i}", [128, TA], BF16) for i in range(2)]
            vt_b = [sba(f"vtb{i}", [128, TA], BF16) for i in range(2)]
            vtm_b = [sba(f"vtm{i}", [128, 256], BF16) for i in range(2)]
            ktm_b = [sba(f"ktm{i}", [128, 256], BF16) for i in range(2)]
            sct_b = [sba(f"sct{i}", [128, 256], BF16) for i in range(2)]
            tmpS = sba("tmpS", [128, 128], F32)
            S_run = sba("S_run", [128, KVH * 128], F32)
            cc_st = sba("cc_st", [128, KVH * 129], F32)

            @block.vector
            def _(vector):
                vector.wait_ge(SV, cnt["SV"])
                inc(vector.memset(S_run[:], 0.0), "SV")

            HNIT = NTILES_A * 4
            NIT = 2 * HNIT
            if a_tiles is not None:
                NIT = a_tiles * 4
            ST0, SC0, SV0, SS0 = (cnt[k] for k in ("ST", "SC", "SV", "SS"))
            TPI, CPI, VPI = 10, 13, 19

            def split_it(it):
                half, rem = divmod(it, HNIT)
                ti, jj = divmod(rem, 4)
                return half, ti, jj, half * 4 + jj

            def stv(it, k):
                return ST0 + it * TPI + k if it >= 0 else ST0

            def scv(it, k):
                return SC0 + it * CPI + k if it >= 0 else SC0

            def svv(it, k):
                return SV0 + it * VPI + k if it >= 0 else SV0

            _ss_cum = []
            _tot = 0
            for _it in range(NIT):
                _, _ti, _, _ = split_it(_it)
                _tot += 5 if _ti == 0 else 6
                _ss_cum.append(_tot)

            def ssv(it):
                return SS0 + 16 * _ss_cum[it] if it >= 0 else SS0

            def sx_tile(half, ti):
                return (half * 17 + 8 + ti + 1) * 16

            @block.gpsimd
            def _(gpsimd):
                phase_start(gpsimd)
                for half in range(2):
                    if half * HNIT >= NIT:
                        break
                    if half == 1:
                        gpsimd.wait_ge(ST_, stv(HNIT - 1, TPI))
                    for jj in range(4):
                        j = half * 4 + jj
                        if cnt["SX"] > 0:
                            gpsimd.wait_ge(SX, cnt["SX"])
                        inc(gpsimd.dma_start(out=wk_sb[:, jj * 4096:(jj + 1) * 4096],
                                             in_=wk_d[j]), "SX", 16)
                        gpsimd.wait_ge(SX, cnt["SX"])
                        inc(gpsimd.dma_start(out=wv_sb[:, jj * 4096:(jj + 1) * 4096],
                                             in_=wv_d[j]), "SX", 16)
                    for ti in range(NTILES_A):
                        gti = half * NTILES_A + ti
                        if half * HNIT + ti * 4 >= NIT:
                            break
                        if gti >= 2:
                            lt = gti - 2
                            lh, lti = divmod(lt, NTILES_A)
                            gpsimd.wait_ge(ST_, stv(lh * HNIT + lti * 4 + 3, 2))
                        gpsimd.wait_ge(SX, cnt["SX"])
                        inc(gpsimd.dma_start(
                            out=xa_b[gti % 2][:].rearrange("p (fc t) -> p fc t", fc=FC),
                            in_=xT[:, ti * TA:(ti + 1) * TA]
                            .rearrange("(fc p) t -> p fc t", p=128)), "SX", 16)

            @block.tensor
            def _(tensor):
                phase_start(tensor)
                tensor.wait_ge(SV, v_const)
                for it in range(NIT):
                    half, ti, jj, j = split_it(it)
                    par = it % 2
                    gti = half * NTILES_A + ti
                    if it >= 2:
                        tensor.wait_ge(SC_, scv(it - 2, CPI))
                        tensor.wait_ge(SV, svv(it - 2, VPI))
                    if jj == 0:
                        tensor.wait_ge(SX, sx_tile(half, ti))
                    xa = xa_b[gti % 2]
                    if it >= 1:
                        tensor.wait_ge(SC_, scv(it - 1, 1))
                    for fc in range(FC):
                        ins = tensor.matmul(pA[:, 0:TA],
                                            wk_sb[:, jj * 4096 + fc * 128: jj * 4096 + fc * 128 + 128],
                                            xa[:, fc * TA:(fc + 1) * TA],
                                            start=(fc == 0), stop=(fc == FC - 1))
                    inc(ins, "ST")
                    if it >= 1:
                        tensor.wait_ge(SC_, scv(it - 1, 2))
                    for fc in range(FC):
                        ins = tensor.matmul(pB[:, 0:TA],
                                            wv_sb[:, jj * 4096 + fc * 128: jj * 4096 + fc * 128 + 128],
                                            xa[:, fc * TA:(fc + 1) * TA],
                                            start=(fc == 0), stop=(fc == FC - 1))
                    inc(ins, "ST")
                    tensor.wait_ge(SC_, scv(it, 1))
                    if it >= 1:
                        tensor.wait_ge(SV, svv(it - 1, 6))
                    inc(tensor.matmul(pC[:, 0:TA], Rts[:], kT_f[par][:],
                                      start=True, stop=True), "ST")
                    tensor.wait_ge(SC_, scv(it, 5))
                    if it >= 1:
                        tensor.wait_ge(SV, svv(it - 1, 4))
                    tensor.matmul(pG[:, 0:1], ek_f[par][:, 0:128], ones[:],
                                  start=True, stop=False)
                    inc(tensor.matmul(pG[:, 1:2], ek_f[par][:, 128:256], ones[:],
                                      start=False, stop=True), "ST")
                    tensor.wait_ge(SC_, scv(it, 2))
                    if it >= 1:
                        tensor.wait_ge(SC_, scv(it - 1, 13))
                    inc(tensor.matmul(pT[:], vt_b[par][:, 0:128], idbs[:],
                                      is_transpose=True, start=True, stop=True),
                        "ST")
                    tensor.wait_ge(SC_, scv(it, 7))
                    inc(tensor.matmul(pT[:], vt_b[par][:, 128:256], idbs[:],
                                      is_transpose=True, start=True, stop=True),
                        "ST")
                    tensor.wait_ge(SC_, scv(it, 8))
                    tensor.wait_ge(SC_, scv(it, 9))
                    inc(tensor.matmul(pT[:], ktil_b[par][:, 0:128], idbs[:],
                                      is_transpose=True, start=True, stop=True),
                        "ST")
                    tensor.wait_ge(SC_, scv(it, 12))
                    inc(tensor.matmul(pT[:], ktil_b[par][:, 128:256], idbs[:],
                                      is_transpose=True, start=True, stop=True),
                        "ST")
                    tensor.wait_ge(SC_, scv(it, 13))
                    if it >= 1:
                        tensor.wait_ge(SV, svv(it - 1, 11))
                    inc(tensor.matmul(pD[:, 0:128], ktm_b[par][:, 0:128],
                                      vtm_b[par][:, 0:128], start=True, stop=True),
                        "ST")
                    if it >= 1:
                        tensor.wait_ge(SV, svv(it - 1, 16))
                    inc(tensor.matmul(pE[:, 0:128], ktm_b[par][:, 128:256],
                                      vtm_b[par][:, 128:256], start=True, stop=True),
                        "ST")
                phase_end(tensor, 0)

            @block.scalar
            def _(scalar):
                phase_start(scalar)

                def sop(fn, name="SC"):
                    scalar.wait_ge(SC_, cnt["SC"])
                    inc(fn(), name)

                for it in range(NIT):
                    half, ti, jj, j = split_it(it)
                    par = it % 2
                    cA = j * NCHH + 2 * ti
                    if it >= 2:
                        scalar.wait_ge(ST_, stv(it - 2, TPI))
                        scalar.wait_ge(SV, svv(it - 2, VPI))
                        scalar.wait_ge(SS, ssv(it - 2))
                    scalar.wait_ge(ST_, stv(it, 1))
                    sop(lambda: scalar.copy(kT_f[par][:], pA[:, 0:TA]))
                    scalar.wait_ge(ST_, stv(it, 2))
                    sop(lambda: scalar.copy(vt_b[par][:], pB[:, 0:TA]))
                    sop(lambda: scalar.activation(e1_f[:], kT_f[par][:], AF.Exp,
                                                  scale=-1.0))
                    sop(lambda: scalar.activation(L_f[par][:], e1_f[:], AF.Ln,
                                                  bias=1.0, scale=1.0))
                    sop(lambda: scalar.activation(ek_f[par][:], kT_f[par][:],
                                                  AF.Exp))
                    scalar.wait_ge(SV, svv(it, 2))
                    sop(lambda: scalar.activation(dc_f[par][:, 0:1],
                                                  P_f[par][:, 127:128],
                                                  AF.Exp, scale=-G16))
                    scalar.wait_ge(ST_, stv(it, 5))
                    sop(lambda: scalar.copy(vtm_b[par][:, 0:128], pT[:]))
                    scalar.wait_ge(ST_, stv(it, 6))
                    sop(lambda: scalar.copy(vtm_b[par][:, 128:256], pT[:]))
                    scalar.wait_ge(SV, svv(it, 3))
                    sop(lambda: scalar.activation(ktil_b[par][:], kp_f[par][:],
                                                  AF.Exp))
                    sop(lambda: scalar.activation(ecs_b[par][:], P_f[par][:],
                                                  AF.Exp, scale=-G16))
                    sop(lambda: scalar.activation(dc_f[par][:, 1:2],
                                                  P_f[par][:, 255:256],
                                                  AF.Exp, scale=-G16))
                    scalar.wait_ge(ST_, stv(it, 7))
                    scalar.wait_ge(SV, svv(it, 4))
                    sop(lambda: scalar.activation(ktm_b[par][:, 0:128], pT[:],
                                                  AF.Copy,
                                                  scale=rk_all[:, cA:cA + 1]))
                    scalar.wait_ge(ST_, stv(it, 8))
                    sop(lambda: scalar.activation(ktm_b[par][:, 128:256], pT[:],
                                                  AF.Copy,
                                                  scale=rk_all[:, cA + 1:cA + 2]))
                phase_end(scalar, 2)

            @block.vector
            def _(vector):
                phase_start(vector)

                def vop(fn):
                    vector.wait_ge(SV, cnt["SV"])
                    inc(fn(), "SV")

                for it in range(NIT):
                    half, ti, jj, j = split_it(it)
                    par = it % 2
                    cA = j * NCHH + 2 * ti
                    jslice = S_run[:, j * 128:(j + 1) * 128]
                    if it >= 2:
                        vector.wait_ge(SC_, scv(it - 2, CPI))
                        vector.wait_ge(ST_, stv(it - 2, TPI))
                        vector.wait_ge(SS, ssv(it - 2))
                    vector.wait_ge(SC_, scv(it, 4))
                    vop(lambda: vector.tensor_tensor_scan(
                        P_f[par][:, 0:128], L_f[par][:, 0:128],
                        zer[:, 0:128], 0.0, ALU.add, ALU.add))
                    vop(lambda: vector.tensor_tensor_scan(
                        P_f[par][:, 128:256], L_f[par][:, 128:256],
                        zer[:, 0:128], 0.0, ALU.add, ALU.add))
                    vop(lambda: vector.scalar_tensor_tensor(
                        kp_f[par][:], P_f[par][:], G16, kT_f[par][:],
                        ALU.mult, ALU.add))
                    vector.wait_ge(ST_, stv(it, 4))
                    vop(lambda: vector.reciprocal(rk_all[:, cA:cA + 2],
                                                  pG[:, 0:2]))
                    vop(lambda: vector.tensor_tensor(
                        t1_f[:], kT_f[par][:], cosw[:, ti * TA:(ti + 1) * TA],
                        ALU.mult))
                    vector.wait_ge(ST_, stv(it, 3))
                    vop(lambda: vector.tensor_tensor(
                        m_f[:], pC[:, 0:TA], sinw[:, ti * TA:(ti + 1) * TA],
                        ALU.mult))
                    vop(lambda: vector.tensor_tensor(sk_b[par][:], t1_f[:],
                                                     m_f[:], ALU.add))
                    vop(lambda: vector.tensor_copy(Ec_all[:, cA:cA + 1],
                                                   E_run[:, j:j + 1]))
                    vector.wait_ge(SC_, scv(it, 6))
                    vop(lambda: vector.tensor_scalar(
                        E_run[:, j:j + 1], E_run[:, j:j + 1], dc_f[par][:, 0:1],
                        None, ALU.mult))
                    vop(lambda: vector.tensor_copy(sct_b[par][:, 0:128], jslice))
                    vector.wait_ge(ST_, stv(it, 9))
                    vop(lambda: vector.tensor_tensor(tmpS[:], jslice,
                                                     pD[:, 0:128], ALU.add))
                    vop(lambda: vector.tensor_scalar(jslice, tmpS[:],
                                                     dc_f[par][:, 0:1], None,
                                                     ALU.mult))
                    vop(lambda: vector.tensor_copy(Ec_all[:, cA + 1:cA + 2],
                                                   E_run[:, j:j + 1]))
                    vector.wait_ge(SC_, scv(it, 11))
                    vop(lambda: vector.tensor_scalar(
                        E_run[:, j:j + 1], E_run[:, j:j + 1], dc_f[par][:, 1:2],
                        None, ALU.mult))
                    vop(lambda: vector.tensor_copy(sct_b[par][:, 128:256],
                                                   jslice))
                    vector.wait_ge(ST_, stv(it, 10))
                    vop(lambda: vector.tensor_tensor(tmpS[:], jslice,
                                                     pE[:, 0:128], ALU.add))
                    vop(lambda: vector.tensor_scalar(jslice, tmpS[:],
                                                     dc_f[par][:, 1:2], None,
                                                     ALU.mult))
                    if it % HNIT == 3:
                        h0 = (it // HNIT) * 4
                        vop(lambda: vector.memset(
                            S_run[:, h0 * 128:(h0 + 4) * 128], 0.0))
                        vop(lambda: vector.memset(E_run[:, h0:h0 + 4], 1.0))
                    else:
                        inc(vector.nop(), "SV")
                        inc(vector.nop(), "SV")
                for j in range(KVH):
                    vector.wait_ge(SV, cnt["SV"])
                    inc(vector.tensor_copy(cc_st[:, j * 129:j * 129 + 128],
                                           S_run[:, j * 128:(j + 1) * 128]), "SV")
                    vector.wait_ge(SV, cnt["SV"])
                    inc(vector.tensor_copy(
                        cc_st[:, j * 129 + 128:j * 129 + 129], E_run[:, j:j + 1]),
                        "SV")
                phase_end(vector, 1)
            SV_CC = cnt["SV"]

            @block.sync
            def _(sync):
                phase_start(sync)
                for it in range(NIT):
                    half, ti, jj, j = split_it(it)
                    par = it % 2
                    t0 = ti * TA
                    sync.wait_ge(SC_, scv(it, 9))
                    sync.wait_ge(SS, cnt["SS"])
                    inc(sync.dma_start(out=kt_sp[j][:, t0:t0 + TA],
                                       in_=ktil_b[par][:]), "SS", 16)
                    sync.wait_ge(SC_, scv(it, 10))
                    sync.wait_ge(SS, cnt["SS"])
                    inc(sync.dma_start(out=ecs_sp[j][:, t0:t0 + TA],
                                       in_=ecs_b[par][:]), "SS", 16)
                    sync.wait_ge(SC_, scv(it, 8))
                    sync.wait_ge(SS, cnt["SS"])
                    inc(sync.dma_start(out=v_sp[t0:t0 + 128, j * 128:(j + 1) * 128],
                                       in_=vtm_b[par][:, 0:128]), "SS", 16)
                    sync.wait_ge(SS, cnt["SS"])
                    inc(sync.dma_start(out=v_sp[t0 + 128:t0 + 256,
                                               j * 128:(j + 1) * 128],
                                       in_=vtm_b[par][:, 128:256]), "SS", 16)
                    sync.wait_ge(SV, svv(it, 7))
                    sync.wait_ge(SS, cnt["SS"])
                    inc(sync.dma_start(out=sk_sp[j][:, t0:t0 + TA],
                                       in_=sk_b[par][:]), "SS", 16)
                    if ti >= 1:
                        sync.wait_ge(SV, svv(it, 15))
                        sync.wait_ge(SS, cnt["SS"])
                        inc(sync.dma_start(
                            out=sc_sp[j][:, (ti - 1) * TA:ti * TA],
                            in_=sct_b[par][:]), "SS", 16)
                sync.wait_ge(SV, SV_CC)
                sync.wait_ge(SS, cnt["SS"])
                inc(sync.dma_start(out=cc_in[:], in_=cc_st[:]), "SS", 16)
                sync.wait_ge(SS, cnt["SS"])
                inc(sync.dma_start(out=ursum[:], in_=cc_st[:]), "SS", 16)
                phase_end(sync, 4)
            A_SS_END = cnt["SS"]

            @block.gpsimd
            def _(gpsimd):
                gpsimd.wait_ge(SS, A_SS_END)
                if with_cc:
                    inc(gpsimd.collective_compute(
                        "AllGather", ALU.bypass,
                        replica_groups=[list(range(NCORE))],
                        ins=[cc_in[:]],
                        outs=[cc_out[:]],
                    ), "SG")
                else:
                    inc(gpsimd.nop(), "SG")
                phase_end(gpsimd, 3)
        AG_DONE = cnt["SG"]
        phase_no[0] += 1

        if upto == "A":
            @block.sync
            def _(sync):
                phase_start(sync)
                n0 = cnt["SS"]
                for fc in range(FC):
                    for tt in range(NT // 256):
                        inc(sync.dma_start(
                            out=outT[fc * 128:(fc + 1) * 128,
                                     tt * 256:(tt + 1) * 256],
                            in_=zer[:, 0:256]), "SS", 16)
                sync.wait_ge(SS, cnt["SS"])
                sync.wait_ge(SG, AG_DONE)
            return nc

        def _tail_zeros():
            @block.sync
            def _(sync):
                phase_start(sync)
                for fc in range(FC):
                    for tt in range(NT // 256):
                        sync.wait_ge(SS, cnt["SS"])
                        inc(sync.dma_start(
                            out=outT[fc * 128:(fc + 1) * 128,
                                     tt * 256:(tt + 1) * 256],
                            in_=zer[:, 0:256]), "SS", 16)
                sync.wait_ge(SS, cnt["SS"])
                sync.wait_ge(SG, AG_DONE)

        with contextlib.ExitStack() as eb:
            sbb = lambda n, s, d: eb.enter_context(nc.sbuf_tensor(n, s, d))
            x_b1 = sbb("x_b1", [128, FC * NT], BF16)
            wq_sb = [sbb(f"wqs{i}", [128, 4096], BF16) for i in range(2)]
            ecs_ld = [sbb(f"ecsl{i}", [128, NT], BF16) for i in range(2)]
            qT_q = sbb("qT_q", [128, NT], F32)
            eq_f = sbb("eq_f", [128, NT], BF16)
            t1q = sbb("t1q", [128, TB], F32)
            mq = sbb("mq", [128, TB], F32)
            qt_st = sbb("qt_st", [128, NT], BF16)
            sq_st = sbb("sq_st", [128, NT], BF16)
            rq_st = sbb("rq_st", [128, 16], F32)

            T0, C0, V0, S0 = (cnt[k] for k in ("ST", "SC", "SV", "SS"))
            TPIq, CPIq, VPIq = 9, 5, 17

            def stq(h, k):
                return T0 + h * TPIq + k if h >= 0 else T0

            def scq(h, k):
                return C0 + h * CPIq + k if h >= 0 else C0

            def svq(h, k):
                return V0 + h * VPIq + k if h >= 0 else V0

            ss_after_load = {}
            ss_after_spill = {}


            @block.sync
            def _(sync):
                phase_start(sync)
                for h in range(NH):
                    j = h // 4
                    if h % 4 == 0:
                        if j >= 2:
                            sync.wait_ge(SV, svq(4 * (j - 2) + 3, 16))
                        sync.wait_ge(SS, cnt["SS"])
                        inc(sync.dma_start(out=ecs_ld[j % 2][:],
                                           in_=ecs_sp[j][:, HALO:NTH]), "SS", 16)
                        ss_after_load[j] = cnt["SS"]
                    sync.wait_ge(SV, svq(h, 16))
                    sync.wait_ge(SS, cnt["SS"])
                    inc(sync.dma_start(out=qt_sp[h][:], in_=qt_st[:]), "SS", 16)
                    sync.wait_ge(SV, svq(h, 12))
                    sync.wait_ge(SS, cnt["SS"])
                    inc(sync.dma_start(out=sq_sp[h][:], in_=sq_st[:]), "SS", 16)
                    sync.wait_ge(SV, svq(h, 17))
                    sync.wait_ge(SS, cnt["SS"])
                    inc(sync.dma_start(out=rq_sp[h][:], in_=rq_st[:]), "SS", 16)
                    ss_after_spill[h] = cnt["SS"]
                phase_end(sync, 4)

            @block.gpsimd
            def _(gpsimd):
                phase_start(gpsimd)
                gpsimd.wait_ge(SS, S0)
                for i in range(8):
                    gpsimd.wait_ge(SX, cnt["SX"])
                    inc(gpsimd.dma_start(
                        out=x_b1[:, i * 4 * NT:(i + 1) * 4 * NT]
                        .rearrange("p (fc t) -> p fc t", fc=4),
                        in_=xT[i * 4 * 128:(i + 1) * 4 * 128, HALO:NTH]
                        .rearrange("(fc p) t -> p fc t", p=128)), "SX", 16)
                for h in range(NH):
                    if h >= 2:
                        gpsimd.wait_ge(ST_, stq(h - 2, 4))
                    gpsimd.wait_ge(SX, cnt["SX"])
                    inc(gpsimd.dma_start(out=wq_sb[h % 2][:], in_=wq_d[h]),
                        "SX", 16)
                phase_end(gpsimd, 3)
            SXq0 = T0 * 0 + cnt["SX"] - NH * 16

            @block.tensor
            def _(tensor):
                phase_start(tensor)
                for h in range(NH):
                    projb = [pA, pB, pC, pD]
                    tensor.wait_ge(SX, SXq0 + (h + 1) * 16)
                    if h >= 1:
                        tensor.wait_ge(SV, svq(h - 1, 17))
                        tensor.wait_ge(SC_, scq(h - 1, 5))
                    for tt in range(NTB):
                        for fc in range(FC):
                            ins = tensor.matmul(
                                projb[tt][:, 0:TB],
                                wq_sb[h % 2][:, fc * 128:(fc + 1) * 128],
                                x_b1[:, fc * NT + tt * TB: fc * NT + (tt + 1) * TB],
                                start=(fc == 0), stop=(fc == FC - 1))
                        inc(ins, "ST")
                    for tt in range(NTB):
                        tensor.wait_ge(SC_, scq(h, tt + 1))
                        if tt == 0:
                            if h >= 1:
                                tensor.wait_ge(SV, svq(h - 1, 11))
                        else:
                            tensor.wait_ge(SV, svq(h, 3 * (tt - 1) + 2))
                        inc(tensor.matmul(pE[:, 0:TB], Rts[:],
                                          qT_q[:, tt * TB:(tt + 1) * TB],
                                          start=True, stop=True), "ST")
                    tensor.wait_ge(SC_, scq(h, 5))
                    if h >= 1:
                        tensor.wait_ge(SV, svq(h - 1, 17))
                    for sub in range(16):
                        ins = tensor.matmul(pG[:, sub:sub + 1],
                                            eq_f[:, sub * 128:(sub + 1) * 128],
                                            twosb[:],
                                            start=(sub == 0), stop=(sub == 15))
                    inc(ins, "ST")
                phase_end(tensor, 0)

            @block.scalar
            def _(scalar):
                phase_start(scalar)

                def sop(fn):
                    scalar.wait_ge(SC_, cnt["SC"])
                    inc(fn(), "SC")

                for h in range(NH):
                    projb = [pA, pB, pC, pD]
                    if h >= 1:
                        scalar.wait_ge(SV, svq(h - 1, 17))
                        scalar.wait_ge(ST_, stq(h - 1, 9))
                    for tt in range(NTB):
                        scalar.wait_ge(ST_, stq(h, tt + 1))
                        sop(lambda tt=tt: scalar.copy(
                            qT_q[:, tt * TB:(tt + 1) * TB], projb[tt][:, 0:TB]))
                    sop(lambda: scalar.activation(eq_f[:], qT_q[:], AF.Exp))
                phase_end(scalar, 2)

            @block.vector
            def _(vector):
                phase_start(vector)

                def vop(fn):
                    vector.wait_ge(SV, cnt["SV"])
                    inc(fn(), "SV")

                for h in range(NH):
                    j = h // 4
                    if h >= 1:
                        vector.wait_ge(SS, ss_after_spill[h - 1])
                        vector.wait_ge(ST_, stq(h - 1, 9))
                    for tt in range(NTB):
                        vector.wait_ge(SC_, scq(h, tt + 1))
                        vop(lambda tt=tt: vector.tensor_tensor(
                            t1q[:], qT_q[:, tt * TB:(tt + 1) * TB],
                            cosw[:, HALO + tt * TB:HALO + (tt + 1) * TB],
                            ALU.mult))
                        vector.wait_ge(ST_, stq(h, 4 + tt + 1))
                        vop(lambda tt=tt: vector.tensor_tensor(
                            mq[:], pE[:, 0:TB],
                            sinw[:, HALO + tt * TB:HALO + (tt + 1) * TB],
                            ALU.mult))
                        vop(lambda tt=tt: vector.tensor_tensor(
                            sq_st[:, tt * TB:(tt + 1) * TB], t1q[:], mq[:],
                            ALU.add))
                    vector.wait_ge(SC_, scq(h, 5))
                    vector.wait_ge(SS, ss_after_load[j])
                    for tt in range(NTB):
                        vop(lambda tt=tt: vector.tensor_tensor(
                            qt_st[:, tt * TB:(tt + 1) * TB],
                            eq_f[:, tt * TB:(tt + 1) * TB],
                            ecs_ld[j % 2][:, tt * TB:(tt + 1) * TB],
                            ALU.mult))
                    vector.wait_ge(ST_, stq(h, 9))
                    vop(lambda: vector.reciprocal(rq_st[:], pG[:, 0:16]))
                phase_end(vector, 1)

        phase_no[0] += 1
        eab.close()
        if upto == "B1":
            _tail_zeros()
            return nc

        ebc = contextlib.ExitStack()
        oTr = ebc.enter_context(nc.sbuf_tensor("oTr", [128, NH * NT], BF16))
        with contextlib.ExitStack() as eb:
            sbb = lambda n, s, d: eb.enter_context(nc.sbuf_tensor(n, s, d))
            ag_sb = sbb("ag_sb", [128, KVH * 129], F32)
            ktil_w = [sbb(f"ktw{i}", [128, NT], BF16) for i in range(2)]
            sk_w = [sbb(f"skw{i}", [128, NT + 128], BF16) for i in range(2)]
            v_w = [sbb(f"vw{i}", [128, 17 * 129], BF16) for i in range(2)]
            sc_w = [sbb(f"scw{i}", [128, NT], BF16) for i in range(2)]
            qt_h = [sbb(f"qth{i}", [128, NT], BF16) for i in range(2)]
            sq_h = [sbb(f"sqh{i}", [128, NT], BF16) for i in range(2)]
            rq_h = [sbb(f"rqh{i}", [128, 16], F32) for i in range(2)]
            oh_acc = [sbb(f"oha{i}", [128, NT], BF16) for i in range(2)]
            Pb = [sbb(f"Pb{i}", [128, 128], F32) for i in range(2)]
            Sin_b = sbb("Sin_b", [128, 128], F32)
            egt_f = [sbb(f"egt{i}", [128, 128], F32) for i in range(2)]
            eAt_f = [sbb(f"eAt{i}", [128, 128], F32) for i in range(2)]
            eBt_f = [sbb(f"eBt{i}", [128, 128], F32) for i in range(2)]
            eg_sb = [sbb(f"egb{i}", [128, 128], BF16) for i in range(2)]
            eA_sb = [sbb(f"eAb{i}", [128, 128], BF16) for i in range(2)]
            eB_sb = [sbb(f"eBb{i}", [128, 128], BF16) for i in range(2)]
            tmp_f = [sbb(f"tmpf{i}", [128, 128], F32) for i in range(2)]
            r_swa_col = [sbb(f"rsw{i}", [128, 1], F32) for i in range(2)]

            T0, C0, V0, S0 = (cnt[k] for k in ("ST", "SC", "SV", "SS"))
            TPIm, CPIm, VPIm = 5, 4, 5
            PVj = 34

            def stm(m, k):
                return T0 + m * TPIm + k if m >= 0 else T0

            def scm(m, k):
                return C0 + m * CPIm + k if m >= 0 else C0

            def svm(m, k):
                j = m // 64
                return V0 + (j + 1) * PVj + m * VPIm + k if m >= 0 else V0

            def svm_prol(j):
                return V0 + (j + 1) * PVj + j * 64 * VPIm

            ss_jload = {}
            ss_hload = {}
            ss_ohspill = {}
            ss_tr_all = [0]

            def oh_to_oTr(sync, h):
                sync.wait_ge(SV, svm((h + 1) * 16 - 1, 5))
                for c in range(16):
                    inc(sync.dma_start(
                        out=oTr[:, h * NT + c * 128:h * NT + (c + 1) * 128],
                        in_=oh_acc[h % 2][:, c * 128:(c + 1) * 128],
                        transpose=True), "SS", 16)
                if "ot_sp" in debug:
                    sync.wait_ge(SS, cnt["SS"])
                    inc(sync.dma_start(out=ot_sp[h][:], in_=oh_acc[h % 2][:]),
                        "SS", 16)
                ss_ohspill[h] = cnt["SS"]

            @block.sync
            def _(sync):
                phase_start(sync)
                sync.wait_ge(SG, AG_DONE)
                sync.wait_ge(SS, cnt["SS"])
                for j in range(KVH):
                    jp = j % 2
                    if j >= 2:
                        sync.wait_ge(ST_, stm((j - 1) * 64 - 1, TPIm))
                        sync.wait_ge(SV, svm((j - 1) * 64 - 1, VPIm))
                    sync.wait_ge(SS, cnt["SS"])
                    inc(sync.dma_start(
                        out=ag_sb[:].rearrange("p (a c) -> p a c", a=KVH),
                        in_=cc_out[:, j * 129:(j + 1) * 129]
                        .rearrange("(a p) c -> p a c", p=128)), "SS", 16)
                    for dst, srcap in [
                            (ktil_w[jp][:], kt_sp[j][:, HALO:NTH]),
                            (sk_w[jp][:], sk_sp[j][:, HALO - 128:NTH]),
                            (sc_w[jp][:], sc_sp[j][:])]:
                        sync.wait_ge(SS, cnt["SS"])
                        inc(sync.dma_start(out=dst, in_=srcap), "SS", 16)
                    for k in range(17):
                        sync.wait_ge(SS, cnt["SS"])
                        inc(sync.dma_start(
                            out=v_w[jp][:, k * 129:k * 129 + 128],
                            in_=v_sp[128 + k * 128:256 + k * 128,
                                     j * 128:(j + 1) * 128]), "SS", 16)
                    ss_jload[j] = cnt["SS"]
                    for hh in range(4):
                        h = j * 4 + hh
                        hp = h % 2
                        if h >= 2:
                            sync.wait_ge(ST_, stm((h - 1) * 16 - 1, TPIm))
                        for dst, srcap in [(qt_h[hp][:], qt_sp[h][:]),
                                           (sq_h[hp][:], sq_sp[h][:]),
                                           (rq_h[hp][:], rq_sp[h][:])]:
                            sync.wait_ge(SS, cnt["SS"])
                            inc(sync.dma_start(out=dst, in_=srcap), "SS", 16)
                        ss_hload[h] = cnt["SS"]
                        if h >= 1:
                            oh_to_oTr(sync, h - 1)
                oh_to_oTr(sync, NH - 1)
                ss_tr_all[0] = cnt["SS"]
                phase_end(sync, 4)

            @block.gpsimd
            def _(gpsimd):
                phase_start(gpsimd)
                phase_end(gpsimd, 3)

            @block.tensor
            def _(tensor):
                phase_start(tensor)
                for m in range(NH * 16):
                    h, c = divmod(m, 16)
                    j = h // 4
                    hp, p2 = h % 2, m % 2
                    scg = pA if p2 == 0 else pB
                    scA = pC if p2 == 0 else pD
                    if c == 0:
                        tensor.wait_ge(SS, ss_hload[h])
                        tensor.wait_ge(SS, ss_jload[j])
                        tensor.wait_ge(SV, svm_prol(j))
                    if m >= 2:
                        tensor.wait_ge(SC_, scm(m - 2, 1))
                        tensor.wait_ge(SC_, scm(m - 2, 2))
                    inc(tensor.matmul(scg[:, 0:128], ktil_w[j % 2][:, c * 128:(c + 1) * 128],
                                      qt_h[hp][:, c * 128:(c + 1) * 128],
                                      start=True, stop=True), "ST")
                    inc(tensor.matmul(scA[:, 0:128], sk_w[j % 2][:, c * 128:(c + 1) * 128],
                                      sq_h[hp][:, c * 128:(c + 1) * 128],
                                      start=True, stop=True), "ST")
                    if m >= 1:
                        tensor.wait_ge(SC_, scm(m - 1, 4))
                        tensor.wait_ge(SV, svm(m - 1, 4))
                    inc(tensor.matmul(pE[:, 0:128],
                                      sk_w[j % 2][:, 128 + c * 128:256 + c * 128],
                                      sq_h[hp][:, c * 128:(c + 1) * 128],
                                      start=True, stop=True), "ST")
                    tensor.wait_ge(SV, svm(m, 1))
                    if m >= 1:
                        tensor.wait_ge(SV, svm(m - 1, 5))
                    tensor.matmul(pG[:, 0:128], eg_sb[p2][:],
                                  v_w[j % 2][:, (c + 1) * 129:(c + 1) * 129 + 128],
                                  start=True, stop=False)
                    inc(tensor.matmul(pG[:, 0:128], qt_h[hp][:, c * 128:(c + 1) * 128],
                                      sc_w[j % 2][:, c * 128:(c + 1) * 128],
                                      start=False, stop=True), "ST")
                    tensor.wait_ge(SV, svm(m, 3))
                    tensor.wait_ge(SC_, scm(m, 3))
                    tensor.matmul(pE[:, 0:129], eA_sb[p2][:],
                                  v_w[j % 2][:, c * 129:c * 129 + 129],
                                  start=True, stop=False)
                    inc(tensor.matmul(pE[:, 0:129], eB_sb[p2][:],
                                      v_w[j % 2][:, (c + 1) * 129:(c + 2) * 129],
                                      start=False, stop=True), "ST")
                phase_end(tensor, 0)

            @block.scalar
            def _(scalar):
                phase_start(scalar)

                def sop(fn):
                    scalar.wait_ge(SC_, cnt["SC"])
                    inc(fn(), "SC")

                for m in range(NH * 16):
                    h, c = divmod(m, 16)
                    j = h // 4
                    p2 = m % 2
                    scg = pA if p2 == 0 else pB
                    scA = pC if p2 == 0 else pD
                    cR = j * NCHH + 2 + c
                    if m >= 2:
                        scalar.wait_ge(SV, svm(m - 2, VPIm))
                    else:
                        scalar.wait_ge(SV, V0)
                    scalar.wait_ge(ST_, stm(m, 1))
                    sop(lambda: scalar.activation(egt_f[p2][:], scg[:, 0:128],
                                                  AF.Copy,
                                                  scale=rk_all[:, cR:cR + 1]))
                    scalar.wait_ge(ST_, stm(m, 2))
                    sop(lambda: scalar.activation(eAt_f[p2][:], scA[:, 0:128],
                                                  AF.Exp, scale=SCL))
                    scalar.wait_ge(ST_, stm(m, 3))
                    sop(lambda: scalar.activation(eBt_f[p2][:], pE[:, 0:128],
                                                  AF.Exp, scale=SCL))
                    scalar.wait_ge(ST_, stm(m, 5))
                    scalar.wait_ge(SV, svm(m, 4))
                    sop(lambda: scalar.activation(tmp_f[p2][:], pE[:, 0:128],
                                                  AF.Copy,
                                                  scale=r_swa_col[p2][:]))
                phase_end(scalar, 2)

            @block.vector
            def _(vector):
                phase_start(vector)

                def vop(fn):
                    vector.wait_ge(SV, cnt["SV"])
                    inc(fn(), "SV")

                for j in range(KVH):
                    jp = j % 2
                    vector.wait_ge(SS, ss_jload[j])
                    nops = 0
                    vop(lambda: vector.memset(
                        v_w[jp][:].rearrange("p (k c) -> p k c", c=129)[:, :, 128:129],
                        2.0)); nops += 1
                    vop(lambda: vector.memset(Sin_b[:], 0.0)); nops += 1
                    for i in range(NCORE):
                        pcur, pprev = Pb[i % 2], Pb[(i + 1) % 2]
                        if i in (0, 4):
                            vop(lambda pcur=pcur: vector.memset(pcur[:], 0.0))
                        else:
                            off = (i - 1) * 129
                            vop(lambda pcur=pcur, pprev=pprev, off=off:
                                vector.scalar_tensor_tensor(
                                    pcur[:], pprev[:],
                                    ag_sb[:, off + 128:off + 129],
                                    ag_sb[:, off:off + 128],
                                    ALU.mult, ALU.add))
                        nops += 1
                        vop(lambda pcur=pcur, i=i: vector.scalar_tensor_tensor(
                            Sin_b[:], pcur[:], ohs[:, i:i + 1], Sin_b[:],
                            ALU.mult, ALU.add))
                        nops += 1
                    for c in range(16):
                        cR = j * NCHH + 2 + c
                        vop(lambda c=c, cR=cR: vector.scalar_tensor_tensor(
                            sc_w[jp][:, c * 128:(c + 1) * 128], Sin_b[:],
                            Ec_all[:, cR:cR + 1],
                            sc_w[jp][:, c * 128:(c + 1) * 128],
                            ALU.mult, ALU.add))
                        nops += 1
                    assert nops == PVj, nops
                    for mm_ in range(64):
                        m = j * 64 + mm_
                        h, c = divmod(m, 16)
                        p2 = m % 2
                        if m >= 2:
                            vector.wait_ge(ST_, stm(m - 2, TPIm))
                            vector.wait_ge(SC_, scm(m - 2, CPIm))
                        if c == 0:
                            vector.wait_ge(SS, ss_ohspill.get(h - 2, 0) or 0)
                        vector.wait_ge(SC_, scm(m, 1))
                        vop(lambda p2=p2: vector.tensor_tensor(
                            eg_sb[p2][:], egt_f[p2][:], mgs[:], ALU.mult))
                        vector.wait_ge(SC_, scm(m, 2))
                        mA_use = mAfs if c == 0 else mAs
                        vop(lambda p2=p2, mA_use=mA_use: vector.tensor_tensor(
                            eA_sb[p2][:], eAt_f[p2][:], mA_use[:], ALU.mult))
                        vector.wait_ge(SC_, scm(m, 3))
                        vop(lambda p2=p2: vector.tensor_tensor(
                            eB_sb[p2][:], eBt_f[p2][:], mBs[:], ALU.mult))
                        vector.wait_ge(ST_, stm(m, 5))
                        vop(lambda p2=p2: vector.reciprocal(
                            r_swa_col[p2][:], pE[:, 128:129]))
                        vector.wait_ge(ST_, stm(m, 4))
                        vector.wait_ge(SC_, scm(m, 4))
                        hp = h % 2
                        vop(lambda p2=p2, hp=hp, h=h, c=c: vector.scalar_tensor_tensor(
                            oh_acc[hp][:, c * 128:(c + 1) * 128], pG[:, 0:128],
                            rq_h[hp][:, c:c + 1], tmp_f[p2][:],
                            ALU.mult, ALU.add))
                phase_end(vector, 1)

        phase_no[0] += 1
        if upto == "B2":
            _tail_zeros()
            return nc

        with contextlib.ExitStack() as ec:
            sbc = lambda n, s, d: ec.enter_context(nc.sbuf_tensor(n, s, d))
            xw = [sbc(f"xw{i}", [128, NT], F32) for i in range(2)]
            s1_sb = [sbc(f"s1sb{i}", [128, NT], F32) for i in range(2)]
            s1sq = [sbc(f"s1sq{i}", [128, TB], BF16) for i in range(4)]
            s1bb = [sbc(f"s1bb{i}", [128, TB], BF16) for i in range(4)]
            wo_sb = [sbc(f"wos{i}", [128, 4096], BF16) for i in range(2)]
            m16 = sbc("m16", [128, 16], F32)
            e16 = sbc("e16", [128, 16], F32)
            t16 = sbc("t16", [128, 16], F32)
            v16 = sbc("v16", [128, 16], F32)

            T0, C0, V0, S0 = (cnt[k] for k in ("ST", "SC", "SV", "SS"))
            T1 = T0
            C1 = C0
            V1 = V0 + 1
            TPIc, CPIc, VPIc = 5, 8, 5

            def stc(do, k):
                return T1 + do * TPIc + k if do >= 0 else T1

            def scc(do, k):
                return C1 + do * CPIc + k if do >= 0 else C1

            def svc(do, k):
                return V1 + do * VPIc + k if do >= 0 else V1

            ss_xw = {}
            ss_s1sp = {}

            @block.sync
            def _(sync):
                phase_start(sync)
                for do in range(FC):
                    if do >= 2:
                        sync.wait_ge(SV, svc(do - 1, 4))
                    sync.wait_ge(SS, cnt["SS"])
                    inc(sync.dma_start(out=xw[do % 2][:],
                                       in_=xT[do * 128:(do + 1) * 128, HALO:NTH]),
                        "SS", 16)
                    ss_xw[do] = cnt["SS"]
                    sync.wait_ge(SV, svc(do, 4))
                    sync.wait_ge(SC_, scc(do, 8))
                    sync.wait_ge(SS, cnt["SS"])
                    inc(sync.dma_start(out=s1_sp[do][:], in_=s1_sb[do % 2][:]),
                        "SS", 16)
                    ss_s1sp[do] = cnt["SS"]
                phase_end(sync, 4)

            @block.gpsimd
            def _(gpsimd):
                phase_start(gpsimd)
                gpsimd.wait_ge(SS, S0)
                for do in range(FC):
                    if do >= 2:
                        gpsimd.wait_ge(ST_, stc(do - 1, 4))
                    gpsimd.wait_ge(SX, cnt["SX"])
                    inc(gpsimd.dma_start(out=wo_sb[do % 2][:], in_=wo_d[do]),
                        "SX", 16)
                phase_end(gpsimd, 3)
            SXc0 = cnt["SX"] - FC * 16

            @block.tensor
            def _(tensor):
                phase_start(tensor)
                tensor.wait_ge(SS, ss_tr_all[0])
                for do in range(FC):
                    tensor.wait_ge(SX, SXc0 + (do + 1) * 16)
                    banks4 = [pA, pB, pC, pD]
                    for tt in range(NTB):
                        if do >= 1:
                            tensor.wait_ge(SV, svc(do - 1, tt + 1))
                        bank = banks4[tt]
                        for he in range(NH):
                            ins = tensor.matmul(
                                bank[:, 0:TB],
                                wo_sb[do % 2][:, he * 128:(he + 1) * 128],
                                oTr[:, he * NT + tt * TB: he * NT + (tt + 1) * TB],
                                start=(he == 0), stop=(he == NH - 1))
                        inc(ins, "ST")
                    if do >= 1:
                        tensor.wait_ge(SV, svc(do - 1, 5))
                    first = True
                    for tt in range(NTB):
                        tensor.wait_ge(SV, svc(do, tt + 1))
                        tensor.wait_ge(SC_, scc(do, 2 * tt + 2))
                        for sub in range(4):
                            col = tt * 8 + sub * 2
                            ins = tensor.matmul(
                                pG[:, col:col + 1],
                                s1bb[tt][:, sub * 128:(sub + 1) * 128],
                                onesb[:], start=first, stop=False)
                            first = False
                            last = (tt == NTB - 1 and sub == 3)
                            ins = tensor.matmul(
                                pG[:, col + 1:col + 2],
                                s1sq[tt][:, sub * 128:(sub + 1) * 128],
                                onesb[:], start=False, stop=last)
                    inc(ins, "ST")
                phase_end(tensor, 0)

            @block.scalar
            def _(scalar):
                phase_start(scalar)

                def sop(fn):
                    scalar.wait_ge(SC_, cnt["SC"])
                    inc(fn(), "SC")

                for do in range(FC):
                    if do >= 1:
                        scalar.wait_ge(ST_, stc(do - 1, 5))
                    for tt in range(NTB):
                        scalar.wait_ge(SV, svc(do, tt + 1))
                        sop(lambda do=do, tt=tt: scalar.activation(
                            s1sq[tt][:], s1_sb[do % 2][:, tt * TB:(tt + 1) * TB],
                            AF.Square))
                        sop(lambda do=do, tt=tt: scalar.copy(
                            s1bb[tt][:], s1_sb[do % 2][:, tt * TB:(tt + 1) * TB]))
                scalar.wait_ge(SV, svc(FC - 1, 5) + 6)
                sop(lambda: scalar.activation(ab1[:, 0:16], v16[:], AF.Sqrt))
                phase_end(scalar, 2)
            C_SQRT = cnt["SC"]

            @block.vector
            def _(vector):
                phase_start(vector)

                def vop(fn):
                    vector.wait_ge(SV, cnt["SV"])
                    inc(fn(), "SV")

                vop(lambda: vector.memset(statacc[:], 0.0))
                for do in range(FC):
                    if do >= 2:
                        vector.wait_ge(SC_, scc(do - 2, 8))
                        vector.wait_ge(ST_, stc(do - 2, 5))
                        vector.wait_ge(SS, ss_s1sp[do - 2])
                    for tt in range(NTB):
                        vector.wait_ge(ST_, stc(do, tt + 1))
                        vector.wait_ge(SS, ss_xw[do])
                        bank = [pA, pB, pC, pD][tt]
                        vop(lambda do=do, tt=tt, bank=bank: vector.scalar_tensor_tensor(
                            s1_sb[do % 2][:, tt * TB:(tt + 1) * TB],
                            bank[:, 0:TB], 1.0, xw[do % 2][:, tt * TB:(tt + 1) * TB],
                            ALU.mult, ALU.add))
                    vector.wait_ge(ST_, stc(do, 5))
                    vop(lambda: vector.tensor_tensor(statacc[:], statacc[:],
                                                     pG[:, 0:32], ALU.add))
                sevn = statacc[:].rearrange("p (g s) -> p g s", s=2)
                vop(lambda: vector.tensor_scalar(m16[:], sevn[:, :, 0:1],
                                                 1.0 / 4096.0, None, ALU.mult))
                vop(lambda: vector.tensor_scalar(e16[:], sevn[:, :, 1:2],
                                                 1.0 / 4096.0, None, ALU.mult))
                vop(lambda: vector.tensor_tensor(t16[:], m16[:], m16[:], ALU.mult))
                vop(lambda: vector.tensor_tensor(v16[:], e16[:], t16[:],
                                                 ALU.subtract))
                vop(lambda: vector.tensor_scalar(v16[:], v16[:], 1e-5, None,
                                                 ALU.add))
                vop(lambda: vector.reciprocal(v16[:], v16[:]))
                vector.wait_ge(SC_, C_SQRT)
                vop(lambda: vector.tensor_tensor(t16[:], m16[:], ab1[:, 0:16],
                                                 ALU.mult))
                vop(lambda: vector.tensor_scalar(ab1[:, 16:32], t16[:], -1.0,
                                                 None, ALU.mult))
                phase_end(vector, 1)

        ebc.close()
        phase_no[0] += 1
        if upto == "C":
            _tail_zeros()
            return nc

        with contextlib.ExitStack() as ed:
            sbd = lambda n, s, d: ed.enter_context(nc.sbuf_tensor(n, s, d))
            x1b = sbd("x1b", [128, FC * NT], BF16)
            xab = sbd("xab", [128, 2 * NT], BF16)
            m16d = sbd("m16d", [128, 16], F32)
            e16d = sbd("e16d", [128, 16], F32)
            t16d = sbd("t16d", [128, 16], F32)
            v16d = sbd("v16d", [128, 16], F32)

            T0, C0, V0, S0 = (cnt[k] for k in ("ST", "SC", "SV", "SS"))

            def svd2(do, k):
                return V0 + 97 + do * 5 + k if do >= 0 else V0 + 97

            ss_s1ld = {}
            ss_x1w = {}
            ss_x1sp = {}
            ss_s2sp = {}

            ed1 = contextlib.ExitStack()
            s1_ld = [ed1.enter_context(nc.sbuf_tensor(f"s1ld{i}", [128, NT], F32))
                     for i in range(2)]
            wa_sb = ed1.enter_context(nc.sbuf_tensor("wa_sb", [128, 2 * 4096], BF16))
            abc_a = ed1.enter_context(nc.sbuf_tensor("abc_a", [128, NT], F32))
            abc_b = ed1.enter_context(nc.sbuf_tensor("abc_b", [128, NT], F32))
            arow = ed1.enter_context(nc.sbuf_tensor("arow", [1, TB], F32))

            @block.sync
            def _(sync):
                phase_start(sync)
                sync.wait_ge(SS, S0)
                for fc in range(FC):
                    if fc >= 2:
                        sync.wait_ge(SC_, C0 + 9 + fc - 2)
                        sync.wait_ge(SS, ss_x1sp[fc - 2])
                        sync.wait_ge(SV, V0 + 32 + 2 * (fc - 2) + 2)
                    sync.wait_ge(SS, cnt["SS"])
                    inc(sync.dma_start(out=s1_ld[fc % 2][:], in_=s1_sp[fc][:]),
                        "SS", 16)
                    ss_s1ld[fc] = cnt["SS"]
                    sync.wait_ge(SV, V0 + 32 + 2 * fc + 2)
                    sync.wait_ge(SS, cnt["SS"])
                    inc(sync.dma_start(out=x1_sp[fc][:], in_=s1_ld[fc % 2][:]),
                        "SS", 16)
                    ss_x1sp[fc] = cnt["SS"]

            @block.gpsimd
            def _(gpsimd):
                phase_start(gpsimd)
                gpsimd.wait_ge(SS, S0)
                gpsimd.wait_ge(SX, cnt["SX"])
                inc(gpsimd.dma_start(out=wa_sb[:, 0:4096], in_=wa_d[0]), "SX", 16)
                gpsimd.wait_ge(SX, cnt["SX"])
                inc(gpsimd.dma_start(out=wa_sb[:, 4096:8192], in_=wa_d[1]),
                    "SX", 16)
            SXd0 = cnt["SX"]

            @block.tensor
            def _(tensor):
                phase_start(tensor)
                for i in range(8):
                    bank = pC if i % 2 == 0 else pD
                    for c in range(4):
                        col = i * 4 + c
                        if col >= 1:
                            tensor.wait_ge(SV, V0 + col)
                        inc(tensor.matmul(pG[0:1, 0:128], ab1[:, col:col + 1],
                                          idfs[:], start=True, stop=True), "ST")
                    if i >= 2:
                        tensor.wait_ge(SC_, C0 + i - 1)
                    tensor.wait_ge(SV, V0 + i * 4 + 4)
                    inc(tensor.matmul(bank[:, 0:TB], onesr[:], arow[0:1, 0:TB],
                                      start=True, stop=True), "ST")
                tensor.wait_ge(SX, SXd0)
                for er in range(2):
                    for tt in range(NTB):
                        gi = er * 4 + tt
                        bank = pA if gi % 2 == 0 else pB
                        if gi >= 2:
                            tensor.wait_ge(SC_, C0 + 39 + gi)
                        tensor.wait_ge(SC_, C0 + 8 + 32)
                        for fc in range(FC):
                            ins = tensor.matmul(
                                bank[:, 0:TB],
                                wa_sb[:, er * 4096 + fc * 128: er * 4096 + (fc + 1) * 128],
                                x1b[:, fc * NT + tt * TB: fc * NT + (tt + 1) * TB],
                                start=(fc == 0), stop=(fc == FC - 1))
                        inc(ins, "ST")

            @block.scalar
            def _(scalar):
                phase_start(scalar)

                def sop(fn):
                    scalar.wait_ge(SC_, cnt["SC"])
                    inc(fn(), "SC")

                for i in range(8):
                    bank = pC if i % 2 == 0 else pD
                    dst = abc_a if i < 4 else abc_b
                    tt = i % 4
                    scalar.wait_ge(ST_, T0 + 5 * i + 5)
                    sop(lambda dst=dst, tt=tt, bank=bank: scalar.copy(
                        dst[:, tt * TB:(tt + 1) * TB], bank[:, 0:TB]))
                for fc in range(FC):
                    scalar.wait_ge(SV, V0 + 32 + 2 * fc + 2)
                    sop(lambda fc=fc: scalar.copy(
                        x1b[:, fc * NT:(fc + 1) * NT], s1_ld[fc % 2][:]))
                for gi in range(8):
                    er, tt = divmod(gi, 4)
                    bank = pA if gi % 2 == 0 else pB
                    scalar.wait_ge(ST_, T0 + 40 + gi + 1)
                    sop(lambda er=er, tt=tt, bank=bank: scalar.activation(
                        xab[:, er * NT + tt * TB: er * NT + (tt + 1) * TB],
                        bank[:, 0:TB], AF.Copy, scale=wvec_s[:, er:er + 1]))

            @block.vector
            def _(vector):
                phase_start(vector)

                def vop(fn):
                    vector.wait_ge(SV, cnt["SV"])
                    inc(fn(), "SV")

                for col in range(32):
                    i, c = divmod(col, 4)
                    vector.wait_ge(ST_, T0 + 5 * i + c + 1)
                    if c == 0 and i >= 1:
                        vector.wait_ge(ST_, T0 + 5 * (i - 1) + 5)
                    vop(lambda i=i, c=c: vector.tensor_copy(
                        arow[0:1, c * 128:(c + 1) * 128], pG[0:1, 0:128]))
                for fc in range(FC):
                    vector.wait_ge(SS, ss_s1ld[fc])
                    vector.wait_ge(SC_, C0 + 8)
                    vop(lambda fc=fc: vector.tensor_tensor(
                        s1_ld[fc % 2][:], s1_ld[fc % 2][:], abc_a[:], ALU.mult))
                    vop(lambda fc=fc: vector.tensor_tensor(
                        s1_ld[fc % 2][:], s1_ld[fc % 2][:], abc_b[:], ALU.add))
                vop(lambda: vector.memset(statacc[:], 0.0))

            ed1.close()
            wf_sb = sbd("wf_sb", [128, 4096], BF16)
            wb_sb = sbd("wb_sb", [128, 256], BF16)
            x1w = [sbd(f"x1w{i}", [128, NT], F32) for i in range(2)]
            s2_sb = [sbd(f"s2sb{i}", [128, NT], F32) for i in range(2)]
            s2sq = [sbd(f"s2sq{i}", [128, TB], BF16) for i in range(4)]
            s2bb = [sbd(f"s2bb{i}", [128, TB], BF16) for i in range(4)]
            D1_SC_END = C0 + 48
            D1_SS_X1 = lambda: ss_x1sp[FC - 1]

            @block.sync
            def _(sync):
                sync.wait_ge(SC_, D1_SC_END)
                sync.wait_ge(SV, V0 + 97)
                for do in range(FC):
                    if do >= 2:
                        sync.wait_ge(SV, svd2(do - 2, 4))
                    sync.wait_ge(SS, cnt["SS"])
                    inc(sync.dma_start(out=x1w[do % 2][:], in_=x1_sp[do][:]),
                        "SS", 16)
                    ss_x1w[do] = cnt["SS"]
                    sync.wait_ge(SV, svd2(do, 4))
                    sync.wait_ge(SC_, C0 + 48 + do * 8 + 8)
                    sync.wait_ge(SS, cnt["SS"])
                    inc(sync.dma_start(out=s2_sp[do][:], in_=s2_sb[do % 2][:]),
                        "SS", 16)
                    ss_s2sp[do] = cnt["SS"]
                phase_end(sync, 4)

            @block.gpsimd
            def _(gpsimd):
                gpsimd.wait_ge(ST_, T0 + 48)
                gpsimd.wait_ge(SC_, D1_SC_END)
                gpsimd.wait_ge(SS, ss_x1sp[FC - 1])
                for do in range(FC):
                    if do >= 1:
                        gpsimd.wait_ge(ST_, T0 + 48 + do * 5)
                    gpsimd.wait_ge(SX, cnt["SX"])
                    inc(gpsimd.dma_start(out=wf_sb[:], in_=wf_d[do]), "SX", 16)
                    gpsimd.wait_ge(SX, cnt["SX"])
                    inc(gpsimd.dma_start(out=wb_sb[:], in_=wb_d[do]), "SX", 16)
                phase_end(gpsimd, 3)
            SXd1 = cnt["SX"] - FC * 32

            @block.tensor
            def _(tensor):
                for do in range(FC):
                    tensor.wait_ge(SX, SXd1 + (do + 1) * 32)
                    for tt in range(NTB):
                        gi2 = do * 4 + tt
                        bank = [pA, pB, pC, pD][tt]
                        if do >= 1:
                            tensor.wait_ge(SV, svd2(do - 1, tt + 1))
                        else:
                            tensor.wait_ge(SC_, D1_SC_END)
                        for fc in range(FC):
                            tensor.matmul(
                                bank[:, 0:TB],
                                wf_sb[:, fc * 128:(fc + 1) * 128],
                                x1b[:, fc * NT + tt * TB: fc * NT + (tt + 1) * TB],
                                start=(fc == 0), stop=False)
                        tensor.matmul(bank[:, 0:TB], wb_sb[:, 0:128],
                                      xab[:, 0 * NT + tt * TB: 0 * NT + (tt + 1) * TB],
                                      start=False, stop=False)
                        ins = tensor.matmul(bank[:, 0:TB], wb_sb[:, 128:256],
                                            xab[:, 1 * NT + tt * TB: 1 * NT + (tt + 1) * TB],
                                            start=False, stop=True)
                        inc(ins, "ST")
                    if do >= 1:
                        tensor.wait_ge(SV, svd2(do - 1, 5))
                    first = True
                    for tt in range(NTB):
                        tensor.wait_ge(SV, svd2(do, tt + 1))
                        tensor.wait_ge(SC_, C0 + 48 + do * 8 + 2 * tt + 2)
                        for sub in range(4):
                            col = tt * 8 + sub * 2
                            tensor.matmul(
                                pG[:, col:col + 1],
                                s2bb[tt][:, sub * 128:(sub + 1) * 128],
                                onesb[:], start=first, stop=False)
                            first = False
                            last = (tt == NTB - 1 and sub == 3)
                            ins = tensor.matmul(
                                pG[:, col + 1:col + 2],
                                s2sq[tt][:, sub * 128:(sub + 1) * 128],
                                onesb[:], start=False, stop=last)
                    inc(ins, "ST")
                phase_end(tensor, 0)

            @block.scalar
            def _(scalar):

                def sop(fn):
                    scalar.wait_ge(SC_, cnt["SC"])
                    inc(fn(), "SC")

                for do in range(FC):
                    if do >= 1:
                        scalar.wait_ge(ST_, T0 + 48 + (do - 1) * 5 + 5)
                    for tt in range(NTB):
                        scalar.wait_ge(SV, svd2(do, tt + 1))
                        sop(lambda do=do, tt=tt: scalar.activation(
                            s2sq[tt][:], s2_sb[do % 2][:, tt * TB:(tt + 1) * TB],
                            AF.Square))
                        sop(lambda do=do, tt=tt: scalar.copy(
                            s2bb[tt][:], s2_sb[do % 2][:, tt * TB:(tt + 1) * TB]))
                scalar.wait_ge(SV, svd2(FC - 1, 5) + 6)
                sop(lambda: scalar.activation(ab2[:, 0:16], v16d[:], AF.Sqrt))
                phase_end(scalar, 2)
            D_SQRT = cnt["SC"]

            @block.vector
            def _(vector):

                def vop(fn):
                    vector.wait_ge(SV, cnt["SV"])
                    inc(fn(), "SV")

                for do in range(FC):
                    if do >= 2:
                        vector.wait_ge(SC_, C0 + 48 + (do - 2) * 8 + 8)
                        vector.wait_ge(SS, ss_s2sp[do - 2])
                    for tt in range(NTB):
                        vector.wait_ge(ST_, T0 + 48 + do * 5 + tt + 1)
                        vector.wait_ge(SS, ss_x1w[do])
                        bank = [pA, pB, pC, pD][tt]
                        vop(lambda do=do, tt=tt, bank=bank: vector.scalar_tensor_tensor(
                            s2_sb[do % 2][:, tt * TB:(tt + 1) * TB],
                            bank[:, 0:TB], bias_s[:, do:do + 1],
                            x1w[do % 2][:, tt * TB:(tt + 1) * TB],
                            ALU.add, ALU.add))
                    vector.wait_ge(ST_, T0 + 48 + do * 5 + 5)
                    vop(lambda: vector.tensor_tensor(statacc[:], statacc[:],
                                                     pG[:, 0:32], ALU.add))
                sevn = statacc[:].rearrange("p (g s) -> p g s", s=2)
                vop(lambda: vector.tensor_scalar(m16d[:], sevn[:, :, 0:1],
                                                 1.0 / 4096.0, None, ALU.mult))
                vop(lambda: vector.tensor_scalar(e16d[:], sevn[:, :, 1:2],
                                                 1.0 / 4096.0, None, ALU.mult))
                vop(lambda: vector.tensor_tensor(t16d[:], m16d[:], m16d[:],
                                                 ALU.mult))
                vop(lambda: vector.tensor_tensor(v16d[:], e16d[:], t16d[:],
                                                 ALU.subtract))
                vop(lambda: vector.tensor_scalar(v16d[:], v16d[:], 1e-5, None,
                                                 ALU.add))
                vop(lambda: vector.reciprocal(v16d[:], v16d[:]))
                vector.wait_ge(SC_, D_SQRT)
                vop(lambda: vector.tensor_tensor(t16d[:], m16d[:], ab2[:, 0:16],
                                                 ALU.mult))
                vop(lambda: vector.tensor_scalar(ab2[:, 16:32], t16d[:], -1.0,
                                                 None, ALU.mult))
                phase_end(vector, 1)

        phase_no[0] += 1
        if upto == "D":
            _tail_zeros()
            return nc

        with contextlib.ExitStack() as ee:
            sbe = lambda n, s, d: ee.enter_context(nc.sbuf_tensor(n, s, d))
            ab2c_a = sbe("ab2c_a", [128, NT], F32)
            ab2c_b = sbe("ab2c_b", [128, NT], F32)
            arow2 = sbe("arow2", [1, TB], F32)
            s2_ld = [sbe(f"s2ld{i}", [128, NT], F32) for i in range(2)]
            out_sb = [sbe(f"outsb{i}", [128, NT], F32) for i in range(2)]

            T0, C0, V0, S0 = (cnt[k] for k in ("ST", "SC", "SV", "SS"))
            ss_s2ld = {}
            ss_out = {}

            @block.sync
            def _(sync):
                phase_start(sync)
                sync.wait_ge(SS, S0)
                for fc in range(FC):
                    if fc >= 2:
                        sync.wait_ge(SV, V0 + 32 + 2 * (fc - 1))
                    sync.wait_ge(SS, cnt["SS"])
                    inc(sync.dma_start(out=s2_ld[fc % 2][:], in_=s2_sp[fc][:]),
                        "SS", 16)
                    ss_s2ld[fc] = cnt["SS"]
                sync.wait_ge(SS, cnt["SS"])
                phase_end(sync, 4)

            @block.gpsimd
            def _(gpsimd):
                phase_start(gpsimd)
                phase_end(gpsimd, 3)

            @block.tensor
            def _(tensor):
                phase_start(tensor)
                for i in range(8):
                    bank = pC if i % 2 == 0 else pD
                    for c in range(4):
                        col = i * 4 + c
                        if col >= 1:
                            tensor.wait_ge(SV, V0 + col)
                        inc(tensor.matmul(pG[0:1, 0:128], ab2[:, col:col + 1],
                                          idfs[:], start=True, stop=True), "ST")
                    if i >= 2:
                        tensor.wait_ge(SC_, C0 + i - 1)
                    tensor.wait_ge(SV, V0 + i * 4 + 4)
                    inc(tensor.matmul(bank[:, 0:TB], onesr[:], arow2[0:1, 0:TB],
                                      start=True, stop=True), "ST")
                phase_end(tensor, 0)

            @block.scalar
            def _(scalar):
                phase_start(scalar)

                def sop(fn):
                    scalar.wait_ge(SC_, cnt["SC"])
                    inc(fn(), "SC")

                for i in range(8):
                    bank = pC if i % 2 == 0 else pD
                    dst = ab2c_a if i < 4 else ab2c_b
                    tt = i % 4
                    scalar.wait_ge(ST_, T0 + 5 * i + 5)
                    sop(lambda dst=dst, tt=tt, bank=bank: scalar.copy(
                        dst[:, tt * TB:(tt + 1) * TB], bank[:, 0:TB]))
                for fc in range(FC):
                    scalar.wait_ge(SV, V0 + 32 + 2 * fc + 2)
                    scalar.wait_ge(SX, cnt["SX"])
                    inc(scalar.dma_start(out=outT[fc * 128:(fc + 1) * 128, :],
                                         in_=out_sb[fc % 2][:]), "SX", 16)
                    ss_out[fc] = cnt["SX"]
                scalar.wait_ge(SX, cnt["SX"])
                phase_end(scalar, 2)

            @block.vector
            def _(vector):
                phase_start(vector)

                def vop(fn):
                    vector.wait_ge(SV, cnt["SV"])
                    inc(fn(), "SV")

                for col in range(32):
                    i, c = divmod(col, 4)
                    vector.wait_ge(ST_, T0 + 5 * i + c + 1)
                    if c == 0 and i >= 1:
                        vector.wait_ge(ST_, T0 + 5 * (i - 1) + 5)
                    vop(lambda i=i, c=c: vector.tensor_copy(
                        arow2[0:1, c * 128:(c + 1) * 128], pG[0:1, 0:128]))
                for fc in range(FC):
                    vector.wait_ge(SS, ss_s2ld[fc])
                    if fc >= 2:
                        vector.wait_ge(SX, ss_out[fc - 2])
                    vector.wait_ge(SC_, C0 + 8)
                    vop(lambda fc=fc: vector.tensor_tensor(
                        out_sb[fc % 2][:], s2_ld[fc % 2][:], ab2c_a[:], ALU.mult))
                    vop(lambda fc=fc: vector.tensor_tensor(
                        out_sb[fc % 2][:], out_sb[fc % 2][:], ab2c_b[:], ALU.add))
                phase_end(vector, 1)

    return nc



def _host_layout(inputs, core):
    b, r = divmod(core, 4)
    x = inputs["L0_attn_hidden_states"]
    tok0 = r * NT
    xb = x[b]
    xs = np.zeros((NTH, 4096), np.float32)
    lo = tok0 - HALO
    src_lo = max(lo, 0)
    xs[src_lo - lo:NTH] = xb[src_lo:tok0 + NT]
    xTl = np.ascontiguousarray(xs.T)

    pos = np.arange(lo, tok0 + NT, dtype=np.float32)
    half = np.arange(64, dtype=np.float32)
    inv = (1.0 / (10000.0 ** (2.0 * half / 128.0))).astype(np.float32)
    fr = pos[None, :] * inv[:, None]
    emb = np.concatenate([fr, fr], axis=0)
    cosl = np.ascontiguousarray(np.cos(emb).astype(np.float32))
    sinl = np.ascontiguousarray(np.sin(emb).astype(np.float32))

    s = np.arange(128)[:, None]
    t = np.arange(128)[None, :]
    mg = (s <= t).astype(np.float32)
    mA = (s - t >= 64).astype(np.float32)
    mB = ((t - s >= 0) & (t - s <= 64)).astype(np.float32)
    ident = np.eye(128, dtype=np.float32)
    Rt = np.zeros((128, 128), np.float32)
    for d in range(64):
        Rt[d + 64, d] = -1.0
        Rt[d, d + 64] = 1.0
    ohv = np.zeros((128, 8), np.float32)
    ohv[:, b * 4 + r] = 1.0

    def stat_blocks(W, ocs):
        return np.ascontiguousarray(
            W.reshape(32, 128, ocs, 128).transpose(2, 1, 0, 3)
            .reshape(ocs, 128, 4096))

    Acat = inputs["L0_ffn_A"].reshape(256, 4096)
    Bcat = np.ascontiguousarray(
        np.transpose(inputs["L0_ffn_Bf"], (0, 2, 1))).reshape(256, 4096)
    return {
        "xT": xTl, "cosT": cosl, "sinT": sinl,
        "mg": mg, "mA": mA, "mAf": (mA if r > 0 else np.zeros_like(mA)),
        "mB": mB, "idb": ident.astype(BF), "idf": ident, "Rt": Rt, "ohv": ohv,
        "wq": stat_blocks(inputs["L0_attn_Wq"], 32),
        "wk": stat_blocks(inputs["L0_attn_Wk"], 8),
        "wv": stat_blocks(inputs["L0_attn_Wv"], 8),
        "wo": stat_blocks(inputs["L0_attn_Wo"], 32),
        "wf": stat_blocks(np.ascontiguousarray(inputs["L0_ffn_W"].T), 32),
        "wa": stat_blocks(np.ascontiguousarray(Acat.T), 2),
        "wb": np.ascontiguousarray(
            Bcat.reshape(2, 128, 32, 128).transpose(2, 1, 0, 3)
            .reshape(32, 128, 256)),
        "bias_pp": np.ascontiguousarray(
            inputs["L0_ffn_bias"].reshape(32, 128).T),
        "cvec": np.ascontiguousarray(
            np.repeat(inputs["L0_ffn_coeffs"], 16).reshape(2, 128).T),
        "svec": np.ascontiguousarray(
            np.repeat(inputs["L0_ffn_scales"], 16).reshape(2, 128).T),
    }


def kernel(**inputs):
    nc = build_nc()
    in_maps = [_host_layout(inputs, c) for c in range(NCORE)]
    res = run_bass_kernel_spmd(nc, in_maps, core_ids=list(range(NCORE)))
    x = inputs["L0_attn_hidden_states"]
    out = np.zeros_like(x)
    for c in range(NCORE):
        b, r = divmod(c, 4)
        out[b, r * NT:(r + 1) * NT] = res.results[c]["outT"].T
    return out


# version 07
# speedup vs baseline: 1.0634x; 1.0032x over previous
import contextlib

import numpy as np
import ml_dtypes

import concourse.bass as bass
import concourse.mybir as mybir
from concourse.bass_utils import run_bass_kernel_spmd

F32 = mybir.dt.float32
BF16 = mybir.dt.bfloat16
AF = mybir.ActivationFunctionType
ALU = mybir.AluOpType
BF = ml_dtypes.bfloat16

NCORE = 8
NT = 2048
HALO = 256
NTH = NT + HALO
C = 128
NCHH = NTH // C
KVH = 8
NH = 32
FC = 32
TA = 256
NTILES_A = NTH // TA
TB = 512
NTB = NT // TB
SCL = 1.0 / float(np.sqrt(np.float32(128)))
G16 = 0.0625


def build_nc(debug=(), upto="E", with_cc=True, a_tiles=None, a_skip=()):
    nc = bass.Bass("TRN2", target_bir_lowering=False)

    def dram(name, shape, dt):
        k = "ExternalOutput" if name in debug else "Internal"
        return nc.dram_tensor(name, shape, dt, kind=k).ap()

    ei = lambda n, s, d: nc.dram_tensor(n, s, d, kind="ExternalInput").ap()
    xT = ei("xT", [4096, NTH], F32)
    cosT = ei("cosT", [128, NTH], F32)
    sinT = ei("sinT", [128, NTH], F32)
    mg_d = ei("mg", [128, 128], F32)
    mA_d = ei("mA", [128, 128], F32)
    mAf_d = ei("mAf", [128, 128], F32)
    mB_d = ei("mB", [128, 128], F32)
    idb_d = ei("idb", [128, 128], BF16)
    idf_d = ei("idf", [128, 128], F32)
    Rt_d = ei("Rt", [128, 128], F32)
    ohv_d = ei("ohv", [128, 8], F32)
    wq_d = ei("wq", [32, 128, 4096], F32)
    wk_d = ei("wk", [8, 128, 4096], F32)
    wv_d = ei("wv", [8, 128, 4096], F32)
    wo_d = ei("wo", [32, 128, 4096], F32)
    wf_d = ei("wf", [32, 128, 4096], F32)
    wa_d = ei("wa", [2, 128, 4096], F32)
    wb_d = ei("wb", [32, 128, 256], F32)
    bias_d = ei("bias_pp", [128, 32], F32)
    cvec_d = ei("cvec", [128, 2], F32)
    svec_d = ei("svec", [128, 2], F32)

    outT = nc.dram_tensor("outT", [4096, NT], F32, kind="ExternalOutput").ap()

    sk_sp = dram("sk_sp", [KVH, 128, NTH], BF16)
    v_sp = dram("v_sp", [NTH, 1024], BF16)
    kt_sp = dram("kt_sp", [KVH, 128, NTH], BF16)
    ecs_sp = dram("ecs_sp", [KVH, 128, NTH], BF16)
    sc_sp = dram("sc_sp", [KVH, 128, NT], BF16)
    qt_sp = dram("qt_sp", [NH, 128, NT], BF16)
    sq_sp = dram("sq_sp", [NH, 128, NT], BF16)
    rq_sp = dram("rq_sp", [NH, 128, 16], F32)
    ot_sp = dram("ot_sp", [NH, 128, NT], BF16)
    s1_sp = dram("s1_sp", [FC, 128, NT], F32)
    x1_sp = dram("x1_sp", [FC, 128, NT], F32)
    s2_sp = dram("s2_sp", [FC, 128, NT], F32)
    ursum = dram("ursum", [128, KVH * 129], F32)
    cc_in = nc.dram_tensor("cc_in", [128, KVH * 129], F32, kind="Internal").ap()
    cc_out = nc.dram_tensor("cc_out", [NCORE * 128, KVH * 129], F32,
                            kind="Internal", addr_space="Shared").ap()

    es = contextlib.ExitStack()
    with es:
        sb = lambda n, s, d: es.enter_context(nc.sbuf_tensor(n, s, d))
        mgs = sb("mgs", [128, 128], F32)
        mAs = sb("mAs", [128, 128], F32)
        mAfs = sb("mAfs", [128, 128], F32)
        mBs = sb("mBs", [128, 128], F32)
        idbs = sb("idbs", [128, 128], BF16)
        idfs = sb("idfs", [128, 128], F32)
        Rts = sb("Rts", [128, 128], F32)
        ohs = sb("ohs", [128, 8], F32)
        ones = sb("ones", [128, 1], F32)
        onesb = sb("onesb", [128, 1], BF16)
        twosb = sb("twosb", [128, 1], BF16)
        twos = sb("twos", [128, 1], F32)
        onesr = sb("onesr", [1, 128], F32)
        zer = sb("zer", [128, TA], F32)
        rk_all = sb("rk_all", [128, KVH * NCHH], F32)
        Ec_all = sb("Ec_all", [128, KVH * NCHH], F32)
        E_run = sb("E_run", [128, KVH], F32)
        statacc = sb("statacc", [128, 32], F32)
        ab1 = sb("ab1", [128, 32], F32)
        ab2 = sb("ab2", [128, 32], F32)
        bias_s = sb("bias_s", [128, 32], F32)
        wvec_s = sb("wvec_s", [128, 2], F32)
        cvec_s = sb("cvec_s", [128, 2], F32)
        svec_s = sb("svec_s", [128, 2], F32)
        eab = contextlib.ExitStack()
        cosw = eab.enter_context(nc.sbuf_tensor("cosw", [128, NTH], F32))
        sinw = eab.enter_context(nc.sbuf_tensor("sinw", [128, NTH], F32))

        pA = es.enter_context(nc.psum_tensor([128, 512], F32))
        pB = es.enter_context(nc.psum_tensor([128, 512], F32))
        pC = es.enter_context(nc.psum_tensor([128, 512], F32))
        pD = es.enter_context(nc.psum_tensor([128, 512], F32))
        pE = es.enter_context(nc.psum_tensor([128, 512], F32))
        pF = es.enter_context(nc.psum_tensor([128, 512], BF16))
        pG = es.enter_context(nc.psum_tensor([128, 129], F32))
        pT = es.enter_context(nc.psum_tensor([128, 128], BF16))

        SX = es.enter_context(nc.semaphore("SX"))
        SW = es.enter_context(nc.semaphore("SW"))
        SO = es.enter_context(nc.semaphore("SO"))
        SS = es.enter_context(nc.semaphore("SS"))
        ST_ = es.enter_context(nc.semaphore("STp"))
        SV = es.enter_context(nc.semaphore("SVp"))
        SC_ = es.enter_context(nc.semaphore("SCp"))
        SG = es.enter_context(nc.semaphore("SGp"))
        PH = [es.enter_context(nc.semaphore(f"PH{i}")) for i in range(5)]

        block = es.enter_context(nc.Block())

        cnt = {"SX": 0, "SW": 0, "SS": 0, "ST": 0, "SV": 0, "SC": 0,
               "SG": 0, "SO": 0}
        sem_of = {"SX": SX, "SW": SW, "SS": SS, "ST": ST_, "SV": SV,
                  "SC": SC_, "SG": SG, "SO": SO}

        def inc(ins, name, n=1):
            ins.then_inc(sem_of[name], n)
            cnt[name] += n
            return cnt[name]

        phase_no = [1]

        def phase_end(eng, idx):
            eng.nop().then_inc(PH[idx], 1)

        def phase_start(eng):
            for i in range(5):
                eng.wait_ge(PH[i], phase_no[0] - 1)

        @block.sync
        def _(sync):
            for dst, src in [(cosw, cosT), (sinw, sinT), (mgs, mg_d),
                             (mAs, mA_d), (mAfs, mAf_d), (mBs, mB_d),
                             (idbs, idb_d), (idfs, idf_d), (Rts, Rt_d),
                             (ohs, ohv_d), (bias_s, bias_d),
                             (cvec_s, cvec_d), (svec_s, svec_d)]:
                inc(sync.dma_start(out=dst[:], in_=src[:]), "SW", 16)
        const_loads = cnt["SW"]

        @block.vector
        def _(vector):
            vector.wait_ge(SW, const_loads)
            inc(vector.memset(ones[:], 1.0), "SV")
            inc(vector.memset(onesb[:], 1.0), "SV")
            inc(vector.memset(twosb[:], 2.0), "SV")
            inc(vector.memset(twos[:], 2.0), "SV")
            inc(vector.memset(onesr[:], 1.0), "SV")
            inc(vector.memset(zer[:], 0.0), "SV")
            inc(vector.memset(E_run[:], 1.0), "SV")
            inc(vector.memset(statacc[:], 0.0), "SV")
            inc(vector.tensor_tensor(wvec_s[:], cvec_s[:], svec_s[:], ALU.mult),
                "SV")
        v_const = cnt["SV"]

        with contextlib.ExitStack() as ea:
            sba = lambda n, s, d: ea.enter_context(nc.sbuf_tensor(n, s, d))
            xa_b = [sba(f"xa{i}", [128, FC * TA], BF16) for i in range(2)]
            wk_sb = sba("wk_sb", [128, 4 * 4096], BF16)
            wv_sb = sba("wv_sb", [128, 4 * 4096], BF16)
            kT_f = [sba(f"kT{i}", [128, TA], F32) for i in range(2)]
            e1_f = sba("e1_f", [128, TA], F32)
            L_f = [sba(f"L{i}", [128, TA], F32) for i in range(2)]
            P_f = [sba(f"P{i}", [128, TA], F32) for i in range(2)]
            kp_f = [sba(f"kp{i}", [128, TA], F32) for i in range(2)]
            ek_f = [sba(f"ek{i}", [128, TA], F32) for i in range(2)]
            t1_f = sba("t1_f", [128, TA], F32)
            m_f = sba("m_f", [128, TA], F32)
            dc_f = [sba(f"dc{i}", [128, 2], F32) for i in range(2)]
            ktil_b = [sba(f"ktil{i}", [128, TA], BF16) for i in range(2)]
            ecs_b = [sba(f"ecsb{i}", [128, TA], BF16) for i in range(2)]
            sk_b = [sba(f"skb{i}", [128, TA], BF16) for i in range(2)]
            vt_b = [sba(f"vtb{i}", [128, TA], BF16) for i in range(2)]
            vtm_b = [sba(f"vtm{i}", [128, 256], BF16) for i in range(2)]
            ktm_b = [sba(f"ktm{i}", [128, 256], BF16) for i in range(2)]
            sct_b = [sba(f"sct{i}", [128, 256], BF16) for i in range(2)]
            tmpS = sba("tmpS", [128, 128], F32)
            S_run = sba("S_run", [128, KVH * 128], F32)
            cc_st = sba("cc_st", [128, KVH * 129], F32)

            @block.vector
            def _(vector):
                vector.wait_ge(SV, cnt["SV"])
                inc(vector.memset(S_run[:], 0.0), "SV")

            HNIT = NTILES_A * 4
            NIT = 2 * HNIT
            if a_tiles is not None:
                NIT = a_tiles * 4
            ST0, SC0, SV0, SS0 = (cnt[k] for k in ("ST", "SC", "SV", "SS"))
            TPI, CPI, VPI = 10, 13, 19

            def split_it(it):
                half, rem = divmod(it, HNIT)
                ti, jj = divmod(rem, 4)
                return half, ti, jj, half * 4 + jj

            def stv(it, k):
                return ST0 + it * TPI + k if it >= 0 else ST0

            def scv(it, k):
                return SC0 + it * CPI + k if it >= 0 else SC0

            def svv(it, k):
                return SV0 + it * VPI + k if it >= 0 else SV0

            _ss_cum = []
            _tot = 0
            for _it in range(NIT):
                _, _ti, _, _ = split_it(_it)
                _tot += 5 if _ti == 0 else 6
                _ss_cum.append(_tot)

            def ssv(it):
                return SS0 + 16 * _ss_cum[it] if it >= 0 else SS0

            def sx_tile(half, ti):
                return (half * 17 + 8 + ti + 1) * 16

            @block.gpsimd
            def _(gpsimd):
                phase_start(gpsimd)
                for half in range(2):
                    if half * HNIT >= NIT:
                        break
                    if half == 1:
                        gpsimd.wait_ge(ST_, stv(HNIT - 1, TPI))
                    for jj in range(4):
                        j = half * 4 + jj
                        if cnt["SX"] > 0:
                            gpsimd.wait_ge(SX, cnt["SX"])
                        inc(gpsimd.dma_start(out=wk_sb[:, jj * 4096:(jj + 1) * 4096],
                                             in_=wk_d[j]), "SX", 16)
                        gpsimd.wait_ge(SX, cnt["SX"])
                        inc(gpsimd.dma_start(out=wv_sb[:, jj * 4096:(jj + 1) * 4096],
                                             in_=wv_d[j]), "SX", 16)
                    for ti in range(NTILES_A):
                        gti = half * NTILES_A + ti
                        if half * HNIT + ti * 4 >= NIT:
                            break
                        if gti >= 2:
                            lt = gti - 2
                            lh, lti = divmod(lt, NTILES_A)
                            gpsimd.wait_ge(ST_, stv(lh * HNIT + lti * 4 + 3, 2))
                        gpsimd.wait_ge(SX, cnt["SX"])
                        inc(gpsimd.dma_start(
                            out=xa_b[gti % 2][:].rearrange("p (fc t) -> p fc t", fc=FC),
                            in_=xT[:, ti * TA:(ti + 1) * TA]
                            .rearrange("(fc p) t -> p fc t", p=128)), "SX", 16)

            @block.tensor
            def _(tensor):
                phase_start(tensor)
                tensor.wait_ge(SV, v_const)
                for it in range(NIT):
                    half, ti, jj, j = split_it(it)
                    par = it % 2
                    gti = half * NTILES_A + ti
                    if it >= 2:
                        tensor.wait_ge(SC_, scv(it - 2, CPI))
                        tensor.wait_ge(SV, svv(it - 2, VPI))
                    if jj == 0:
                        tensor.wait_ge(SX, sx_tile(half, ti))
                    xa = xa_b[gti % 2]
                    if it >= 1:
                        tensor.wait_ge(SC_, scv(it - 1, 1))
                    for fc in range(FC):
                        ins = tensor.matmul(pA[:, 0:TA],
                                            wk_sb[:, jj * 4096 + fc * 128: jj * 4096 + fc * 128 + 128],
                                            xa[:, fc * TA:(fc + 1) * TA],
                                            start=(fc == 0), stop=(fc == FC - 1))
                    inc(ins, "ST")
                    if it >= 1:
                        tensor.wait_ge(SC_, scv(it - 1, 2))
                    for fc in range(FC):
                        ins = tensor.matmul(pB[:, 0:TA],
                                            wv_sb[:, jj * 4096 + fc * 128: jj * 4096 + fc * 128 + 128],
                                            xa[:, fc * TA:(fc + 1) * TA],
                                            start=(fc == 0), stop=(fc == FC - 1))
                    inc(ins, "ST")
                    tensor.wait_ge(SC_, scv(it, 1))
                    if it >= 1:
                        tensor.wait_ge(SV, svv(it - 1, 6))
                    inc(tensor.matmul(pC[:, 0:TA], Rts[:], kT_f[par][:],
                                      start=True, stop=True), "ST")
                    tensor.wait_ge(SC_, scv(it, 5))
                    if it >= 1:
                        tensor.wait_ge(SV, svv(it - 1, 4))
                    tensor.matmul(pG[:, 0:1], ek_f[par][:, 0:128], ones[:],
                                  start=True, stop=False)
                    inc(tensor.matmul(pG[:, 1:2], ek_f[par][:, 128:256], ones[:],
                                      start=False, stop=True), "ST")
                    tensor.wait_ge(SC_, scv(it, 2))
                    if it >= 1:
                        tensor.wait_ge(SC_, scv(it - 1, 13))
                    inc(tensor.matmul(pT[:], vt_b[par][:, 0:128], idbs[:],
                                      is_transpose=True, start=True, stop=True),
                        "ST")
                    tensor.wait_ge(SC_, scv(it, 7))
                    inc(tensor.matmul(pT[:], vt_b[par][:, 128:256], idbs[:],
                                      is_transpose=True, start=True, stop=True),
                        "ST")
                    tensor.wait_ge(SC_, scv(it, 8))
                    tensor.wait_ge(SC_, scv(it, 9))
                    inc(tensor.matmul(pT[:], ktil_b[par][:, 0:128], idbs[:],
                                      is_transpose=True, start=True, stop=True),
                        "ST")
                    tensor.wait_ge(SC_, scv(it, 12))
                    inc(tensor.matmul(pT[:], ktil_b[par][:, 128:256], idbs[:],
                                      is_transpose=True, start=True, stop=True),
                        "ST")
                    tensor.wait_ge(SC_, scv(it, 13))
                    if it >= 1:
                        tensor.wait_ge(SV, svv(it - 1, 11))
                    inc(tensor.matmul(pD[:, 0:128], ktm_b[par][:, 0:128],
                                      vtm_b[par][:, 0:128], start=True, stop=True),
                        "ST")
                    if it >= 1:
                        tensor.wait_ge(SV, svv(it - 1, 16))
                    inc(tensor.matmul(pE[:, 0:128], ktm_b[par][:, 128:256],
                                      vtm_b[par][:, 128:256], start=True, stop=True),
                        "ST")
                phase_end(tensor, 0)

            @block.scalar
            def _(scalar):
                phase_start(scalar)

                def sop(fn, name="SC"):
                    scalar.wait_ge(SC_, cnt["SC"])
                    inc(fn(), name)

                for it in range(NIT):
                    half, ti, jj, j = split_it(it)
                    par = it % 2
                    cA = j * NCHH + 2 * ti
                    if it >= 2:
                        scalar.wait_ge(ST_, stv(it - 2, TPI))
                        scalar.wait_ge(SV, svv(it - 2, VPI))
                        scalar.wait_ge(SS, ssv(it - 2))
                    scalar.wait_ge(ST_, stv(it, 1))
                    sop(lambda: scalar.copy(kT_f[par][:], pA[:, 0:TA]))
                    scalar.wait_ge(ST_, stv(it, 2))
                    sop(lambda: scalar.copy(vt_b[par][:], pB[:, 0:TA]))
                    sop(lambda: scalar.activation(e1_f[:], kT_f[par][:], AF.Exp,
                                                  scale=-1.0))
                    sop(lambda: scalar.activation(L_f[par][:], e1_f[:], AF.Ln,
                                                  bias=1.0, scale=1.0))
                    sop(lambda: scalar.activation(ek_f[par][:], kT_f[par][:],
                                                  AF.Exp))
                    scalar.wait_ge(SV, svv(it, 2))
                    sop(lambda: scalar.activation(dc_f[par][:, 0:1],
                                                  P_f[par][:, 127:128],
                                                  AF.Exp, scale=-G16))
                    scalar.wait_ge(ST_, stv(it, 5))
                    sop(lambda: scalar.copy(vtm_b[par][:, 0:128], pT[:]))
                    scalar.wait_ge(ST_, stv(it, 6))
                    sop(lambda: scalar.copy(vtm_b[par][:, 128:256], pT[:]))
                    scalar.wait_ge(SV, svv(it, 3))
                    sop(lambda: scalar.activation(ktil_b[par][:], kp_f[par][:],
                                                  AF.Exp))
                    sop(lambda: scalar.activation(ecs_b[par][:], P_f[par][:],
                                                  AF.Exp, scale=-G16))
                    sop(lambda: scalar.activation(dc_f[par][:, 1:2],
                                                  P_f[par][:, 255:256],
                                                  AF.Exp, scale=-G16))
                    scalar.wait_ge(ST_, stv(it, 7))
                    scalar.wait_ge(SV, svv(it, 4))
                    sop(lambda: scalar.activation(ktm_b[par][:, 0:128], pT[:],
                                                  AF.Copy,
                                                  scale=rk_all[:, cA:cA + 1]))
                    scalar.wait_ge(ST_, stv(it, 8))
                    sop(lambda: scalar.activation(ktm_b[par][:, 128:256], pT[:],
                                                  AF.Copy,
                                                  scale=rk_all[:, cA + 1:cA + 2]))
                phase_end(scalar, 2)

            @block.vector
            def _(vector):
                phase_start(vector)

                def vop(fn):
                    vector.wait_ge(SV, cnt["SV"])
                    inc(fn(), "SV")

                for it in range(NIT):
                    half, ti, jj, j = split_it(it)
                    par = it % 2
                    cA = j * NCHH + 2 * ti
                    jslice = S_run[:, j * 128:(j + 1) * 128]
                    if it >= 2:
                        vector.wait_ge(SC_, scv(it - 2, CPI))
                        vector.wait_ge(ST_, stv(it - 2, TPI))
                        vector.wait_ge(SS, ssv(it - 2))
                    vector.wait_ge(SC_, scv(it, 4))
                    vop(lambda: vector.tensor_tensor_scan(
                        P_f[par][:, 0:128], L_f[par][:, 0:128],
                        zer[:, 0:128], 0.0, ALU.add, ALU.add))
                    vop(lambda: vector.tensor_tensor_scan(
                        P_f[par][:, 128:256], L_f[par][:, 128:256],
                        zer[:, 0:128], 0.0, ALU.add, ALU.add))
                    vop(lambda: vector.scalar_tensor_tensor(
                        kp_f[par][:], P_f[par][:], G16, kT_f[par][:],
                        ALU.mult, ALU.add))
                    vector.wait_ge(ST_, stv(it, 4))
                    vop(lambda: vector.reciprocal(rk_all[:, cA:cA + 2],
                                                  pG[:, 0:2]))
                    vop(lambda: vector.tensor_tensor(
                        t1_f[:], kT_f[par][:], cosw[:, ti * TA:(ti + 1) * TA],
                        ALU.mult))
                    vector.wait_ge(ST_, stv(it, 3))
                    vop(lambda: vector.tensor_tensor(
                        m_f[:], pC[:, 0:TA], sinw[:, ti * TA:(ti + 1) * TA],
                        ALU.mult))
                    vop(lambda: vector.tensor_tensor(sk_b[par][:], t1_f[:],
                                                     m_f[:], ALU.add))
                    vop(lambda: vector.tensor_copy(Ec_all[:, cA:cA + 1],
                                                   E_run[:, j:j + 1]))
                    vector.wait_ge(SC_, scv(it, 6))
                    vop(lambda: vector.tensor_scalar(
                        E_run[:, j:j + 1], E_run[:, j:j + 1], dc_f[par][:, 0:1],
                        None, ALU.mult))
                    vop(lambda: vector.tensor_copy(sct_b[par][:, 0:128], jslice))
                    vector.wait_ge(ST_, stv(it, 9))
                    vop(lambda: vector.tensor_tensor(tmpS[:], jslice,
                                                     pD[:, 0:128], ALU.add))
                    vop(lambda: vector.tensor_scalar(jslice, tmpS[:],
                                                     dc_f[par][:, 0:1], None,
                                                     ALU.mult))
                    vop(lambda: vector.tensor_copy(Ec_all[:, cA + 1:cA + 2],
                                                   E_run[:, j:j + 1]))
                    vector.wait_ge(SC_, scv(it, 11))
                    vop(lambda: vector.tensor_scalar(
                        E_run[:, j:j + 1], E_run[:, j:j + 1], dc_f[par][:, 1:2],
                        None, ALU.mult))
                    vop(lambda: vector.tensor_copy(sct_b[par][:, 128:256],
                                                   jslice))
                    vector.wait_ge(ST_, stv(it, 10))
                    vop(lambda: vector.tensor_tensor(tmpS[:], jslice,
                                                     pE[:, 0:128], ALU.add))
                    vop(lambda: vector.tensor_scalar(jslice, tmpS[:],
                                                     dc_f[par][:, 1:2], None,
                                                     ALU.mult))
                    if it % HNIT == 3:
                        h0 = (it // HNIT) * 4
                        vop(lambda: vector.memset(
                            S_run[:, h0 * 128:(h0 + 4) * 128], 0.0))
                        vop(lambda: vector.memset(E_run[:, h0:h0 + 4], 1.0))
                    else:
                        inc(vector.nop(), "SV")
                        inc(vector.nop(), "SV")
                for j in range(KVH):
                    vector.wait_ge(SV, cnt["SV"])
                    inc(vector.tensor_copy(cc_st[:, j * 129:j * 129 + 128],
                                           S_run[:, j * 128:(j + 1) * 128]), "SV")
                    vector.wait_ge(SV, cnt["SV"])
                    inc(vector.tensor_copy(
                        cc_st[:, j * 129 + 128:j * 129 + 129], E_run[:, j:j + 1]),
                        "SV")
                phase_end(vector, 1)
            SV_CC = cnt["SV"]

            @block.sync
            def _(sync):
                phase_start(sync)
                for it in range(NIT):
                    half, ti, jj, j = split_it(it)
                    par = it % 2
                    t0 = ti * TA
                    sync.wait_ge(SC_, scv(it, 9))
                    sync.wait_ge(SS, cnt["SS"])
                    inc(sync.dma_start(out=kt_sp[j][:, t0:t0 + TA],
                                       in_=ktil_b[par][:]), "SS", 16)
                    sync.wait_ge(SC_, scv(it, 10))
                    sync.wait_ge(SS, cnt["SS"])
                    inc(sync.dma_start(out=ecs_sp[j][:, t0:t0 + TA],
                                       in_=ecs_b[par][:]), "SS", 16)
                    sync.wait_ge(SC_, scv(it, 8))
                    sync.wait_ge(SS, cnt["SS"])
                    inc(sync.dma_start(out=v_sp[t0:t0 + 128, j * 128:(j + 1) * 128],
                                       in_=vtm_b[par][:, 0:128]), "SS", 16)
                    sync.wait_ge(SS, cnt["SS"])
                    inc(sync.dma_start(out=v_sp[t0 + 128:t0 + 256,
                                               j * 128:(j + 1) * 128],
                                       in_=vtm_b[par][:, 128:256]), "SS", 16)
                    sync.wait_ge(SV, svv(it, 7))
                    sync.wait_ge(SS, cnt["SS"])
                    inc(sync.dma_start(out=sk_sp[j][:, t0:t0 + TA],
                                       in_=sk_b[par][:]), "SS", 16)
                    if ti >= 1:
                        sync.wait_ge(SV, svv(it, 15))
                        sync.wait_ge(SS, cnt["SS"])
                        inc(sync.dma_start(
                            out=sc_sp[j][:, (ti - 1) * TA:ti * TA],
                            in_=sct_b[par][:]), "SS", 16)
                sync.wait_ge(SV, SV_CC)
                sync.wait_ge(SS, cnt["SS"])
                inc(sync.dma_start(out=cc_in[:], in_=cc_st[:]), "SS", 16)
                sync.wait_ge(SS, cnt["SS"])
                inc(sync.dma_start(out=ursum[:], in_=cc_st[:]), "SS", 16)
                phase_end(sync, 4)
            A_SS_END = cnt["SS"]

            @block.gpsimd
            def _(gpsimd):
                gpsimd.wait_ge(SS, A_SS_END)
                if with_cc:
                    inc(gpsimd.collective_compute(
                        "AllGather", ALU.bypass,
                        replica_groups=[list(range(NCORE))],
                        ins=[cc_in[:]],
                        outs=[cc_out[:]],
                    ), "SG")
                else:
                    inc(gpsimd.nop(), "SG")
                phase_end(gpsimd, 3)
        AG_DONE = cnt["SG"]
        phase_no[0] += 1

        if upto == "A":
            @block.sync
            def _(sync):
                phase_start(sync)
                n0 = cnt["SS"]
                for fc in range(FC):
                    for tt in range(NT // 256):
                        inc(sync.dma_start(
                            out=outT[fc * 128:(fc + 1) * 128,
                                     tt * 256:(tt + 1) * 256],
                            in_=zer[:, 0:256]), "SS", 16)
                sync.wait_ge(SS, cnt["SS"])
                sync.wait_ge(SG, AG_DONE)
            return nc

        def _tail_zeros():
            @block.sync
            def _(sync):
                phase_start(sync)
                for fc in range(FC):
                    for tt in range(NT // 256):
                        sync.wait_ge(SS, cnt["SS"])
                        inc(sync.dma_start(
                            out=outT[fc * 128:(fc + 1) * 128,
                                     tt * 256:(tt + 1) * 256],
                            in_=zer[:, 0:256]), "SS", 16)
                sync.wait_ge(SS, cnt["SS"])
                sync.wait_ge(SG, AG_DONE)

        with contextlib.ExitStack() as eb:
            sbb = lambda n, s, d: eb.enter_context(nc.sbuf_tensor(n, s, d))
            x_b1 = sbb("x_b1", [128, FC * NT], BF16)
            wq_sb = [sbb(f"wqs{i}", [128, 4096], BF16) for i in range(2)]
            ecs_ld = [sbb(f"ecsl{i}", [128, NT], BF16) for i in range(2)]
            qT_q = sbb("qT_q", [128, NT], F32)
            eq_f = sbb("eq_f", [128, NT], BF16)
            t1q = sbb("t1q", [128, TB], F32)
            mq = sbb("mq", [128, TB], F32)
            qt_st = sbb("qt_st", [128, NT], BF16)
            sq_st = sbb("sq_st", [128, NT], BF16)
            rq_st = sbb("rq_st", [128, 16], F32)

            T0, C0, V0, S0 = (cnt[k] for k in ("ST", "SC", "SV", "SS"))
            TPIq, CPIq, VPIq = 9, 5, 17

            def stq(h, k):
                return T0 + h * TPIq + k if h >= 0 else T0

            def scq(h, k):
                return C0 + h * CPIq + k if h >= 0 else C0

            def svq(h, k):
                return V0 + h * VPIq + k if h >= 0 else V0

            ss_after_load = {}
            ss_after_spill = {}


            @block.sync
            def _(sync):
                phase_start(sync)
                for h in range(NH):
                    j = h // 4
                    if h % 4 == 0:
                        if j >= 2:
                            sync.wait_ge(SV, svq(4 * (j - 2) + 3, 16))
                        sync.wait_ge(SS, cnt["SS"])
                        inc(sync.dma_start(out=ecs_ld[j % 2][:],
                                           in_=ecs_sp[j][:, HALO:NTH]), "SS", 16)
                        ss_after_load[j] = cnt["SS"]
                    sync.wait_ge(SV, svq(h, 16))
                    sync.wait_ge(SS, cnt["SS"])
                    inc(sync.dma_start(out=qt_sp[h][:], in_=qt_st[:]), "SS", 16)
                    sync.wait_ge(SV, svq(h, 12))
                    sync.wait_ge(SS, cnt["SS"])
                    inc(sync.dma_start(out=sq_sp[h][:], in_=sq_st[:]), "SS", 16)
                    sync.wait_ge(SV, svq(h, 17))
                    sync.wait_ge(SS, cnt["SS"])
                    inc(sync.dma_start(out=rq_sp[h][:], in_=rq_st[:]), "SS", 16)
                    ss_after_spill[h] = cnt["SS"]
                phase_end(sync, 4)

            @block.gpsimd
            def _(gpsimd):
                phase_start(gpsimd)
                gpsimd.wait_ge(SS, S0)
                for i in range(8):
                    gpsimd.wait_ge(SX, cnt["SX"])
                    inc(gpsimd.dma_start(
                        out=x_b1[:, i * 4 * NT:(i + 1) * 4 * NT]
                        .rearrange("p (fc t) -> p fc t", fc=4),
                        in_=xT[i * 4 * 128:(i + 1) * 4 * 128, HALO:NTH]
                        .rearrange("(fc p) t -> p fc t", p=128)), "SX", 16)
                for h in range(NH):
                    if h >= 2:
                        gpsimd.wait_ge(ST_, stq(h - 2, 4))
                    gpsimd.wait_ge(SX, cnt["SX"])
                    inc(gpsimd.dma_start(out=wq_sb[h % 2][:], in_=wq_d[h]),
                        "SX", 16)
                phase_end(gpsimd, 3)
            SXq0 = T0 * 0 + cnt["SX"] - NH * 16

            @block.tensor
            def _(tensor):
                phase_start(tensor)
                for h in range(NH):
                    projb = [pA, pB, pC, pD]
                    tensor.wait_ge(SX, SXq0 + (h + 1) * 16)
                    if h >= 1:
                        tensor.wait_ge(SV, svq(h - 1, 17))
                        tensor.wait_ge(SC_, scq(h - 1, 5))
                    for tt in range(NTB):
                        for fc in range(FC):
                            ins = tensor.matmul(
                                projb[tt][:, 0:TB],
                                wq_sb[h % 2][:, fc * 128:(fc + 1) * 128],
                                x_b1[:, fc * NT + tt * TB: fc * NT + (tt + 1) * TB],
                                start=(fc == 0), stop=(fc == FC - 1))
                        inc(ins, "ST")
                    for tt in range(NTB):
                        tensor.wait_ge(SC_, scq(h, tt + 1))
                        if tt == 0:
                            if h >= 1:
                                tensor.wait_ge(SV, svq(h - 1, 11))
                        else:
                            tensor.wait_ge(SV, svq(h, 3 * (tt - 1) + 2))
                        inc(tensor.matmul(pE[:, 0:TB], Rts[:],
                                          qT_q[:, tt * TB:(tt + 1) * TB],
                                          start=True, stop=True), "ST")
                    tensor.wait_ge(SC_, scq(h, 5))
                    if h >= 1:
                        tensor.wait_ge(SV, svq(h - 1, 17))
                    for sub in range(16):
                        ins = tensor.matmul(pG[:, sub:sub + 1],
                                            eq_f[:, sub * 128:(sub + 1) * 128],
                                            twosb[:],
                                            start=(sub == 0), stop=(sub == 15))
                    inc(ins, "ST")
                phase_end(tensor, 0)

            @block.scalar
            def _(scalar):
                phase_start(scalar)

                def sop(fn):
                    scalar.wait_ge(SC_, cnt["SC"])
                    inc(fn(), "SC")

                for h in range(NH):
                    projb = [pA, pB, pC, pD]
                    if h >= 1:
                        scalar.wait_ge(SV, svq(h - 1, 17))
                        scalar.wait_ge(ST_, stq(h - 1, 9))
                    for tt in range(NTB):
                        scalar.wait_ge(ST_, stq(h, tt + 1))
                        sop(lambda tt=tt: scalar.copy(
                            qT_q[:, tt * TB:(tt + 1) * TB], projb[tt][:, 0:TB]))
                    sop(lambda: scalar.activation(eq_f[:], qT_q[:], AF.Exp))
                phase_end(scalar, 2)

            @block.vector
            def _(vector):
                phase_start(vector)

                def vop(fn):
                    vector.wait_ge(SV, cnt["SV"])
                    inc(fn(), "SV")

                for h in range(NH):
                    j = h // 4
                    if h >= 1:
                        vector.wait_ge(SS, ss_after_spill[h - 1])
                        vector.wait_ge(ST_, stq(h - 1, 9))
                    for tt in range(NTB):
                        vector.wait_ge(SC_, scq(h, tt + 1))
                        vop(lambda tt=tt: vector.tensor_tensor(
                            t1q[:], qT_q[:, tt * TB:(tt + 1) * TB],
                            cosw[:, HALO + tt * TB:HALO + (tt + 1) * TB],
                            ALU.mult))
                        vector.wait_ge(ST_, stq(h, 4 + tt + 1))
                        vop(lambda tt=tt: vector.tensor_tensor(
                            mq[:], pE[:, 0:TB],
                            sinw[:, HALO + tt * TB:HALO + (tt + 1) * TB],
                            ALU.mult))
                        vop(lambda tt=tt: vector.tensor_tensor(
                            sq_st[:, tt * TB:(tt + 1) * TB], t1q[:], mq[:],
                            ALU.add))
                    vector.wait_ge(SC_, scq(h, 5))
                    vector.wait_ge(SS, ss_after_load[j])
                    for tt in range(NTB):
                        vop(lambda tt=tt: vector.tensor_tensor(
                            qt_st[:, tt * TB:(tt + 1) * TB],
                            eq_f[:, tt * TB:(tt + 1) * TB],
                            ecs_ld[j % 2][:, tt * TB:(tt + 1) * TB],
                            ALU.mult))
                    vector.wait_ge(ST_, stq(h, 9))
                    vop(lambda: vector.reciprocal(rq_st[:], pG[:, 0:16]))
                phase_end(vector, 1)

        phase_no[0] += 1
        eab.close()
        if upto == "B1":
            _tail_zeros()
            return nc

        ebc = contextlib.ExitStack()
        oTr = ebc.enter_context(nc.sbuf_tensor("oTr", [128, NH * NT], BF16))
        with contextlib.ExitStack() as eb:
            sbb = lambda n, s, d: eb.enter_context(nc.sbuf_tensor(n, s, d))
            ag_sb = sbb("ag_sb", [128, KVH * 129], F32)
            ktil_w = [sbb(f"ktw{i}", [128, NT], BF16) for i in range(2)]
            sk_w = [sbb(f"skw{i}", [128, NT + 128], BF16) for i in range(2)]
            v_w = [sbb(f"vw{i}", [128, 17 * 129], BF16) for i in range(2)]
            sc_w = [sbb(f"scw{i}", [128, NT], BF16) for i in range(2)]
            qt_h = [sbb(f"qth{i}", [128, NT], BF16) for i in range(2)]
            sq_h = [sbb(f"sqh{i}", [128, NT], BF16) for i in range(2)]
            rq_h = [sbb(f"rqh{i}", [128, 16], F32) for i in range(2)]
            oh_acc = [sbb(f"oha{i}", [128, NT], BF16) for i in range(2)]
            Pb = [sbb(f"Pb{i}", [128, 128], F32) for i in range(2)]
            Sin_b = sbb("Sin_b", [128, 128], F32)
            egt_f = [sbb(f"egt{i}", [128, 128], F32) for i in range(2)]
            eAt_f = [sbb(f"eAt{i}", [128, 128], F32) for i in range(2)]
            eBt_f = [sbb(f"eBt{i}", [128, 128], F32) for i in range(2)]
            eg_sb = [sbb(f"egb{i}", [128, 128], BF16) for i in range(2)]
            eA_sb = [sbb(f"eAb{i}", [128, 128], BF16) for i in range(2)]
            eB_sb = [sbb(f"eBb{i}", [128, 128], BF16) for i in range(2)]
            tmp_f = [sbb(f"tmpf{i}", [128, 128], F32) for i in range(2)]
            r_swa_col = [sbb(f"rsw{i}", [128, 1], F32) for i in range(2)]

            T0, C0, V0, S0 = (cnt[k] for k in ("ST", "SC", "SV", "SS"))
            TPIm, CPIm, VPIm = 5, 4, 5
            PVj = 34

            def stm(m, k):
                return T0 + m * TPIm + k if m >= 0 else T0

            def scm(m, k):
                return C0 + m * CPIm + k if m >= 0 else C0

            def svm(m, k):
                j = m // 64
                return V0 + (j + 1) * PVj + m * VPIm + k if m >= 0 else V0

            def svm_prol(j):
                return V0 + (j + 1) * PVj + j * 64 * VPIm

            ss_jload = {}
            ss_hload = {}
            ss_ohspill = {}
            ss_tr_all = [0]

            def oh_to_oTr(sync, h):
                sync.wait_ge(SV, svm((h + 1) * 16 - 1, 5))
                for c in range(16):
                    inc(sync.dma_start(
                        out=oTr[:, h * NT + c * 128:h * NT + (c + 1) * 128],
                        in_=oh_acc[h % 2][:, c * 128:(c + 1) * 128],
                        transpose=True), "SS", 16)
                if "ot_sp" in debug:
                    sync.wait_ge(SS, cnt["SS"])
                    inc(sync.dma_start(out=ot_sp[h][:], in_=oh_acc[h % 2][:]),
                        "SS", 16)
                ss_ohspill[h] = cnt["SS"]

            @block.sync
            def _(sync):
                phase_start(sync)
                sync.wait_ge(SG, AG_DONE)
                sync.wait_ge(SS, cnt["SS"])
                for j in range(KVH):
                    jp = j % 2
                    if j >= 2:
                        sync.wait_ge(ST_, stm((j - 1) * 64 - 1, TPIm))
                        sync.wait_ge(SV, svm((j - 1) * 64 - 1, VPIm))
                    sync.wait_ge(SS, cnt["SS"])
                    inc(sync.dma_start(
                        out=ag_sb[:].rearrange("p (a c) -> p a c", a=KVH),
                        in_=cc_out[:, j * 129:(j + 1) * 129]
                        .rearrange("(a p) c -> p a c", p=128)), "SS", 16)
                    for dst, srcap in [
                            (ktil_w[jp][:], kt_sp[j][:, HALO:NTH]),
                            (sk_w[jp][:], sk_sp[j][:, HALO - 128:NTH]),
                            (sc_w[jp][:], sc_sp[j][:])]:
                        sync.wait_ge(SS, cnt["SS"])
                        inc(sync.dma_start(out=dst, in_=srcap), "SS", 16)
                    for k in range(17):
                        sync.wait_ge(SS, cnt["SS"])
                        inc(sync.dma_start(
                            out=v_w[jp][:, k * 129:k * 129 + 128],
                            in_=v_sp[128 + k * 128:256 + k * 128,
                                     j * 128:(j + 1) * 128]), "SS", 16)
                    ss_jload[j] = cnt["SS"]
                    for hh in range(4):
                        h = j * 4 + hh
                        hp = h % 2
                        if h >= 2:
                            sync.wait_ge(ST_, stm((h - 1) * 16 - 1, TPIm))
                        for dst, srcap in [(qt_h[hp][:], qt_sp[h][:]),
                                           (sq_h[hp][:], sq_sp[h][:]),
                                           (rq_h[hp][:], rq_sp[h][:])]:
                            sync.wait_ge(SS, cnt["SS"])
                            inc(sync.dma_start(out=dst, in_=srcap), "SS", 16)
                        ss_hload[h] = cnt["SS"]
                        if h >= 1:
                            oh_to_oTr(sync, h - 1)
                oh_to_oTr(sync, NH - 1)
                ss_tr_all[0] = cnt["SS"]
                phase_end(sync, 4)

            @block.gpsimd
            def _(gpsimd):
                phase_start(gpsimd)
                phase_end(gpsimd, 3)

            @block.tensor
            def _(tensor):
                phase_start(tensor)
                for m in range(NH * 16):
                    h, c = divmod(m, 16)
                    j = h // 4
                    hp, p2 = h % 2, m % 2
                    scg = pA if p2 == 0 else pB
                    scA = pC if p2 == 0 else pD
                    if c == 0:
                        tensor.wait_ge(SS, ss_hload[h])
                        tensor.wait_ge(SS, ss_jload[j])
                        tensor.wait_ge(SV, svm_prol(j))
                    if m >= 2:
                        tensor.wait_ge(SC_, scm(m - 2, 1))
                        tensor.wait_ge(SC_, scm(m - 2, 2))
                    inc(tensor.matmul(scg[:, 0:128], ktil_w[j % 2][:, c * 128:(c + 1) * 128],
                                      qt_h[hp][:, c * 128:(c + 1) * 128],
                                      start=True, stop=True), "ST")
                    inc(tensor.matmul(scA[:, 0:128], sk_w[j % 2][:, c * 128:(c + 1) * 128],
                                      sq_h[hp][:, c * 128:(c + 1) * 128],
                                      start=True, stop=True), "ST")
                    if m >= 1:
                        tensor.wait_ge(SC_, scm(m - 1, 4))
                        tensor.wait_ge(SV, svm(m - 1, 4))
                    inc(tensor.matmul(pE[:, 0:128],
                                      sk_w[j % 2][:, 128 + c * 128:256 + c * 128],
                                      sq_h[hp][:, c * 128:(c + 1) * 128],
                                      start=True, stop=True), "ST")
                    tensor.wait_ge(SV, svm(m, 1))
                    if m >= 1:
                        tensor.wait_ge(SV, svm(m - 1, 5))
                    tensor.matmul(pG[:, 0:128], eg_sb[p2][:],
                                  v_w[j % 2][:, (c + 1) * 129:(c + 1) * 129 + 128],
                                  start=True, stop=False)
                    inc(tensor.matmul(pG[:, 0:128], qt_h[hp][:, c * 128:(c + 1) * 128],
                                      sc_w[j % 2][:, c * 128:(c + 1) * 128],
                                      start=False, stop=True), "ST")
                    tensor.wait_ge(SV, svm(m, 3))
                    tensor.wait_ge(SC_, scm(m, 3))
                    tensor.matmul(pE[:, 0:129], eA_sb[p2][:],
                                  v_w[j % 2][:, c * 129:c * 129 + 129],
                                  start=True, stop=False)
                    inc(tensor.matmul(pE[:, 0:129], eB_sb[p2][:],
                                      v_w[j % 2][:, (c + 1) * 129:(c + 2) * 129],
                                      start=False, stop=True), "ST")
                phase_end(tensor, 0)

            @block.scalar
            def _(scalar):
                phase_start(scalar)

                def sop(fn):
                    scalar.wait_ge(SC_, cnt["SC"])
                    inc(fn(), "SC")

                for m in range(NH * 16):
                    h, c = divmod(m, 16)
                    j = h // 4
                    p2 = m % 2
                    scg = pA if p2 == 0 else pB
                    scA = pC if p2 == 0 else pD
                    cR = j * NCHH + 2 + c
                    if m >= 2:
                        scalar.wait_ge(SV, svm(m - 2, VPIm))
                    else:
                        scalar.wait_ge(SV, V0)
                    scalar.wait_ge(ST_, stm(m, 1))
                    sop(lambda: scalar.activation(egt_f[p2][:], scg[:, 0:128],
                                                  AF.Copy,
                                                  scale=rk_all[:, cR:cR + 1]))
                    scalar.wait_ge(ST_, stm(m, 2))
                    sop(lambda: scalar.activation(eAt_f[p2][:], scA[:, 0:128],
                                                  AF.Exp, scale=SCL))
                    scalar.wait_ge(ST_, stm(m, 3))
                    sop(lambda: scalar.activation(eBt_f[p2][:], pE[:, 0:128],
                                                  AF.Exp, scale=SCL))
                    scalar.wait_ge(ST_, stm(m, 5))
                    scalar.wait_ge(SV, svm(m, 4))
                    sop(lambda: scalar.activation(tmp_f[p2][:], pE[:, 0:128],
                                                  AF.Copy,
                                                  scale=r_swa_col[p2][:]))
                phase_end(scalar, 2)

            @block.vector
            def _(vector):
                phase_start(vector)

                def vop(fn):
                    vector.wait_ge(SV, cnt["SV"])
                    inc(fn(), "SV")

                for j in range(KVH):
                    jp = j % 2
                    vector.wait_ge(SS, ss_jload[j])
                    nops = 0
                    vop(lambda: vector.memset(
                        v_w[jp][:].rearrange("p (k c) -> p k c", c=129)[:, :, 128:129],
                        2.0)); nops += 1
                    vop(lambda: vector.memset(Sin_b[:], 0.0)); nops += 1
                    for i in range(NCORE):
                        pcur, pprev = Pb[i % 2], Pb[(i + 1) % 2]
                        if i in (0, 4):
                            vop(lambda pcur=pcur: vector.memset(pcur[:], 0.0))
                        else:
                            off = (i - 1) * 129
                            vop(lambda pcur=pcur, pprev=pprev, off=off:
                                vector.scalar_tensor_tensor(
                                    pcur[:], pprev[:],
                                    ag_sb[:, off + 128:off + 129],
                                    ag_sb[:, off:off + 128],
                                    ALU.mult, ALU.add))
                        nops += 1
                        vop(lambda pcur=pcur, i=i: vector.scalar_tensor_tensor(
                            Sin_b[:], pcur[:], ohs[:, i:i + 1], Sin_b[:],
                            ALU.mult, ALU.add))
                        nops += 1
                    for c in range(16):
                        cR = j * NCHH + 2 + c
                        vop(lambda c=c, cR=cR: vector.scalar_tensor_tensor(
                            sc_w[jp][:, c * 128:(c + 1) * 128], Sin_b[:],
                            Ec_all[:, cR:cR + 1],
                            sc_w[jp][:, c * 128:(c + 1) * 128],
                            ALU.mult, ALU.add))
                        nops += 1
                    assert nops == PVj, nops
                    for mm_ in range(64):
                        m = j * 64 + mm_
                        h, c = divmod(m, 16)
                        p2 = m % 2
                        if m >= 2:
                            vector.wait_ge(ST_, stm(m - 2, TPIm))
                            vector.wait_ge(SC_, scm(m - 2, CPIm))
                        if c == 0:
                            vector.wait_ge(SS, ss_ohspill.get(h - 2, 0) or 0)
                        vector.wait_ge(SC_, scm(m, 1))
                        vop(lambda p2=p2: vector.tensor_tensor(
                            eg_sb[p2][:], egt_f[p2][:], mgs[:], ALU.mult))
                        vector.wait_ge(SC_, scm(m, 2))
                        mA_use = mAfs if c == 0 else mAs
                        vop(lambda p2=p2, mA_use=mA_use: vector.tensor_tensor(
                            eA_sb[p2][:], eAt_f[p2][:], mA_use[:], ALU.mult))
                        vector.wait_ge(SC_, scm(m, 3))
                        vop(lambda p2=p2: vector.tensor_tensor(
                            eB_sb[p2][:], eBt_f[p2][:], mBs[:], ALU.mult))
                        vector.wait_ge(ST_, stm(m, 5))
                        vop(lambda p2=p2: vector.reciprocal(
                            r_swa_col[p2][:], pE[:, 128:129]))
                        vector.wait_ge(ST_, stm(m, 4))
                        vector.wait_ge(SC_, scm(m, 4))
                        hp = h % 2
                        vop(lambda p2=p2, hp=hp, h=h, c=c: vector.scalar_tensor_tensor(
                            oh_acc[hp][:, c * 128:(c + 1) * 128], pG[:, 0:128],
                            rq_h[hp][:, c:c + 1], tmp_f[p2][:],
                            ALU.mult, ALU.add))
                phase_end(vector, 1)

        phase_no[0] += 1
        if upto == "B2":
            _tail_zeros()
            return nc

        with contextlib.ExitStack() as ec:
            sbc = lambda n, s, d: ec.enter_context(nc.sbuf_tensor(n, s, d))
            xw = [sbc(f"xw{i}", [128, NT], F32) for i in range(2)]
            s1_sb = [sbc(f"s1sb{i}", [128, NT], F32) for i in range(2)]
            s1sq = [sbc(f"s1sq{i}", [128, TB], BF16) for i in range(4)]
            s1bb = [sbc(f"s1bb{i}", [128, TB], BF16) for i in range(4)]
            wo_sb = [sbc(f"wos{i}", [128, 4096], BF16) for i in range(2)]
            m16 = sbc("m16", [128, 16], F32)
            e16 = sbc("e16", [128, 16], F32)
            t16 = sbc("t16", [128, 16], F32)
            v16 = sbc("v16", [128, 16], F32)

            T0, C0, V0, S0 = (cnt[k] for k in ("ST", "SC", "SV", "SS"))
            T1 = T0
            C1 = C0
            V1 = V0 + 1
            TPIc, CPIc, VPIc = 5, 8, 5

            def stc(do, k):
                return T1 + do * TPIc + k if do >= 0 else T1

            def scc(do, k):
                return C1 + do * CPIc + k if do >= 0 else C1

            def svc(do, k):
                return V1 + do * VPIc + k if do >= 0 else V1

            ss_xw = {}
            ss_s1sp = {}

            @block.sync
            def _(sync):
                phase_start(sync)
                for do in range(FC):
                    if do >= 2:
                        sync.wait_ge(SV, svc(do - 1, 4))
                    sync.wait_ge(SS, cnt["SS"])
                    inc(sync.dma_start(out=xw[do % 2][:],
                                       in_=xT[do * 128:(do + 1) * 128, HALO:NTH]),
                        "SS", 16)
                    ss_xw[do] = cnt["SS"]
                    sync.wait_ge(SV, svc(do, 4))
                    sync.wait_ge(SC_, scc(do, 8))
                    sync.wait_ge(SS, cnt["SS"])
                    inc(sync.dma_start(out=s1_sp[do][:], in_=s1_sb[do % 2][:]),
                        "SS", 16)
                    ss_s1sp[do] = cnt["SS"]
                phase_end(sync, 4)

            @block.gpsimd
            def _(gpsimd):
                phase_start(gpsimd)
                gpsimd.wait_ge(SS, S0)
                for do in range(FC):
                    if do >= 2:
                        gpsimd.wait_ge(ST_, stc(do - 1, 4))
                    gpsimd.wait_ge(SX, cnt["SX"])
                    inc(gpsimd.dma_start(out=wo_sb[do % 2][:], in_=wo_d[do]),
                        "SX", 16)
                phase_end(gpsimd, 3)
            SXc0 = cnt["SX"] - FC * 16

            @block.tensor
            def _(tensor):
                phase_start(tensor)
                tensor.wait_ge(SS, ss_tr_all[0])
                for do in range(FC):
                    tensor.wait_ge(SX, SXc0 + (do + 1) * 16)
                    banks4 = [pA, pB, pC, pD]
                    for tt in range(NTB):
                        if do >= 1:
                            tensor.wait_ge(SV, svc(do - 1, tt + 1))
                        bank = banks4[tt]
                        for he in range(NH):
                            ins = tensor.matmul(
                                bank[:, 0:TB],
                                wo_sb[do % 2][:, he * 128:(he + 1) * 128],
                                oTr[:, he * NT + tt * TB: he * NT + (tt + 1) * TB],
                                start=(he == 0), stop=(he == NH - 1))
                        inc(ins, "ST")
                    if do >= 1:
                        tensor.wait_ge(SV, svc(do - 1, 5))
                    first = True
                    for tt in range(NTB):
                        tensor.wait_ge(SV, svc(do, tt + 1))
                        tensor.wait_ge(SC_, scc(do, 2 * tt + 2))
                        for sub in range(4):
                            col = tt * 8 + sub * 2
                            ins = tensor.matmul(
                                pG[:, col:col + 1],
                                s1bb[tt][:, sub * 128:(sub + 1) * 128],
                                onesb[:], start=first, stop=False)
                            first = False
                            last = (tt == NTB - 1 and sub == 3)
                            ins = tensor.matmul(
                                pG[:, col + 1:col + 2],
                                s1sq[tt][:, sub * 128:(sub + 1) * 128],
                                onesb[:], start=False, stop=last)
                    inc(ins, "ST")
                phase_end(tensor, 0)

            @block.scalar
            def _(scalar):
                phase_start(scalar)

                def sop(fn):
                    scalar.wait_ge(SC_, cnt["SC"])
                    inc(fn(), "SC")

                for do in range(FC):
                    if do >= 1:
                        scalar.wait_ge(ST_, stc(do - 1, 5))
                    for tt in range(NTB):
                        scalar.wait_ge(SV, svc(do, tt + 1))
                        sop(lambda do=do, tt=tt: scalar.activation(
                            s1sq[tt][:], s1_sb[do % 2][:, tt * TB:(tt + 1) * TB],
                            AF.Square))
                        sop(lambda do=do, tt=tt: scalar.copy(
                            s1bb[tt][:], s1_sb[do % 2][:, tt * TB:(tt + 1) * TB]))
                scalar.wait_ge(SV, svc(FC - 1, 5) + 6)
                sop(lambda: scalar.activation(ab1[:, 0:16], v16[:], AF.Sqrt))
                phase_end(scalar, 2)
            C_SQRT = cnt["SC"]

            @block.vector
            def _(vector):
                phase_start(vector)

                def vop(fn):
                    vector.wait_ge(SV, cnt["SV"])
                    inc(fn(), "SV")

                vop(lambda: vector.memset(statacc[:], 0.0))
                for do in range(FC):
                    if do >= 2:
                        vector.wait_ge(SC_, scc(do - 2, 8))
                        vector.wait_ge(ST_, stc(do - 2, 5))
                        vector.wait_ge(SS, ss_s1sp[do - 2])
                    for tt in range(NTB):
                        vector.wait_ge(ST_, stc(do, tt + 1))
                        vector.wait_ge(SS, ss_xw[do])
                        bank = [pA, pB, pC, pD][tt]
                        vop(lambda do=do, tt=tt, bank=bank: vector.scalar_tensor_tensor(
                            s1_sb[do % 2][:, tt * TB:(tt + 1) * TB],
                            bank[:, 0:TB], 1.0, xw[do % 2][:, tt * TB:(tt + 1) * TB],
                            ALU.mult, ALU.add))
                    vector.wait_ge(ST_, stc(do, 5))
                    vop(lambda: vector.tensor_tensor(statacc[:], statacc[:],
                                                     pG[:, 0:32], ALU.add))
                sevn = statacc[:].rearrange("p (g s) -> p g s", s=2)
                vop(lambda: vector.tensor_scalar(m16[:], sevn[:, :, 0:1],
                                                 1.0 / 4096.0, None, ALU.mult))
                vop(lambda: vector.tensor_scalar(e16[:], sevn[:, :, 1:2],
                                                 1.0 / 4096.0, None, ALU.mult))
                vop(lambda: vector.tensor_tensor(t16[:], m16[:], m16[:], ALU.mult))
                vop(lambda: vector.tensor_tensor(v16[:], e16[:], t16[:],
                                                 ALU.subtract))
                vop(lambda: vector.tensor_scalar(v16[:], v16[:], 1e-5, None,
                                                 ALU.add))
                vop(lambda: vector.reciprocal(v16[:], v16[:]))
                vector.wait_ge(SC_, C_SQRT)
                vop(lambda: vector.tensor_tensor(t16[:], m16[:], ab1[:, 0:16],
                                                 ALU.mult))
                vop(lambda: vector.tensor_scalar(ab1[:, 16:32], t16[:], -1.0,
                                                 None, ALU.mult))
                phase_end(vector, 1)

        ebc.close()
        phase_no[0] += 1
        if upto == "C":
            _tail_zeros()
            return nc

        with contextlib.ExitStack() as ed:
            sbd = lambda n, s, d: ed.enter_context(nc.sbuf_tensor(n, s, d))
            x1b = sbd("x1b", [128, FC * NT], BF16)
            xab = sbd("xab", [128, 2 * NT], BF16)
            m16d = sbd("m16d", [128, 16], F32)
            e16d = sbd("e16d", [128, 16], F32)
            t16d = sbd("t16d", [128, 16], F32)
            v16d = sbd("v16d", [128, 16], F32)

            T0, C0, V0, S0 = (cnt[k] for k in ("ST", "SC", "SV", "SS"))

            def svd2(do, k):
                return V0 + 97 + do * 5 + k if do >= 0 else V0 + 97

            ss_s1ld = {}
            ss_x1w = {}
            ss_x1sp = {}
            ss_s2sp = {}

            ed1 = contextlib.ExitStack()
            s1_ld = [ed1.enter_context(nc.sbuf_tensor(f"s1ld{i}", [128, NT], F32))
                     for i in range(2)]
            wa_sb = ed1.enter_context(nc.sbuf_tensor("wa_sb", [128, 2 * 4096], BF16))
            abc_a = ed1.enter_context(nc.sbuf_tensor("abc_a", [128, NT], F32))
            abc_b = ed1.enter_context(nc.sbuf_tensor("abc_b", [128, NT], F32))
            arow = ed1.enter_context(nc.sbuf_tensor("arow", [1, TB], F32))

            @block.sync
            def _(sync):
                phase_start(sync)
                sync.wait_ge(SS, S0)
                for fc in range(FC):
                    if fc >= 2:
                        sync.wait_ge(SC_, C0 + 9 + fc - 2)
                        sync.wait_ge(SS, ss_x1sp[fc - 2])
                        sync.wait_ge(SV, V0 + 32 + 2 * (fc - 2) + 2)
                    sync.wait_ge(SS, cnt["SS"])
                    inc(sync.dma_start(out=s1_ld[fc % 2][:], in_=s1_sp[fc][:]),
                        "SS", 16)
                    ss_s1ld[fc] = cnt["SS"]
                    sync.wait_ge(SV, V0 + 32 + 2 * fc + 2)
                    sync.wait_ge(SS, cnt["SS"])
                    inc(sync.dma_start(out=x1_sp[fc][:], in_=s1_ld[fc % 2][:]),
                        "SS", 16)
                    ss_x1sp[fc] = cnt["SS"]

            @block.gpsimd
            def _(gpsimd):
                phase_start(gpsimd)
                gpsimd.wait_ge(SS, S0)
                gpsimd.wait_ge(SX, cnt["SX"])
                inc(gpsimd.dma_start(out=wa_sb[:, 0:4096], in_=wa_d[0]), "SX", 16)
                gpsimd.wait_ge(SX, cnt["SX"])
                inc(gpsimd.dma_start(out=wa_sb[:, 4096:8192], in_=wa_d[1]),
                    "SX", 16)
            SXd0 = cnt["SX"]

            @block.tensor
            def _(tensor):
                phase_start(tensor)
                for i in range(8):
                    bank = pC if i % 2 == 0 else pD
                    for c in range(4):
                        col = i * 4 + c
                        if col >= 1:
                            tensor.wait_ge(SV, V0 + col)
                        inc(tensor.matmul(pG[0:1, 0:128], ab1[:, col:col + 1],
                                          idfs[:], start=True, stop=True), "ST")
                    if i >= 2:
                        tensor.wait_ge(SC_, C0 + i - 1)
                    tensor.wait_ge(SV, V0 + i * 4 + 4)
                    inc(tensor.matmul(bank[:, 0:TB], onesr[:], arow[0:1, 0:TB],
                                      start=True, stop=True), "ST")
                tensor.wait_ge(SX, SXd0)
                for er in range(2):
                    for tt in range(NTB):
                        gi = er * 4 + tt
                        bank = pA if gi % 2 == 0 else pB
                        if gi >= 2:
                            tensor.wait_ge(SC_, C0 + 39 + gi)
                        tensor.wait_ge(SC_, C0 + 8 + 32)
                        for fc in range(FC):
                            ins = tensor.matmul(
                                bank[:, 0:TB],
                                wa_sb[:, er * 4096 + fc * 128: er * 4096 + (fc + 1) * 128],
                                x1b[:, fc * NT + tt * TB: fc * NT + (tt + 1) * TB],
                                start=(fc == 0), stop=(fc == FC - 1))
                        inc(ins, "ST")

            @block.scalar
            def _(scalar):
                phase_start(scalar)

                def sop(fn):
                    scalar.wait_ge(SC_, cnt["SC"])
                    inc(fn(), "SC")

                for i in range(8):
                    bank = pC if i % 2 == 0 else pD
                    dst = abc_a if i < 4 else abc_b
                    tt = i % 4
                    scalar.wait_ge(ST_, T0 + 5 * i + 5)
                    sop(lambda dst=dst, tt=tt, bank=bank: scalar.copy(
                        dst[:, tt * TB:(tt + 1) * TB], bank[:, 0:TB]))
                for fc in range(FC):
                    scalar.wait_ge(SV, V0 + 32 + 2 * fc + 2)
                    sop(lambda fc=fc: scalar.copy(
                        x1b[:, fc * NT:(fc + 1) * NT], s1_ld[fc % 2][:]))
                for gi in range(8):
                    er, tt = divmod(gi, 4)
                    bank = pA if gi % 2 == 0 else pB
                    scalar.wait_ge(ST_, T0 + 40 + gi + 1)
                    sop(lambda er=er, tt=tt, bank=bank: scalar.activation(
                        xab[:, er * NT + tt * TB: er * NT + (tt + 1) * TB],
                        bank[:, 0:TB], AF.Copy, scale=wvec_s[:, er:er + 1]))

            @block.vector
            def _(vector):
                phase_start(vector)

                def vop(fn):
                    vector.wait_ge(SV, cnt["SV"])
                    inc(fn(), "SV")

                for col in range(32):
                    i, c = divmod(col, 4)
                    vector.wait_ge(ST_, T0 + 5 * i + c + 1)
                    if c == 0 and i >= 1:
                        vector.wait_ge(ST_, T0 + 5 * (i - 1) + 5)
                    vop(lambda i=i, c=c: vector.tensor_copy(
                        arow[0:1, c * 128:(c + 1) * 128], pG[0:1, 0:128]))
                for fc in range(FC):
                    vector.wait_ge(SS, ss_s1ld[fc])
                    vector.wait_ge(SC_, C0 + 8)
                    vop(lambda fc=fc: vector.tensor_tensor(
                        s1_ld[fc % 2][:], s1_ld[fc % 2][:], abc_a[:], ALU.mult))
                    vop(lambda fc=fc: vector.tensor_tensor(
                        s1_ld[fc % 2][:], s1_ld[fc % 2][:], abc_b[:], ALU.add))
                vop(lambda: vector.memset(statacc[:], 0.0))

            ed1.close()
            wf_sb = sbd("wf_sb", [128, 4096], BF16)
            wb_sb = sbd("wb_sb", [128, 256], BF16)
            x1w = [sbd(f"x1w{i}", [128, NT], F32) for i in range(2)]
            s2_sb = [sbd(f"s2sb{i}", [128, NT], F32) for i in range(2)]
            s2sq = [sbd(f"s2sq{i}", [128, TB], BF16) for i in range(4)]
            s2bb = [sbd(f"s2bb{i}", [128, TB], BF16) for i in range(4)]
            D1_SC_END = C0 + 48
            D1_SS_X1 = lambda: ss_x1sp[FC - 1]

            @block.sync
            def _(sync):
                sync.wait_ge(SC_, D1_SC_END)
                sync.wait_ge(SV, V0 + 97)
                for do in range(FC):
                    if do >= 2:
                        sync.wait_ge(SV, svd2(do - 2, 4))
                    sync.wait_ge(SS, cnt["SS"])
                    inc(sync.dma_start(out=x1w[do % 2][:], in_=x1_sp[do][:]),
                        "SS", 16)
                    ss_x1w[do] = cnt["SS"]
                    sync.wait_ge(SV, svd2(do, 4))
                    sync.wait_ge(SC_, C0 + 48 + do * 8 + 8)
                    sync.wait_ge(SS, cnt["SS"])
                    inc(sync.dma_start(out=s2_sp[do][:], in_=s2_sb[do % 2][:]),
                        "SS", 16)
                    ss_s2sp[do] = cnt["SS"]
                phase_end(sync, 4)

            @block.gpsimd
            def _(gpsimd):
                gpsimd.wait_ge(ST_, T0 + 48)
                gpsimd.wait_ge(SC_, D1_SC_END)
                gpsimd.wait_ge(SS, ss_x1sp[FC - 1])
                for do in range(FC):
                    if do >= 1:
                        gpsimd.wait_ge(ST_, T0 + 48 + do * 5)
                    gpsimd.wait_ge(SX, cnt["SX"])
                    inc(gpsimd.dma_start(out=wf_sb[:], in_=wf_d[do]), "SX", 16)
                    gpsimd.wait_ge(SX, cnt["SX"])
                    inc(gpsimd.dma_start(out=wb_sb[:], in_=wb_d[do]), "SX", 16)
                phase_end(gpsimd, 3)
            SXd1 = cnt["SX"] - FC * 32

            @block.tensor
            def _(tensor):
                for do in range(FC):
                    tensor.wait_ge(SX, SXd1 + (do + 1) * 32)
                    for tt in range(NTB):
                        gi2 = do * 4 + tt
                        bank = [pA, pB, pC, pD][tt]
                        if do >= 1:
                            tensor.wait_ge(SV, svd2(do - 1, tt + 1))
                        else:
                            tensor.wait_ge(SC_, D1_SC_END)
                        for fc in range(FC):
                            tensor.matmul(
                                bank[:, 0:TB],
                                wf_sb[:, fc * 128:(fc + 1) * 128],
                                x1b[:, fc * NT + tt * TB: fc * NT + (tt + 1) * TB],
                                start=(fc == 0), stop=False)
                        tensor.matmul(bank[:, 0:TB], wb_sb[:, 0:128],
                                      xab[:, 0 * NT + tt * TB: 0 * NT + (tt + 1) * TB],
                                      start=False, stop=False)
                        ins = tensor.matmul(bank[:, 0:TB], wb_sb[:, 128:256],
                                            xab[:, 1 * NT + tt * TB: 1 * NT + (tt + 1) * TB],
                                            start=False, stop=True)
                        inc(ins, "ST")
                    if do >= 1:
                        tensor.wait_ge(SV, svd2(do - 1, 5))
                    first = True
                    for tt in range(NTB):
                        tensor.wait_ge(SV, svd2(do, tt + 1))
                        tensor.wait_ge(SC_, C0 + 48 + do * 8 + 2 * tt + 2)
                        for sub in range(4):
                            col = tt * 8 + sub * 2
                            tensor.matmul(
                                pG[:, col:col + 1],
                                s2bb[tt][:, sub * 128:(sub + 1) * 128],
                                onesb[:], start=first, stop=False)
                            first = False
                            last = (tt == NTB - 1 and sub == 3)
                            ins = tensor.matmul(
                                pG[:, col + 1:col + 2],
                                s2sq[tt][:, sub * 128:(sub + 1) * 128],
                                onesb[:], start=False, stop=last)
                    inc(ins, "ST")
                phase_end(tensor, 0)

            @block.scalar
            def _(scalar):

                def sop(fn):
                    scalar.wait_ge(SC_, cnt["SC"])
                    inc(fn(), "SC")

                for do in range(FC):
                    if do >= 1:
                        scalar.wait_ge(ST_, T0 + 48 + (do - 1) * 5 + 5)
                    for tt in range(NTB):
                        scalar.wait_ge(SV, svd2(do, tt + 1))
                        sop(lambda do=do, tt=tt: scalar.activation(
                            s2sq[tt][:], s2_sb[do % 2][:, tt * TB:(tt + 1) * TB],
                            AF.Square))
                        sop(lambda do=do, tt=tt: scalar.copy(
                            s2bb[tt][:], s2_sb[do % 2][:, tt * TB:(tt + 1) * TB]))
                scalar.wait_ge(SV, svd2(FC - 1, 5) + 6)
                sop(lambda: scalar.activation(ab2[:, 0:16], v16d[:], AF.Sqrt))
                phase_end(scalar, 2)
            D_SQRT = cnt["SC"]

            @block.vector
            def _(vector):

                def vop(fn):
                    vector.wait_ge(SV, cnt["SV"])
                    inc(fn(), "SV")

                for do in range(FC):
                    if do >= 2:
                        vector.wait_ge(SC_, C0 + 48 + (do - 2) * 8 + 8)
                        vector.wait_ge(SS, ss_s2sp[do - 2])
                    for tt in range(NTB):
                        vector.wait_ge(ST_, T0 + 48 + do * 5 + tt + 1)
                        vector.wait_ge(SS, ss_x1w[do])
                        bank = [pA, pB, pC, pD][tt]
                        vop(lambda do=do, tt=tt, bank=bank: vector.scalar_tensor_tensor(
                            s2_sb[do % 2][:, tt * TB:(tt + 1) * TB],
                            bank[:, 0:TB], bias_s[:, do:do + 1],
                            x1w[do % 2][:, tt * TB:(tt + 1) * TB],
                            ALU.add, ALU.add))
                    vector.wait_ge(ST_, T0 + 48 + do * 5 + 5)
                    vop(lambda: vector.tensor_tensor(statacc[:], statacc[:],
                                                     pG[:, 0:32], ALU.add))
                sevn = statacc[:].rearrange("p (g s) -> p g s", s=2)
                vop(lambda: vector.tensor_scalar(m16d[:], sevn[:, :, 0:1],
                                                 1.0 / 4096.0, None, ALU.mult))
                vop(lambda: vector.tensor_scalar(e16d[:], sevn[:, :, 1:2],
                                                 1.0 / 4096.0, None, ALU.mult))
                vop(lambda: vector.tensor_tensor(t16d[:], m16d[:], m16d[:],
                                                 ALU.mult))
                vop(lambda: vector.tensor_tensor(v16d[:], e16d[:], t16d[:],
                                                 ALU.subtract))
                vop(lambda: vector.tensor_scalar(v16d[:], v16d[:], 1e-5, None,
                                                 ALU.add))
                vop(lambda: vector.reciprocal(v16d[:], v16d[:]))
                vector.wait_ge(SC_, D_SQRT)
                vop(lambda: vector.tensor_tensor(t16d[:], m16d[:], ab2[:, 0:16],
                                                 ALU.mult))
                vop(lambda: vector.tensor_scalar(ab2[:, 16:32], t16d[:], -1.0,
                                                 None, ALU.mult))
                phase_end(vector, 1)

        phase_no[0] += 1
        if upto == "D":
            _tail_zeros()
            return nc

        with contextlib.ExitStack() as ee:
            sbe = lambda n, s, d: ee.enter_context(nc.sbuf_tensor(n, s, d))
            ab2c_a = sbe("ab2c_a", [128, NT], F32)
            ab2c_b = sbe("ab2c_b", [128, NT], F32)
            arow2 = sbe("arow2", [1, TB], F32)
            s2_ld = [sbe(f"s2ld{i}", [128, NT], F32) for i in range(2)]
            out_sb = [sbe(f"outsb{i}", [128, NT], F32) for i in range(2)]

            T0, C0, V0, S0 = (cnt[k] for k in ("ST", "SC", "SV", "SS"))
            ss_s2ld = {}
            ss_out = {}

            @block.sync
            def _(sync):
                phase_start(sync)
                sync.wait_ge(SS, S0)
                for fc in range(0, FC, 2):
                    if fc >= 2:
                        sync.wait_ge(SV, V0 + 32 + 2 * (fc - 1))
                    sync.wait_ge(SS, cnt["SS"])
                    inc(sync.dma_start(out=s2_ld[0][:], in_=s2_sp[fc][:]),
                        "SS", 16)
                    ss_s2ld[fc] = (SS, cnt["SS"])
                sync.wait_ge(SS, cnt["SS"])
                phase_end(sync, 4)

            @block.gpsimd
            def _(gpsimd):
                phase_start(gpsimd)
                gpsimd.wait_ge(SS, S0)
                for fc in range(1, FC, 2):
                    if fc >= 2:
                        gpsimd.wait_ge(SV, V0 + 32 + 2 * (fc - 1))
                    gpsimd.wait_ge(SX, cnt["SX"])
                    inc(gpsimd.dma_start(out=s2_ld[1][:], in_=s2_sp[fc][:]),
                        "SX", 16)
                    ss_s2ld[fc] = (SX, cnt["SX"])
                phase_end(gpsimd, 3)

            @block.tensor
            def _(tensor):
                phase_start(tensor)
                for i in range(8):
                    bank = pC if i % 2 == 0 else pD
                    for c in range(4):
                        col = i * 4 + c
                        if col >= 1:
                            tensor.wait_ge(SV, V0 + col)
                        inc(tensor.matmul(pG[0:1, 0:128], ab2[:, col:col + 1],
                                          idfs[:], start=True, stop=True), "ST")
                    if i >= 2:
                        tensor.wait_ge(SC_, C0 + i - 1)
                    tensor.wait_ge(SV, V0 + i * 4 + 4)
                    inc(tensor.matmul(bank[:, 0:TB], onesr[:], arow2[0:1, 0:TB],
                                      start=True, stop=True), "ST")
                phase_end(tensor, 0)

            @block.scalar
            def _(scalar):
                phase_start(scalar)

                def sop(fn):
                    scalar.wait_ge(SC_, cnt["SC"])
                    inc(fn(), "SC")

                for i in range(8):
                    bank = pC if i % 2 == 0 else pD
                    dst = ab2c_a if i < 4 else ab2c_b
                    tt = i % 4
                    scalar.wait_ge(ST_, T0 + 5 * i + 5)
                    sop(lambda dst=dst, tt=tt, bank=bank: scalar.copy(
                        dst[:, tt * TB:(tt + 1) * TB], bank[:, 0:TB]))
                for fc in range(FC):
                    scalar.wait_ge(SV, V0 + 32 + 2 * fc + 2)
                    scalar.wait_ge(SO, cnt["SO"])
                    inc(scalar.dma_start(out=outT[fc * 128:(fc + 1) * 128, :],
                                         in_=out_sb[fc % 2][:]), "SO", 16)
                    ss_out[fc] = cnt["SO"]
                scalar.wait_ge(SO, cnt["SO"])
                phase_end(scalar, 2)

            @block.vector
            def _(vector):
                phase_start(vector)

                def vop(fn):
                    vector.wait_ge(SV, cnt["SV"])
                    inc(fn(), "SV")

                for col in range(32):
                    i, c = divmod(col, 4)
                    vector.wait_ge(ST_, T0 + 5 * i + c + 1)
                    if c == 0 and i >= 1:
                        vector.wait_ge(ST_, T0 + 5 * (i - 1) + 5)
                    vop(lambda i=i, c=c: vector.tensor_copy(
                        arow2[0:1, c * 128:(c + 1) * 128], pG[0:1, 0:128]))
                for fc in range(FC):
                    vector.wait_ge(*ss_s2ld[fc])
                    if fc >= 2:
                        vector.wait_ge(SO, ss_out[fc - 2])
                    vector.wait_ge(SC_, C0 + 8)
                    vop(lambda fc=fc: vector.tensor_tensor(
                        out_sb[fc % 2][:], s2_ld[fc % 2][:], ab2c_a[:], ALU.mult))
                    vop(lambda fc=fc: vector.tensor_tensor(
                        out_sb[fc % 2][:], out_sb[fc % 2][:], ab2c_b[:], ALU.add))
                phase_end(vector, 1)

    return nc



def _host_layout(inputs, core):
    b, r = divmod(core, 4)
    x = inputs["L0_attn_hidden_states"]
    tok0 = r * NT
    xb = x[b]
    xs = np.zeros((NTH, 4096), np.float32)
    lo = tok0 - HALO
    src_lo = max(lo, 0)
    xs[src_lo - lo:NTH] = xb[src_lo:tok0 + NT]
    xTl = np.ascontiguousarray(xs.T)

    pos = np.arange(lo, tok0 + NT, dtype=np.float32)
    half = np.arange(64, dtype=np.float32)
    inv = (1.0 / (10000.0 ** (2.0 * half / 128.0))).astype(np.float32)
    fr = pos[None, :] * inv[:, None]
    emb = np.concatenate([fr, fr], axis=0)
    cosl = np.ascontiguousarray(np.cos(emb).astype(np.float32))
    sinl = np.ascontiguousarray(np.sin(emb).astype(np.float32))

    s = np.arange(128)[:, None]
    t = np.arange(128)[None, :]
    mg = (s <= t).astype(np.float32)
    mA = (s - t >= 64).astype(np.float32)
    mB = ((t - s >= 0) & (t - s <= 64)).astype(np.float32)
    ident = np.eye(128, dtype=np.float32)
    Rt = np.zeros((128, 128), np.float32)
    for d in range(64):
        Rt[d + 64, d] = -1.0
        Rt[d, d + 64] = 1.0
    ohv = np.zeros((128, 8), np.float32)
    ohv[:, b * 4 + r] = 1.0

    def stat_blocks(W, ocs):
        return np.ascontiguousarray(
            W.reshape(32, 128, ocs, 128).transpose(2, 1, 0, 3)
            .reshape(ocs, 128, 4096))

    Acat = inputs["L0_ffn_A"].reshape(256, 4096)
    Bcat = np.ascontiguousarray(
        np.transpose(inputs["L0_ffn_Bf"], (0, 2, 1))).reshape(256, 4096)
    return {
        "xT": xTl, "cosT": cosl, "sinT": sinl,
        "mg": mg, "mA": mA, "mAf": (mA if r > 0 else np.zeros_like(mA)),
        "mB": mB, "idb": ident.astype(BF), "idf": ident, "Rt": Rt, "ohv": ohv,
        "wq": stat_blocks(inputs["L0_attn_Wq"], 32),
        "wk": stat_blocks(inputs["L0_attn_Wk"], 8),
        "wv": stat_blocks(inputs["L0_attn_Wv"], 8),
        "wo": stat_blocks(inputs["L0_attn_Wo"], 32),
        "wf": stat_blocks(np.ascontiguousarray(inputs["L0_ffn_W"].T), 32),
        "wa": stat_blocks(np.ascontiguousarray(Acat.T), 2),
        "wb": np.ascontiguousarray(
            Bcat.reshape(2, 128, 32, 128).transpose(2, 1, 0, 3)
            .reshape(32, 128, 256)),
        "bias_pp": np.ascontiguousarray(
            inputs["L0_ffn_bias"].reshape(32, 128).T),
        "cvec": np.ascontiguousarray(
            np.repeat(inputs["L0_ffn_coeffs"], 16).reshape(2, 128).T),
        "svec": np.ascontiguousarray(
            np.repeat(inputs["L0_ffn_scales"], 16).reshape(2, 128).T),
    }


def kernel(**inputs):
    nc = build_nc()
    in_maps = [_host_layout(inputs, c) for c in range(NCORE)]
    res = run_bass_kernel_spmd(nc, in_maps, core_ids=list(range(NCORE)))
    x = inputs["L0_attn_hidden_states"]
    out = np.zeros_like(x)
    for c in range(NCORE):
        b, r = divmod(c, 4)
        out[b, r * NT:(r + 1) * NT] = res.results[c]["outT"].T
    return out
